# version 06
# baseline (speedup 1.0000x reference)
import numpy as np
from contextlib import ExitStack

import concourse.bass as bass
import concourse.mybir as mybir
from concourse.bass_utils import run_bass_kernel_spmd

F32 = mybir.dt.float32
BF16 = mybir.dt.bfloat16
AF = mybir.ActivationFunctionType
ALU = mybir.AluOpType
AX = mybir.AxisListType

P = 128
T = 2048
NSEQ = 2
D = 1024
DC = D // P
FT = 512
NF = T // FT
NTILE = T // P
TPF = FT // P
DM = 256
DF = 4096
E = 8
H = 512
LNC = 16.11809565095832

PHASES = ("A", "B", "C", "D")
DEBUG_SCRATCH = False
LIMIT_NF = None
LIMIT_NSEQ = None
A_PARTS = ("tiles", "stats", "c2r", "diag", "evac", "kmem", "conv", "decay")


class Rec:
    __slots__ = ("eng", "fn", "reads", "writes", "dma", "flag", "semval", "idx", "deps",
                 "slot", "use")

    def __init__(self, eng, fn, reads, writes, dma):
        self.eng, self.fn = eng, fn
        self.reads, self.writes = list(reads), list(writes)
        self.dma = dma
        self.flag = dma
        self.semval = None
        self.deps = []
        self.slot = self.use = 0


NSLOT = 8


class Prog:
    ENGINES = ("sync", "gpsimd", "tensor", "vector", "scalar")

    def __init__(self, nc, es):
        self.nc = nc
        self.ops = []
        self.last_w = {}
        self.readers = {}
        self.c_sem = {e: es.enter_context(nc.semaphore(f"c_{e}")) for e in self.ENGINES}
        self.d_sem = {e: [es.enter_context(nc.semaphore(f"d_{e}{i}")) for i in range(NSLOT)]
                      for e in ("sync", "gpsimd", "scalar")}
        self.dma_count = {e: 0 for e in ("sync", "gpsimd", "scalar")}

    PSUMNAMES = {"pbig", "pbig2", "ps0", "ps1", "ps2", "prow"}

    def op(self, eng, fn, reads=(), writes=(), dma=False):
        o = Rec(eng, fn, reads, writes, dma)
        o.idx = len(self.ops)
        if dma:
            k = self.dma_count[eng]
            o.slot, o.use = k % NSLOT, k // NSLOT
            o.semval = 16 * (o.use + 1)
            self.dma_count[eng] = k + 1
        deps = {}
        for k in o.reads:
            d = self.last_w.get(k)
            if d is not None:
                deps[d.idx] = d
        for k in o.writes:
            for d in self.readers.get(k, ()):
                deps[d.idx] = d
            d = self.last_w.get(k)
            if d is not None:
                deps[d.idx] = d
        if not dma and eng in ("scalar", "vector") and any(
                ((k if isinstance(k, str) else k[0]) in self.PSUMNAMES)
                for k in list(o.reads) + list(o.writes)):
            prev = getattr(self, "_last_psum_op", None)
            if prev is not None and prev.eng != eng:
                deps[prev.idx] = prev
            self._last_psum_op = o
        o.deps = list(deps.values())
        for k in o.writes:
            self.last_w[k] = o
            self.readers[k] = []
        for k in o.reads:
            self.readers.setdefault(k, []).append(o)
        self.ops.append(o)
        return o

    def finalize(self):
        for o in self.ops:
            for d in o.deps:
                if d.eng == o.eng == "tensor" and not d.dma:
                    continue
                d.flag = True
        cnt = {}
        for o in self.ops:
            if o.flag and not o.dma:
                cnt[o.eng] = cnt.get(o.eng, 0) + 1
                o.semval = cnt[o.eng]

    def emit(self, eng_name, eng):
        watermark = {}

        def wait(sem, v):
            if watermark.get(id(sem), -1) < v:
                eng.wait_ge(sem, v)
                watermark[id(sem)] = v

        for o in self.ops:
            if o.eng != eng_name:
                continue
            waits = {}
            for d in o.deps:
                if d.eng == o.eng and not d.dma and (o.eng == "tensor" or not d.flag):
                    continue
                sem = self.d_sem[d.eng][d.slot] if d.dma else self.c_sem[d.eng]
                cur = waits.get(id(sem))
                if cur is None or d.semval > cur[1]:
                    waits[id(sem)] = (sem, d.semval)
            for sem, v in waits.values():
                wait(sem, v)
            if o.dma and o.use > 0:
                wait(self.d_sem[o.eng][o.slot], 16 * o.use)
            inst = o.fn(eng)
            if o.flag:
                if o.dma:
                    inst.then_inc(self.d_sem[o.eng][o.slot], 16)
                else:
                    inst.then_inc(self.c_sem[o.eng], 1)

    def fence(self, eng):
        for e2 in ("sync", "gpsimd", "scalar"):
            n = self.dma_count[e2]
            for i in range(NSLOT):
                uses = (n - i + NSLOT - 1) // NSLOT if n > i else 0
                if uses > 0:
                    eng.wait_ge(self.d_sem[e2][i], 16 * uses)


def build():
    nc = bass.Bass("TRN2", target_bir_lowering=False, debug=False)

    def din(name, shape, dt=F32):
        return nc.dram_tensor(name, list(shape), dt, kind="ExternalInput").ap()

    x_in = din("x", [NSEQ, T, D])
    y_out = nc.dram_tensor("y", [NSEQ, T, D], F32, kind="ExternalOutput").ap()

    Wdecay = din("Wdecay", [D, D]); Wgate = din("Wgate", [D, D])
    Wk = din("Wk", [D, DM]); MT = din("MT", [DM, DM]); Wmem = din("Wmem", [DM, D])
    Wc = din("Wc", [3 * D, D])
    Wg = din("Wg", [D, DF]); Wup = din("Wup", [D, DF]); Wdown = din("Wdown", [DF, D])
    W1 = din("W1", [E, D, H]); W2 = din("W2", [E, H, D])
    Ws1 = din("Ws1", [D, H]); Ws2 = din("Ws2", [H, D])
    gateW = din("gateW", [D, E])
    n1w_i = din("n1w_c", [P, DC]); n2w_i = din("n2w_c", [P, DC])
    bd_i = din("bd_c", [P, DC]); bg_i = din("bg_c", [P, DC])
    cw_i = din("cw_cols", [P, DC * 7])
    bc_i = din("bc_row", [1, D]); bs1_i = din("bs1_row", [1, H]); bs2_i = din("bs2_row", [1, D])
    b1_i = din("b1_rows", [1, E, H]); b2_i = din("b2_pm", [E, D])
    ss_i = din("ss", [1, 1]); ms_i = din("ms", [1, 1]); rs_i = din("rs", [1, 1])

    skind = "ExternalOutput" if DEBUG_SCRATCH else "Internal"
    scr_loc = nc.dram_tensor("scr_loc", [NSEQ, P, DC, T], BF16, kind=skind).ap()
    scr_st = nc.dram_tensor("scr_st", [NSEQ, P, DC, T], BF16, kind=skind).ap()
    scr_mk = nc.dram_tensor("scr_mk", [NSEQ, P, 2, T], BF16, kind=skind).ap()
    scr_mix = nc.dram_tensor("scr_mix", [NSEQ, T, D], F32, kind=skind).ap()
    scr_x1 = nc.dram_tensor("scr_x1", [NSEQ, T, D], F32, kind=skind).ap()
    scr_ffa = nc.dram_tensor("scr_ffa", [NSEQ, T, D], F32, kind=skind).ap()
    scr_fna = nc.dram_tensor("scr_fna", [NSEQ, T, D], F32, kind=skind).ap()
    scr_dbg = nc.dram_tensor("scr_dbg", [P, DC, T], BF16, kind=skind).ap()
    scr_dbg2 = nc.dram_tensor("scr_dbg2", [P, DC, FT], F32, kind=skind).ap()
    scr_dbg3 = nc.dram_tensor("scr_dbg3", [P, 4, NTILE], F32, kind=skind).ap()

    eye_i = nc.inline_tensor(np.eye(P, dtype=np.float32), "eye_i").ap()

    es = ExitStack()
    with es:
        def sb(name, shape, dt=F32):
            return es.enter_context(nc.sbuf_tensor(name, list(shape), dt))

        def psum(name, shape):
            return es.enter_context(nc.psum_tensor(name, list(shape), F32))

        eye_f = sb("eye_f", [P, P]); eye_b = sb("eye_b", [P, P], BF16)
        ones_f = sb("ones_f", [1, P]); ones_b = sb("ones_b", [1, P], BF16)
        zerosF = sb("zerosF", [P, FT])
        consts = sb("consts", [P, 8])
        n1w = sb("n1w", [P, DC]); n2w = sb("n2w", [P, DC])
        bd_c = sb("bdcol", [P, DC]); bg_c = sb("bgcol", [P, DC])
        cw_c = sb("cwcol", [P, DC * 7])
        diags = sb("diags", [P, DC * 7 * P], BF16)
        b1_r = diags[0:1, 0:E * H].rearrange("o (e2 h2) -> o e2 h2", e2=E)
        b1K = ["b1"] + [("diag", i) for i in range(32)]
        MT_b = sb("MT_b", [P, 2, DM], BF16)
        gateW_b = sb("gateW_b", [P, DC, E], BF16)
        sscol = sb("sscol", [P, 1]); mscol = sb("mscol", [P, 1]); rscol = sb("rscol", [P, 1])
        nss = sb("nss", [P, DC])
        carA = sb("carA", [P, DC]); carC = sb("carC", [P, DC])

        MUc = [sb(f"MUc{s}", [P, NTILE]) for s in range(NSEQ)]
        Rc = [sb(f"Rc{s}", [P, NTILE]) for s in range(NSEQ)]
        Sc = [sb(f"Sc{s}", [P, NTILE]) for s in range(NSEQ)]
        NMSc = [sb(f"NMSc{s}", [P, NTILE]) for s in range(NSEQ)]
        R2c = [sb(f"R2c{s}", [P, NTILE]) for s in range(NSEQ)]
        MU4c = [sb(f"MU4c{s}", [P, NTILE]) for s in range(NSEQ)]
        R4c = [sb(f"R4c{s}", [P, NTILE]) for s in range(NSEQ)]
        GF = sb("GFt", [P, NTILE, E])
        GCM = sb("GCMt", [E, T], BF16)
        stat_t = sb("stat_t", [P, 8])
        rowb = sb("rowb", [1, 2 * P])

        actT = sb("actT", [P, DC, T], BF16)
        wpool = sb("wpool", [P, 32768], BF16)

        g32a = sb("g32a", [P, DC, FT])
        ldc = sb("ldc", [P, FT]); Acb = sb("Acb", [P, FT]); cdc = sb("cdc", [P, FT])
        ecb = sb("ecb", [P, FT]); ucb = sb("ucb", [P, FT]); Ccb = sb("Ccb", [P, FT])
        sluws = sb("sluws", [P, 1, FT])
        cma = [sb(f"cma{i}", [P, DC, FT], BF16) for i in range(2)]
        kbf = cma[1]
        cmb0 = sb("cmb0", [P, 2, FT], BF16)
        xws = sb("xws", [P, 2, D])
        sqh = sb("sqh", [P, FT])
        diagws = sb("diagws", [P, 1, P])
        mixws = sb("mixws", [P, 2, D])
        accws = sb("accws", [P, 2, D])
        fows = sb("fows", [P, D])
        x1ws = sb("x1ws", [P, D])
        gscb = sb("gscb", [P, 2, H], BF16)
        gcmw = sb("gcmw", [P, 1, 4, P], BF16)
        scb = sb("scb", [P, E])
        scb2 = sb("scb2", [P, E])

        pbig = psum("pbig", [P, 1024])
        pbig2 = psum("pbig2", [P, 1024])
        ps0 = psum("ps0", [P, FT]); ps1 = psum("ps1", [P, FT]); ps2 = psum("ps2", [P, FT])
        prow = psum("prow", [E, P])

        cvals = [1e-5, 1e-6, -0.5, -1.0]
        for i, cv in enumerate(cvals):
            nc.const_aps.register(F32, cv, consts[:, i:i + 1])

        prog = Prog(nc, es)
        OP = prog.op

        def wstrips(off, n):
            return [f"wp{j}" for j in range(off // 4096, (off + n - 1) // 4096 + 1)]

        def wview(off, shape):
            n = int(np.prod(shape))
            ap = wpool[:, off:off + n]
            if len(shape) > 1:
                pat = "p (" + " ".join(f"d{i}" for i in range(len(shape))) + ") -> p " + \
                      " ".join(f"d{i}" for i in range(len(shape)))
                ap = ap.rearrange(pat, **{f"d{i}": s for i, s in enumerate(shape)})
            return ap, wstrips(off, n)

        WdV, WdK = wview(0, [DC, D])
        WgV, WgK = wview(8192, [DC, D])
        WkV, WkK = wview(16384, [DC, DM])
        WcV, WcK = wview(0, [24, D])
        WmV, WmK = wview(24576, [2, D])
        bcV = wpool[0:1, 26624:27648]; bcK = wstrips(26624, 1024)
        WgqV, WgqK = wview(0, [DC, 1024])
        WuqV, WuqK = wview(8192, [DC, 1024])
        WdqV, WdqK = wview(16384, [DC, 1024])
        Ws1V, Ws1K = wview(0, [DC, H])
        Ws2V, Ws2K = wview(4096, [4, D])
        bs1V = wpool[0:1, 8192:8704]; bs1K = wstrips(8192, 512)
        bs2V = wpool[0:1, 8704:9728]; bs2K = wstrips(8704, 1024)
        b2V = wpool[0:8, 9728:10752]; b2K = wstrips(9728, 1024)
        W1hV, W1hK = wview(0, [4, DC, H])
        W2hV, W2hK = wview(16384, [4, 4, D])

        membf = actT[:, :, 0:FT]
        membfK = [("actT", q) for q in range(TPF)]

        def load_w(dst, src_ap, keys, eng="gpsimd"):
            OP(eng, lambda e, d=dst, s=src_ap: e.dma_start(out=d, in_=s),
               reads=[], writes=keys, dma=True)

        def setup():
            def ld(dst, src, keys, eng="sync"):
                OP(eng, lambda e, d=dst, s=src: e.dma_start(out=d, in_=s),
                   reads=[], writes=keys, dma=True)

            ld(eye_f[:], eye_i, ["eye_f"])
            ld(n1w[:], n1w_i, ["n1w"]); ld(n2w[:], n2w_i, ["n2w"])
            ld(bd_c[:], bd_i, ["bd"]); ld(bg_c[:], bg_i, ["bg"])
            ld(cw_c[:], cw_i, ["cw"])
            ld(MT_b[:], MT.rearrange("(c p) n -> p c n", p=P), ["MT"], eng="gpsimd")
            ld(gateW_b[:], gateW.rearrange("(c p) n -> p c n", p=P), ["gateW"], eng="gpsimd")
            ld(sscol[0:1, :], ss_i, ["ssv"]); ld(mscol[0:1, :], ms_i, ["msv"])
            ld(rscol[0:1, :], rs_i, ["rsv"])

            OP("vector", lambda e: e.memset(ones_f[:], 1.0), writes=["ones_f"])
            OP("vector", lambda e: e.memset(ones_b[:], 1.0), writes=["ones_b"])
            OP("vector", lambda e: e.memset(zerosF[:], 0.0), writes=["zerosF"])
            for i, cv in enumerate(cvals):
                OP("vector", lambda e, i=i, cv=cv: e.memset(consts[:, i:i + 1], cv),
                   writes=["consts"])
            OP("vector", lambda e: e.tensor_copy(eye_b[:], eye_f[:]),
               reads=["eye_f"], writes=["eye_b"])
            for name, col in (("ssv", sscol), ("msv", mscol), ("rsv", rscol)):
                OP("tensor", lambda e, c=col: e.matmul(ps2[:, 0:1], ones_f[:], c[0:1, :],
                                                       start=True, stop=True),
                   reads=[name, "ones_f"], writes=["ps2"])
                OP("vector", lambda e, c=col: e.tensor_copy(c[:], ps2[:, 0:1]),
                   reads=["ps2"], writes=[name])
            OP("vector", lambda e: e.tensor_scalar_mul(nss[:], n1w[:], sscol[:]),
               reads=["n1w", "ssv"], writes=["nss"])
            for c in range(DC):
                for j in range(7):
                    idx = c * 7 + j
                    OP("vector", lambda e, idx=idx: e.tensor_scalar_mul(
                        diags[:, idx * P:(idx + 1) * P], eye_f[:], cw_c[:, idx:idx + 1]),
                       reads=["eye_f", "cw"], writes=[("diag", idx)])

        def stats_pass(src_key, src_ap, mu_ap, mu_key, r_ap, r_key):
            OP("vector", lambda e: e.reduce_sum(stat_t[:, 0:1], src_ap, axis=AX.X),
               reads=[src_key], writes=["stat"])
            for hh in range(2):
                OP("scalar", lambda e, hh=hh: e.activation(
                    sqh[:], src_ap[:, hh * FT:(hh + 1) * FT], AF.Square),
                   reads=[src_key], writes=["sqh"])
                OP("vector", lambda e, hh=hh: e.reduce_sum(stat_t[:, 1 + hh:2 + hh], sqh[:],
                                                           axis=AX.X),
                   reads=["sqh"], writes=["stat"])
            OP("vector", lambda e: e.tensor_add(stat_t[:, 1:2], stat_t[:, 1:2], stat_t[:, 2:3]),
               reads=["stat"], writes=["stat"])
            OP("vector", lambda e: e.tensor_scalar_mul(mu_ap, stat_t[:, 0:1], 1.0 / D),
               reads=["stat"], writes=[mu_key])
            OP("vector", lambda e: e.tensor_tensor(stat_t[:, 3:4], mu_ap, mu_ap, ALU.mult),
               reads=[mu_key], writes=["stat"])
            OP("vector", lambda e: e.tensor_scalar_mul(stat_t[:, 2:3], stat_t[:, 1:2], 1.0 / D),
               reads=["stat"], writes=["stat"])
            OP("vector", lambda e: e.tensor_tensor(stat_t[:, 4:5], stat_t[:, 2:3],
                                                   stat_t[:, 3:4], ALU.subtract),
               reads=["stat"], writes=["stat"])
            OP("scalar", lambda e: e.activation(stat_t[:, 3:4], stat_t[:, 4:5], AF.Ln, bias=1e-5),
               reads=["stat"], writes=["stat"])
            OP("scalar", lambda e: e.activation(r_ap, stat_t[:, 3:4], AF.Exp, scale=-0.5),
               reads=["stat"], writes=[r_key])

        def col_to_row(col_ap, col_key, slot=0):
            OP("tensor", lambda e: e.matmul(prow[0:1, :], col_ap, eye_f[:], start=True, stop=True),
               reads=[col_key, "eye_f"], writes=["prow"])
            OP("scalar", lambda e, slot=slot: e.copy(rowb[:, slot * P:(slot + 1) * P],
                                                     prow[0:1, :]),
               reads=["prow"], writes=[("rowb", slot)])

        def diag_transpose(src_ap, src_key, scol_ap, scol_key, with_row,
                           pt=None, kb="pbig", slot=0):
            pt = pbig if pt is None else pt
            dw = diagws[:, 0]
            dwk = "diagws"
            OP("vector", lambda e, dw=dw: e.tensor_scalar_mul(dw, eye_f[:], scol_ap),
               reads=["eye_f", scol_key], writes=[dwk])
            for c in range(DC):
                OP("tensor", lambda e, c=c, pt=pt, dw=dw: e.matmul(
                    pt[:, c * P:(c + 1) * P], src_ap[:, c * P:(c + 1) * P], dw,
                    start=True, stop=not with_row),
                   reads=[src_key, dwk], writes=[(kb, c // 4)])
                if with_row:
                    OP("tensor", lambda e, c=c, pt=pt, slot=slot: e.matmul(
                        pt[:, c * P:(c + 1) * P], ones_f[:], rowb[:, slot * P:(slot + 1) * P],
                        start=False, stop=True),
                       reads=["ones_f", ("rowb", slot)], writes=[(kb, c // 4)])

        def phase_a(seq):
            OP("vector", lambda e: e.memset(carA[:], 0.0), writes=["carA"])
            OP("vector", lambda e: e.memset(carC[:], 0.0), writes=["carC"])
            for fi in range(LIMIT_NF or NF):
                t0 = fi * FT
                for tt in range(TPF):
                    g = fi * TPF + tt
                    xv = xws[:, g % 2]
                    xk = ("xws", g % 2)
                    OP("sync", lambda e, g=g, xv=xv: e.dma_start(
                        out=xv, in_=x_in[seq, g * P:(g + 1) * P, :]),
                       reads=[], writes=[xk], dma=True)
                    if "stats" not in A_PARTS:
                        continue
                    stats_pass(xk, xv, MUc[seq][:, g:g + 1], ("MUc", seq, g),
                               Rc[seq][:, g:g + 1], ("Rc", seq, g))
                    OP("vector", lambda e, g=g: e.tensor_scalar(
                        stat_t[:, 6:7], stat_t[:, 4:5], Rc[seq][:, g:g + 1],
                        Rc[seq][:, g:g + 1], ALU.mult, ALU.mult),
                       reads=["stat", ("Rc", seq, g)], writes=["stat"])
                    OP("scalar", lambda e: e.activation(stat_t[:, 6:7], stat_t[:, 6:7],
                                                        AF.Ln, bias=1e-6),
                       reads=["stat"], writes=["stat"])
                    OP("scalar", lambda e: e.activation(stat_t[:, 7:8], stat_t[:, 6:7],
                                                        AF.Exp, scale=-0.5),
                       reads=["stat"], writes=["stat"])
                    OP("vector", lambda e, g=g: e.tensor_tensor(
                        Sc[seq][:, g:g + 1], Rc[seq][:, g:g + 1], stat_t[:, 7:8], ALU.mult),
                       reads=["stat", ("Rc", seq, g)], writes=[("Sc", seq, g)])
                    OP("vector", lambda e, g=g: e.scalar_tensor_tensor(
                        NMSc[seq][:, g:g + 1], MUc[seq][:, g:g + 1], -1.0,
                        Sc[seq][:, g:g + 1], ALU.mult, ALU.mult),
                       reads=[("MUc", seq, g), ("Sc", seq, g)], writes=[("NMSc", seq, g)])
                    if "c2r" not in A_PARTS:
                        continue
                    col_to_row(NMSc[seq][:, g:g + 1], ("NMSc", seq, g), slot=g % 2)
                    if "diag" not in A_PARTS:
                        continue
                    diag_transpose(xv, xk, Sc[seq][:, g:g + 1], ("Sc", seq, g), True,
                                   slot=g % 2)
                    if "evac" not in A_PARTS:
                        continue
                    for hh in range(2):
                        OP("scalar", lambda e, tt=tt, hh=hh: e.copy(
                            g32a[:, hh * 4:(hh + 1) * 4, tt * P:(tt + 1) * P],
                            pbig[:, hh * FT:(hh + 1) * FT].rearrange("p (c q) -> p c q", q=P)),
                           reads=[("pbig", hh)], writes=["g32a"])
                    for c in range(DC):
                        OP("vector", lambda e, c=c, g=g: e.tensor_scalar_mul(
                            actT[:, c, g * P:(g + 1) * P], pbig[:, c * P:(c + 1) * P],
                            n1w[:, c:c + 1]),
                           reads=[("pbig", c // 4), "n1w"], writes=[("actT", g)])
                hbT_keys = [("actT", fi * TPF + q) for q in range(TPF)]
                hbT_prev = [("actT", fi * TPF - 1)] if fi > 0 else []

                for koc in range(2 if "kmem" in A_PARTS else 0):
                    for kc in range(DC):
                        OP("tensor", lambda e, koc=koc, kc=kc, fi=fi: e.matmul(
                            ps2[:], WkV[:, kc, koc * P:(koc + 1) * P],
                            actT[:, kc, fi * FT:(fi + 1) * FT],
                            start=(kc == 0), stop=(kc == DC - 1)),
                           reads=WkK + hbT_keys, writes=["ps2"])
                    OP("vector", lambda e, koc=koc: e.tensor_copy(kbf[:, koc], ps2[:]),
                       reads=["ps2"], writes=["cma1"])
                for ioc in range(2 if "kmem" in A_PARTS else 0):
                    for jc in range(2):
                        OP("tensor", lambda e, ioc=ioc, jc=jc: e.matmul(
                            ps2[:], MT_b[:, jc, ioc * P:(ioc + 1) * P], kbf[:, jc],
                            start=(jc == 0), stop=(jc == 1)),
                           reads=["MT", "cma1"], writes=["ps2"])
                    OP("vector", lambda e, ioc=ioc: e.tensor_scalar_mul(
                        cmb0[:, ioc], ps2[:], mscol[:]),
                       reads=["ps2", "msv"], writes=["cmb0"])
                if "kmem" in A_PARTS:
                    OP("gpsimd", lambda e, t0=t0: e.dma_start(
                        out=scr_mk[seq, :, :, t0:t0 + FT], in_=cmb0[:]),
                       reads=["cmb0"], writes=[("scr_mk", seq, t0)], dma=True)

                for c in range(DC if "conv" in A_PARTS else 0):
                    pb, pk = (ps0, "ps0") if c % 2 == 0 else (ps1, "ps1")
                    for s in range(7):
                        idx = c * 7 + (6 - s)
                        if fi == 0 and s > 0:
                            OP("tensor", lambda e, c=c, s=s, idx=idx, pb=pb: e.matmul(
                                pb[:, s:FT], diags[:, idx * P:(idx + 1) * P],
                                actT[:, c, 0:FT - s], start=False, stop=(s == 6)),
                               reads=[("diag", idx)] + hbT_keys, writes=[pk])
                        else:
                            OP("tensor", lambda e, c=c, s=s, idx=idx, pb=pb, fi=fi: e.matmul(
                                pb[:], diags[:, idx * P:(idx + 1) * P],
                                actT[:, c, fi * FT - s:(fi + 1) * FT - s],
                                start=(s == 0), stop=(s == 6)),
                               reads=[("diag", idx)] + hbT_keys + hbT_prev, writes=[pk])
                    OP("scalar", lambda e, c=c, pb=pb: e.copy(cma[0][:, c, :], pb[:]),
                       reads=[pk], writes=["cma0"])
                if "conv" in A_PARTS:
                    OP("gpsimd", lambda e, t0=t0: e.dma_start(
                        out=scr_loc[seq, :, :, t0:t0 + FT], in_=cma[0][:]),
                       reads=["cma0"], writes=[("scr_loc", seq, t0)], dma=True)

                for c in range(DC if "decay" in A_PARTS else 0):
                    pzd, pzdk = (ps0, "ps0") if c % 2 == 0 else (ps2, "ps2")
                    if c % 2 == 0:
                        bldc, bAcb, bcdc, becb, bucb, bCcb = (
                            ldc[:], Acb[:], cdc[:], ecb[:], ucb[:], Ccb[:])
                        kldc = kAcb = "ldc_a"
                        kcdc = kecb = "cde_a"
                        kucb = kCcb = "uc_a"
                    else:
                        bldc, bAcb = mixws[:, 0, 0:FT], mixws[:, 0, FT:2 * FT]
                        bcdc, becb = mixws[:, 1, 0:FT], mixws[:, 1, FT:2 * FT]
                        bucb, bCcb = accws[:, 0, 0:FT], accws[:, 0, FT:2 * FT]
                        kldc = kAcb = ("mixws", 0)
                        kcdc = kecb = ("mixws", 1)
                        kucb = kCcb = ("accws", 0)
                    for kc in range(DC):
                        OP("tensor", lambda e, c=c, kc=kc, fi=fi, pzd=pzd: e.matmul(
                            pzd[:], WdV[:, kc, c * P:(c + 1) * P],
                            actT[:, kc, fi * FT:(fi + 1) * FT],
                            start=(kc == 0), stop=(kc == DC - 1)),
                           reads=WdK + hbT_keys, writes=[pzdk])
                    OP("scalar", lambda e, c=c, pzd=pzd: e.activation(
                        pbig2[:, 0:FT], pzd[:], AF.Sigmoid, bias=bd_c[:, c:c + 1]),
                       reads=[pzdk, "bd"], writes=[("pbig2", 0)])
                    for kc in range(DC):
                        OP("tensor", lambda e, c=c, kc=kc, fi=fi: e.matmul(
                            ps1[:], WgV[:, kc, c * P:(c + 1) * P],
                            actT[:, kc, fi * FT:(fi + 1) * FT],
                            start=(kc == 0), stop=(kc == DC - 1)),
                           reads=WgK + hbT_keys, writes=["ps1"])
                    OP("scalar", lambda e, c=c: e.activation(
                        pbig2[:, FT:2 * FT], ps1[:], AF.Sigmoid, bias=bg_c[:, c:c + 1]),
                       reads=["ps1", "bg"], writes=[("pbig2", 1)])
                    OP("vector", lambda e, bldc=bldc: e.tensor_scalar(
                        bldc, pbig2[:, 0:FT], 0.95, 0.05, ALU.min, ALU.max),
                       reads=[("pbig2", 0)], writes=[kldc])
                    OP("scalar", lambda e, bldc=bldc: e.activation(bldc, bldc, AF.Ln),
                       reads=[kldc], writes=[kldc])
                    OP("vector", lambda e, c=c, bldc=bldc, bAcb=bAcb: e.tensor_tensor_scan(
                        bAcb, bldc, zerosF[:], carA[:, c:c + 1], ALU.add, ALU.add),
                       reads=[kldc, "zerosF", "carA"], writes=[kAcb])
                    OP("vector", lambda e, c=c, bldc=bldc: e.tensor_scalar(
                        bldc[:, 0:1], carA[:, c:c + 1], -1.0, LNC, ALU.mult, ALU.min),
                       reads=["carA"], writes=[kldc])
                    OP("vector", lambda e, bldc=bldc, bAcb=bAcb: e.tensor_scalar(
                        bldc[:, 1:FT], bAcb[:, 0:FT - 1], -1.0, LNC, ALU.mult, ALU.min),
                       reads=[kAcb], writes=[kldc])
                    OP("vector", lambda e, c=c, bAcb=bAcb: e.tensor_copy(
                        carA[:, c:c + 1], bAcb[:, FT - 1:FT]),
                       reads=[kAcb], writes=["carA"])
                    OP("scalar", lambda e, bcdc=bcdc, bAcb=bAcb: e.activation(bcdc, bAcb, AF.Exp),
                       reads=[kAcb], writes=[kcdc])
                    OP("scalar", lambda e, becb=becb, bldc=bldc: e.activation(becb, bldc, AF.Exp),
                       reads=[kldc], writes=[kecb])
                    OP("vector", lambda e, c=c, bucb=bucb: e.tensor_mul(
                        bucb, pbig2[:, FT:2 * FT], g32a[:, c, :]),
                       reads=[("pbig2", 1), "g32a"], writes=[kucb])
                    OP("vector", lambda e, bldc=bldc, bucb=bucb, becb=becb: e.tensor_mul(
                        bldc, bucb, becb),
                       reads=[kucb, kecb], writes=[kldc])
                    OP("vector", lambda e, c=c, bCcb=bCcb, bldc=bldc: e.tensor_tensor_scan(
                        bCcb, bldc, zerosF[:], carC[:, c:c + 1], ALU.add, ALU.add),
                       reads=[kldc, "zerosF", "carC"], writes=[kCcb])
                    OP("vector", lambda e, c=c, bCcb=bCcb: e.tensor_copy(
                        carC[:, c:c + 1], bCcb[:, FT - 1:FT]),
                       reads=[kCcb], writes=["carC"])
                    OP("vector", lambda e, bucb=bucb, bCcb=bCcb, bcdc=bcdc: e.tensor_mul(
                        bucb, bCcb, bcdc),
                       reads=[kCcb, kcdc], writes=[kucb])
                    OP("vector", lambda e, c=c, bucb=bucb: e.tensor_scalar_mul(
                        cma[1][:, c, :], bucb, nss[:, c:c + 1]),
                       reads=[kucb, "nss"], writes=["cma1"])
                if "decay" in A_PARTS:
                    OP("gpsimd", lambda e, t0=t0: e.dma_start(
                        out=scr_st[seq, :, :, t0:t0 + FT], in_=cma[1][:]),
                       reads=["cma1"], writes=[("scr_st", seq, t0)], dma=True)
            if DEBUG_SCRATCH and seq == 0:
                OP("gpsimd", lambda e: e.dma_start(out=scr_dbg, in_=actT[:]),
                   reads=[("actT", q) for q in range(NTILE)], writes=[], dma=True)
                OP("gpsimd", lambda e: e.dma_start(out=scr_dbg2, in_=g32a[:]),
                   reads=["g32a"], writes=[], dma=True)
                for i3, (tn, tk) in enumerate([(MUc[0], "MUc"), (Rc[0], "Rc"),
                                               (Sc[0], "Sc"), (NMSc[0], "NMSc")]):
                    OP("gpsimd", lambda e, i3=i3, tn=tn: e.dma_start(
                        out=scr_dbg3[:, i3, :], in_=tn[:]),
                       reads=[(tk, 0, q) for q in range(NTILE)], writes=[], dma=True)

        def phase_b(seq):
            for fi in range(LIMIT_NF or NF):
                t0 = fi * FT
                OP("sync", lambda e, t0=t0: e.dma_start(
                    out=cma[0][:], in_=scr_loc[seq, :, :, t0:t0 + FT]),
                   reads=[("scr_loc", seq, t0)], writes=["cma0"], dma=True)
                OP("sync", lambda e, t0=t0: e.dma_start(
                    out=cma[1][:], in_=scr_st[seq, :, :, t0:t0 + FT]),
                   reads=[("scr_st", seq, t0)], writes=["cma1"], dma=True)
                OP("sync", lambda e, t0=t0: e.dma_start(
                    out=cmb0[:], in_=scr_mk[seq, :, :, t0:t0 + FT]),
                   reads=[("scr_mk", seq, t0)], writes=["cmb0"], dma=True)
                for moc in range(DC):
                    pb, pk = (ps0, "ps0") if moc % 2 == 0 else (ps1, "ps1")
                    for jc in range(2):
                        OP("tensor", lambda e, moc=moc, jc=jc, pb=pb: e.matmul(
                            pb[:], WmV[:, jc, moc * P:(moc + 1) * P], cmb0[:, jc],
                            start=(jc == 0), stop=(jc == 1)),
                           reads=WmK + ["cmb0"], writes=[pk])
                    OP("vector", lambda e, moc=moc, pb=pb: e.tensor_copy(membf[:, moc], pb[:]),
                       reads=[pk], writes=membfK)
                for tt in range(TPF):
                    g = fi * TPF + tt
                    xv, xk = xws[:, g % 2], ("xws", g % 2)
                    pt, kb = (pbig, "pbig") if g % 2 == 0 else (pbig2, "pbig2")
                    mv, mk2 = mixws[:, g % 2], ("mixws", g % 2)
                    OP("sync", lambda e, g=g, xv=xv: e.dma_start(
                        out=xv, in_=x_in[seq, g * P:(g + 1) * P, :]),
                       writes=[xk], dma=True)
                    for half in range(2):
                        OP("tensor", lambda e, half=half, pt=pt: e.matmul(
                            pt[:, half * FT:(half + 1) * FT], ones_b[:],
                            bcV[:, half * FT:(half + 1) * FT], start=True, stop=False),
                           reads=["ones_b"] + bcK, writes=[(kb, half)])
                    for kc in range(24):
                        if kc < 8:
                            opnd, rk = cma[0][:, kc, tt * P:(tt + 1) * P], ["cma0"]
                        elif kc < 16:
                            opnd, rk = cma[1][:, kc - 8, tt * P:(tt + 1) * P], ["cma1"]
                        else:
                            opnd, rk = membf[:, kc - 16, tt * P:(tt + 1) * P], membfK
                        for half in range(2):
                            OP("tensor", lambda e, kc=kc, half=half, opnd=opnd, pt=pt: e.matmul(
                                pt[:, half * FT:(half + 1) * FT], opnd,
                                WcV[:, kc, half * FT:(half + 1) * FT],
                                start=False, stop=(kc == 23)),
                               reads=rk + WcK, writes=[(kb, half)])
                    OP("vector", lambda e, g=g, xv=xv: e.tensor_scalar(
                        fows[:], xv, MUc[seq][:, g:g + 1], Rc[seq][:, g:g + 1],
                        ALU.subtract, ALU.mult),
                       reads=[xk, ("MUc", seq, g), ("Rc", seq, g)], writes=["fows"])
                    for hh in range(2):
                        OP("vector", lambda e, hh=hh, pt=pt, mv=mv: e.tensor_add(
                            mv[:, hh * FT:(hh + 1) * FT], pt[:, hh * FT:(hh + 1) * FT],
                            fows[:, hh * FT:(hh + 1) * FT]),
                           reads=[(kb, hh), "fows"], writes=[mk2])
                    for hh in range(2):
                        OP("scalar", lambda e, hh=hh, mv=mv: e.activation(
                            sqh[:], mv[:, hh * FT:(hh + 1) * FT], AF.Square),
                           reads=[mk2], writes=["sqh"])
                        OP("vector", lambda e, hh=hh: e.reduce_sum(
                            stat_t[:, 1 + hh:2 + hh], sqh[:], axis=AX.X),
                           reads=["sqh"], writes=["stat"])
                    OP("vector", lambda e: e.tensor_add(stat_t[:, 1:2], stat_t[:, 1:2],
                                                        stat_t[:, 2:3]),
                       reads=["stat"], writes=["stat"])
                    OP("vector", lambda e: e.tensor_scalar_mul(stat_t[:, 4:5], stat_t[:, 1:2],
                                                               1.0 / D),
                       reads=["stat"], writes=["stat"])
                    OP("scalar", lambda e: e.activation(stat_t[:, 3:4], stat_t[:, 4:5],
                                                        AF.Ln, bias=1e-6),
                       reads=["stat"], writes=["stat"])
                    OP("scalar", lambda e, g=g: e.activation(
                        R2c[seq][:, g:g + 1], stat_t[:, 3:4], AF.Exp, scale=-0.5),
                       reads=["stat"], writes=[("R2c", seq, g)])
                    OP("gpsimd", lambda e, g=g, mv=mv: e.dma_start(
                        out=scr_mix[seq, g * P:(g + 1) * P, :], in_=mv),
                       reads=[mk2], writes=[("scr_mix", seq, g)], dma=True)

        def phase_c(seq):
            for g in range(NTILE):
                mv, mk2 = mixws[:, g % 2], ("mixws", g % 2)
                pt, kb = (pbig, "pbig") if g % 2 == 0 else (pbig2, "pbig2")
                OP("sync", lambda e, g=g, mv=mv: e.dma_start(
                    out=mv, in_=scr_mix[seq, g * P:(g + 1) * P, :]),
                   reads=[("scr_mix", seq, g)], writes=[mk2], dma=True)
                diag_transpose(mv, mk2, R2c[seq][:, g:g + 1], ("R2c", seq, g), False,
                               pt=pt, kb=kb, slot=g % 2)
                for c in range(DC):
                    OP("vector", lambda e, c=c, g=g, pt=pt: e.tensor_scalar_mul(
                        actT[:, c, g * P:(g + 1) * P], pt[:, c * P:(c + 1) * P],
                        n2w[:, c:c + 1]),
                       reads=[(kb, c // 4), "n2w"], writes=[("actT", g)])

            for q in range(4):
                hc0 = q * 1024
                load_w(WgqV, Wg[:, hc0:hc0 + 1024].rearrange("(c p) n -> p c n", p=P), WgqK)
                load_w(WuqV, Wup[:, hc0:hc0 + 1024].rearrange("(c p) n -> p c n", p=P), WuqK)
                load_w(WdqV, Wdown[hc0:hc0 + 1024, :].rearrange("(c p) n -> p c n", p=P), WdqK)
                for fi in range(NF):
                    t0 = fi * FT
                    hbF = [("actT", fi * TPF + qq) for qq in range(TPF)]
                    for fc in range(8):
                        if fc % 2 == 0:
                            pg, pgk = ps0[:], "ps0"
                            pu, puk = ps1[:], "ps1"
                        else:
                            pg, pgk = pbig[:, 0:FT], ("pbig", 0)
                            pu, puk = pbig[:, FT:2 * FT], ("pbig", 1)
                        slu = sluws[:, 0]
                        sluk = "sluws"
                        for kc in range(DC):
                            OP("tensor", lambda e, fc=fc, kc=kc, t0=t0, pg=pg: e.matmul(
                                pg, WgqV[:, kc, fc * P:(fc + 1) * P],
                                actT[:, kc, t0:t0 + FT],
                                start=(kc == 0), stop=(kc == DC - 1)),
                               reads=WgqK + hbF, writes=[pgk])
                        for kc in range(DC):
                            OP("tensor", lambda e, fc=fc, kc=kc, t0=t0, pu=pu: e.matmul(
                                pu, WuqV[:, kc, fc * P:(fc + 1) * P],
                                actT[:, kc, t0:t0 + FT],
                                start=(kc == 0), stop=(kc == DC - 1)),
                               reads=WuqK + hbF, writes=[puk])
                        OP("scalar", lambda e, pg=pg, slu=slu: e.activation(slu, pg, AF.Silu),
                           reads=[pgk], writes=[sluk])
                        OP("vector", lambda e, fc=fc, pu=pu, slu=slu: e.tensor_mul(
                            cma[0][:, fc, :], slu, pu),
                           reads=[sluk, puk], writes=["cma0"])
                    for tt in range(TPF):
                        g = fi * TPF + tt
                        av, avk = accws[:, g % 2], ("accws", g % 2)
                        mv, mk2 = mixws[:, g % 2], ("mixws", g % 2)
                        for fc in range(8):
                            for hh in range(2):
                                OP("tensor", lambda e, fc=fc, hh=hh, tt=tt: e.matmul(
                                    pbig2[:, hh * FT:(hh + 1) * FT],
                                    cma[0][:, fc, tt * P:(tt + 1) * P],
                                    WdqV[:, fc, hh * FT:(hh + 1) * FT],
                                    start=(fc == 0), stop=(fc == 7)),
                                   reads=["cma0"] + WdqK, writes=[("pbig2", hh)])
                        if q < 3:
                            if q == 0:
                                for hh in range(2):
                                    OP("scalar", lambda e, hh=hh, av=av: e.copy(
                                        av[:, hh * FT:(hh + 1) * FT],
                                        pbig2[:, hh * FT:(hh + 1) * FT]),
                                       reads=[("pbig2", hh)], writes=[avk])
                            else:
                                OP("sync", lambda e, g=g, av=av: e.dma_start(
                                    out=av, in_=scr_ffa[seq, g * P:(g + 1) * P, :]),
                                   reads=[("scr_ffa", seq, g)], writes=[avk], dma=True)
                                for hh in range(2):
                                    OP("vector", lambda e, hh=hh, av=av: e.tensor_add(
                                        av[:, hh * FT:(hh + 1) * FT],
                                        av[:, hh * FT:(hh + 1) * FT],
                                        pbig2[:, hh * FT:(hh + 1) * FT]),
                                       reads=[("pbig2", hh), avk], writes=[avk])
                            OP("gpsimd", lambda e, g=g, av=av: e.dma_start(
                                out=scr_ffa[seq, g * P:(g + 1) * P, :], in_=av),
                               reads=[avk], writes=[("scr_ffa", seq, g)], dma=True)
                        else:
                            OP("sync", lambda e, g=g, av=av: e.dma_start(
                                out=av, in_=scr_ffa[seq, g * P:(g + 1) * P, :]),
                               reads=[("scr_ffa", seq, g)], writes=[avk], dma=True)
                            OP("sync", lambda e, g=g, mv=mv: e.dma_start(
                                out=mv, in_=scr_mix[seq, g * P:(g + 1) * P, :]),
                               reads=[("scr_mix", seq, g)], writes=[mk2], dma=True)
                            OP("sync", lambda e, g=g: e.dma_start(
                                out=x1ws[:], in_=x_in[seq, g * P:(g + 1) * P, :]),
                               writes=["x1ws"], dma=True)
                            for hh in range(2):
                                OP("vector", lambda e, hh=hh, av=av: e.tensor_add(
                                    av[:, hh * FT:(hh + 1) * FT],
                                    av[:, hh * FT:(hh + 1) * FT],
                                    pbig2[:, hh * FT:(hh + 1) * FT]),
                                   reads=[("pbig2", hh), avk], writes=[avk])
                            OP("vector", lambda e, av=av, mv=mv: e.tensor_add(fows[:], av, mv),
                               reads=[avk, mk2], writes=["fows"])
                            stats_pass("fows", fows[:], stat_t[:, 5:6], "stat",
                                       stat_t[:, 6:7], "stat")
                            OP("vector", lambda e: e.tensor_scalar(
                                fows[:], fows[:], stat_t[:, 5:6], stat_t[:, 6:7],
                                ALU.subtract, ALU.mult),
                               reads=["fows", "stat"], writes=["fows"])
                            OP("vector", lambda e: e.tensor_add(x1ws[:], x1ws[:], fows[:]),
                               reads=["fows", "x1ws"], writes=["x1ws"])
                            stats_pass("x1ws", x1ws[:], MU4c[seq][:, g:g + 1],
                                       ("MU4c", seq, g), R4c[seq][:, g:g + 1], ("R4c", seq, g))
                            OP("gpsimd", lambda e, g=g: e.dma_start(
                                out=scr_x1[seq, g * P:(g + 1) * P, :], in_=x1ws[:]),
                               reads=["x1ws"], writes=[("scr_x1", seq, g)], dma=True)

        def phase_d(seq):
            for g in range(NTILE):
                pt, kb = (pbig, "pbig") if g % 2 == 0 else (pbig2, "pbig2")
                OP("sync", lambda e, g=g: e.dma_start(
                    out=x1ws[:], in_=scr_x1[seq, g * P:(g + 1) * P, :]),
                   reads=[("scr_x1", seq, g)], writes=["x1ws"], dma=True)
                OP("vector", lambda e, g=g: e.scalar_tensor_tensor(
                    stat_t[:, 7:8], MU4c[seq][:, g:g + 1], -1.0, R4c[seq][:, g:g + 1],
                    ALU.mult, ALU.mult),
                   reads=[("MU4c", seq, g), ("R4c", seq, g)], writes=["stat"])
                col_to_row(stat_t[:, 7:8], "stat", slot=g % 2)
                diag_transpose(x1ws[:], "x1ws", R4c[seq][:, g:g + 1], ("R4c", seq, g), True,
                               pt=pt, kb=kb, slot=g % 2)
                for hh in range(2):
                    OP("scalar", lambda e, g=g, hh=hh, pt=pt: e.copy(
                        actT[:, hh * 4:(hh + 1) * 4, g * P:(g + 1) * P],
                        pt[:, hh * FT:(hh + 1) * FT].rearrange("p (c q) -> p c q", q=P)),
                       reads=[(kb, hh)], writes=[("actT", g)])
                for kc in range(DC):
                    OP("tensor", lambda e, kc=kc, g=g: e.matmul(
                        ps2[:, 0:E], actT[:, kc, g * P:(g + 1) * P], gateW_b[:, kc, :],
                        start=(kc == 0), stop=(kc == DC - 1)),
                       reads=["gateW", ("actT", g)], writes=["ps2"])
                OP("scalar", lambda e: e.activation(scb[:], ps2[:, 0:E], AF.Sigmoid),
                   reads=["ps2"], writes=["scb"])
                OP("vector", lambda e: e.tensor_scalar_mul(scb[:], scb[:], rscol[:]),
                   reads=["scb", "rsv"], writes=["scb"])
                bc_v1 = bass.broadcast_tensor_aps(scb[:], stat_t[:, 0:1])
                bc_v2 = bass.broadcast_tensor_aps(scb[:], stat_t[:, 1:2])
                OP("vector", lambda e: e.reduce_max(stat_t[:, 0:1], scb[:], axis=AX.X),
                   reads=["scb"], writes=["stat"])
                OP("vector", lambda e, bc=bc_v1: e.tensor_tensor(scb2[:], bc[0], bc[1], ALU.is_ge),
                   reads=["scb", "stat"], writes=["scb2"])
                OP("vector", lambda e: e.scalar_tensor_tensor(
                    scb2[:], scb2[:], -1e30, scb[:], ALU.mult, ALU.add),
                   reads=["scb", "scb2"], writes=["scb2"])
                OP("vector", lambda e: e.reduce_max(stat_t[:, 1:2], scb2[:], axis=AX.X),
                   reads=["scb2"], writes=["stat"])
                OP("vector", lambda e, bc=bc_v2: e.tensor_tensor(scb2[:], bc[0], bc[1], ALU.is_ge),
                   reads=["scb", "stat"], writes=["scb2"])
                OP("vector", lambda e: e.tensor_mul(scb2[:], scb2[:], scb[:]),
                   reads=["scb", "scb2"], writes=["scb2"])
                OP("vector", lambda e: e.reduce_sum(stat_t[:, 2:3], scb2[:], axis=AX.X),
                   reads=["scb2"], writes=["stat"])
                OP("vector", lambda e: e.reciprocal(stat_t[:, 3:4], stat_t[:, 2:3]),
                   reads=["stat"], writes=["stat"])
                OP("vector", lambda e, g=g: e.tensor_scalar_mul(
                    GF[:, g, :], scb2[:], stat_t[:, 3:4]),
                   reads=["scb2", "stat"], writes=[("GF", g)])
                OP("tensor", lambda e, g=g: e.matmul(
                    prow[:], GF[:, g, :], eye_f[:], start=True, stop=True),
                   reads=[("GF", g), "eye_f"], writes=["prow"])
                OP("vector", lambda e, g=g: e.tensor_copy(GCM[:, g * P:(g + 1) * P], prow[:]),
                   reads=["prow"], writes=[("GCM", g)])

            load_w(Ws1V, Ws1.rearrange("(c p) n -> p c n", p=P), Ws1K)
            load_w(Ws2V, Ws2.rearrange("(c p) n -> p c n", p=P), Ws2K)
            load_w(bs1V, bs1_i, bs1K)
            load_w(bs2V, bs2_i, bs2K)
            load_w(b2V, b2_i, b2K)
            for g in range(NTILE):
                gc0 = g * P
                pt, kb = (pbig2, "pbig2") if g % 2 == 0 else (pbig, "pbig")
                psh, pshk = (ps0, "ps0") if g % 2 == 0 else (ps2, "ps2")
                gsv, gsk = gscb[:, g % 2], ("gscb", g % 2)
                av, avk = accws[:, g % 2], ("accws", g % 2)
                for hh in range(2):
                    OP("tensor", lambda e, hh=hh, pt=pt: e.matmul(
                        pt[:, hh * FT:(hh + 1) * FT], ones_b[:],
                        bs2V[:, hh * FT:(hh + 1) * FT], start=True, stop=False),
                       reads=["ones_b"] + bs2K, writes=[(kb, hh)])
                    OP("tensor", lambda e, hh=hh, g=g, gc0=gc0, pt=pt: e.matmul(
                        pt[:, hh * FT:(hh + 1) * FT], GCM[:, gc0:gc0 + P],
                        b2V[:, hh * FT:(hh + 1) * FT], start=False, stop=False),
                       reads=[("GCM", g)] + b2K, writes=[(kb, hh)])
                OP("tensor", lambda e, psh=psh: e.matmul(
                    psh[:], ones_b[:], bs1V[:], start=True, stop=False),
                   reads=["ones_b"] + bs1K, writes=[pshk])
                for kc in range(DC):
                    OP("tensor", lambda e, kc=kc, g=g, gc0=gc0, psh=psh: e.matmul(
                        psh[:], actT[:, kc, gc0:gc0 + P], Ws1V[:, kc, :],
                        start=False, stop=(kc == DC - 1)),
                       reads=Ws1K + [("actT", g)], writes=[pshk])
                OP("scalar", lambda e, psh=psh, gsv=gsv: e.activation(gsv, psh[:], AF.Gelu),
                   reads=[pshk], writes=[gsk])
                for hcc in range(4):
                    OP("tensor", lambda e, hcc=hcc, gsv=gsv, g=g: e.matmul(
                        ps1[:, 0:P], gsv[:, hcc * P:(hcc + 1) * P], eye_b[:],
                        start=True, stop=True),
                       reads=[gsk, "eye_b"], writes=["ps1"])
                    OP("vector", lambda e, hcc=hcc, g=g: e.tensor_copy(
                        gcmw[:, 0, hcc], ps1[:, 0:P]),
                       reads=["ps1"], writes=[("gcmw", hcc)])
                for hcc in range(4):
                    for hh in range(2):
                        OP("tensor", lambda e, hcc=hcc, hh=hh, g=g, pt=pt: e.matmul(
                            pt[:, hh * FT:(hh + 1) * FT], gcmw[:, 0, hcc],
                            Ws2V[:, hcc, hh * FT:(hh + 1) * FT],
                            start=False, stop=(hcc == 3)),
                           reads=[("gcmw", hcc)] + Ws2K, writes=[(kb, hh)])
                for hh in range(2):
                    OP("scalar", lambda e, hh=hh, pt=pt, av=av: e.copy(
                        av[:, hh * FT:(hh + 1) * FT], pt[:, hh * FT:(hh + 1) * FT]),
                       reads=[(kb, hh)], writes=[avk])
                OP("gpsimd", lambda e, g=g, gc0=gc0, av=av: e.dma_start(
                    out=scr_fna[seq, gc0:gc0 + P, :], in_=av),
                   reads=[avk], writes=[("scr_fna", seq, g)], dma=True)

            for dh in range(2):
                for ei in range(4):
                    ex = dh * 4 + ei
                    load_w(W1hV[:, ei], W1[ex].rearrange("(c p) h -> p c h", p=P),
                           wstrips(ei * 4096, 4096))
                    load_w(W2hV[:, ei], W2[ex].rearrange("(c p) n -> p c n", p=P),
                           wstrips(16384 + ei * 4096, 4096))
                for g in range(NTILE):
                    gc0 = g * P
                    pt, kb = (pbig2, "pbig2") if g % 2 == 0 else (pbig, "pbig")
                    av, avk = accws[:, g % 2], ("accws", g % 2)
                    for ei in range(4):
                        ex = dh * 4 + ei
                        pe1, pe1k = (ps0, "ps0") if ei % 2 == 0 else (ps2, "ps2")
                        slu, sluk = sluws[:, 0], "sluws"
                        gsv, gsk = gscb[:, ei % 2], ("gscb", ei % 2)
                        OP("tensor", lambda e, ex=ex, pe1=pe1: e.matmul(
                            pe1[:], ones_b[:], b1_r[:, ex, :], start=True, stop=False),
                           reads=["ones_b", "b1"], writes=[pe1k])
                        for kc in range(DC):
                            OP("tensor", lambda e, kc=kc, ei=ei, g=g, gc0=gc0,
                               pe1=pe1: e.matmul(
                                pe1[:], actT[:, kc, gc0:gc0 + P], W1hV[:, ei, kc, :],
                                start=False, stop=(kc == DC - 1)),
                               reads=wstrips(ei * 4096, 4096) + [("actT", g)], writes=[pe1k])
                        OP("scalar", lambda e, pe1=pe1, slu=slu: e.activation(
                            slu, pe1[:], AF.Gelu),
                           reads=[pe1k], writes=[sluk])
                        OP("vector", lambda e, ex=ex, g=g, slu=slu, gsv=gsv: e.tensor_scalar_mul(
                            gsv, slu, GF[:, g, ex:ex + 1]),
                           reads=[sluk, ("GF", g)], writes=[gsk])
                        for hcc in range(4):
                            OP("tensor", lambda e, hcc=hcc, gsv=gsv: e.matmul(
                                ps1[:, 0:P], gsv[:, hcc * P:(hcc + 1) * P], eye_b[:],
                                start=True, stop=True),
                               reads=[gsk, "eye_b"], writes=["ps1"])
                            OP("vector", lambda e, hcc=hcc, ei=ei: e.tensor_copy(
                                gcmw[:, 0, hcc], ps1[:, 0:P]),
                               reads=["ps1"], writes=[("gcmw", hcc)])
                        for hcc in range(4):
                            for hh in range(2):
                                st_f = (ei == 0 and hcc == 0)
                                OP("tensor", lambda e, hcc=hcc, hh=hh, ei=ei, pt=pt,
                                   st=st_f: e.matmul(
                                    pt[:, hh * FT:(hh + 1) * FT], gcmw[:, 0, hcc],
                                    W2hV[:, ei, hcc, hh * FT:(hh + 1) * FT],
                                    start=st, stop=(ei == 3 and hcc == 3)),
                                   reads=[("gcmw", hcc)]
                                       + wstrips(16384 + ei * 4096, 4096),
                                   writes=[(kb, hh)])
                    OP("sync", lambda e, g=g, gc0=gc0, av=av: e.dma_start(
                        out=av, in_=scr_fna[seq, gc0:gc0 + P, :]),
                       reads=[("scr_fna", seq, g)], writes=[avk], dma=True)
                    for hh in range(2):
                        OP("vector", lambda e, hh=hh, pt=pt, av=av: e.tensor_add(
                            av[:, hh * FT:(hh + 1) * FT], av[:, hh * FT:(hh + 1) * FT],
                            pt[:, hh * FT:(hh + 1) * FT]),
                           reads=[(kb, hh), avk], writes=[avk])
                    if dh == 0:
                        OP("gpsimd", lambda e, g=g, gc0=gc0, av=av: e.dma_start(
                            out=scr_fna[seq, gc0:gc0 + P, :], in_=av),
                           reads=[avk], writes=[("scr_fna", seq, g)], dma=True)
                    else:
                        OP("sync", lambda e, g=g, gc0=gc0: e.dma_start(
                            out=x1ws[:], in_=scr_x1[seq, gc0:gc0 + P, :]),
                           reads=[("scr_x1", seq, g)], writes=["x1ws"], dma=True)
                        stats_pass(avk, av, stat_t[:, 5:6], "stat",
                                   stat_t[:, 6:7], "stat")
                        OP("vector", lambda e, av=av: e.tensor_scalar(
                            av, av, stat_t[:, 5:6], stat_t[:, 6:7],
                            ALU.subtract, ALU.mult),
                           reads=[avk, "stat"], writes=[avk])
                        OP("vector", lambda e, av=av: e.tensor_add(av, av, x1ws[:]),
                           reads=["x1ws", avk], writes=[avk])
                        OP("gpsimd", lambda e, g=g, gc0=gc0, av=av: e.dma_start(
                            out=y_out[seq, gc0:gc0 + P, :], in_=av),
                           reads=[avk], writes=[], dma=True)

        setup()
        if "A" in PHASES:
            load_w(WdV, Wdecay.rearrange("(c p) n -> p c n", p=P), WdK)
            load_w(WgV, Wgate.rearrange("(c p) n -> p c n", p=P), WgK)
            load_w(WkV, Wk.rearrange("(c p) n -> p c n", p=P), WkK)
            for seq in range(LIMIT_NSEQ or NSEQ):
                phase_a(seq)
        if "B" in PHASES:
            load_w(WcV, Wc.rearrange("(c p) n -> p c n", p=P), WcK)
            load_w(WmV, Wmem.rearrange("(c p) n -> p c n", p=P), WmK)
            load_w(bcV, bc_i, bcK)
            for seq in range(LIMIT_NSEQ or NSEQ):
                phase_b(seq)
        if "C" in PHASES:
            for seq in range(LIMIT_NSEQ or NSEQ):
                phase_c(seq)
        if "D" in PHASES:
            OP("gpsimd", lambda e: e.dma_start(out=b1_r, in_=b1_i),
               reads=[], writes=b1K, dma=True)
            for seq in range(LIMIT_NSEQ or NSEQ):
                phase_d(seq)

        prog.finalize()

        with nc.Block() as block:
            @block.sync
            def _(eng):
                prog.emit("sync", eng)

            @block.gpsimd
            def _(eng):
                prog.emit("gpsimd", eng)
                prog.fence(eng)

            @block.tensor
            def _(eng):
                prog.emit("tensor", eng)

            @block.vector
            def _(eng):
                prog.emit("vector", eng)

            @block.scalar
            def _(eng):
                prog.emit("scalar", eng)

    return nc


_nc_cache = None


def kernel(**inputs):
    global _nc_cache
    f32 = np.float32

    def cont(a):
        return np.ascontiguousarray(a, dtype=f32)

    x = cont(inputs["L0_attn_x"])
    conv_w = cont(inputs["L0_attn_conv_w"])
    aux = {
        "Wdecay": cont(inputs["L0_attn_Wdecay"]),
        "Wgate": cont(inputs["L0_attn_Wgate"]),
        "Wk": cont(inputs["L0_attn_Wk"]),
        "MT": cont(inputs["L0_attn_M"].T),
        "Wmem": cont(inputs["L0_attn_Wmem"]),
        "Wc": cont(inputs["L0_attn_Wc"]),
        "Wg": cont(inputs["L0_attn_Wg"]),
        "Wup": cont(inputs["L0_attn_Wup"]),
        "Wdown": cont(inputs["L0_attn_Wdown"]),
        "W1": cont(inputs["L0_ffn_W1"]),
        "W2": cont(inputs["L0_ffn_W2"]),
        "Ws1": cont(inputs["L0_ffn_Ws1"]),
        "Ws2": cont(inputs["L0_ffn_Ws2"]),
        "gateW": cont(inputs["L0_ffn_gate_W"]),
        "n1w_c": cont(inputs["L0_attn_norm1_w"].reshape(DC, P).T),
        "n2w_c": cont(inputs["L0_attn_norm2_w"].reshape(DC, P).T),
        "bd_c": cont(inputs["L0_attn_bdecay"].reshape(DC, P).T),
        "bg_c": cont(inputs["L0_attn_bgate"].reshape(DC, P).T),
        "cw_cols": cont(conv_w.reshape(DC, P, 7).transpose(1, 0, 2).reshape(P, DC * 7)),
        "bc_row": cont(inputs["L0_attn_bc"].reshape(1, D)),
        "bs1_row": cont(inputs["L0_ffn_bs1"].reshape(1, H)),
        "bs2_row": cont(inputs["L0_ffn_bs2"].reshape(1, D)),
        "b1_rows": cont(inputs["L0_ffn_b1"].reshape(1, E, H)),
        "b2_pm": cont(inputs["L0_ffn_b2"]),
        "ss": cont(np.reshape(inputs["L0_attn_state_scale"], (1, 1))),
        "ms": cont(np.reshape(inputs["L0_attn_mem_scale"], (1, 1))),
        "rs": cont(np.reshape(inputs["L0_ffn_route_scale"], (1, 1))),
    }
    if _nc_cache is None:
        _nc_cache = build()
    nc = _nc_cache
    in_maps = []
    for c in range(8):
        m = dict(aux)
        m["x"] = x[c * NSEQ:(c + 1) * NSEQ]
        in_maps.append(m)
    res = run_bass_kernel_spmd(nc, in_maps, core_ids=list(range(8)))
    return np.concatenate([r["y"] for r in res.results], axis=0)


# version 07
# speedup vs baseline: 1.0043x; 1.0043x over previous
import numpy as np
from contextlib import ExitStack

import concourse.bass as bass
import concourse.mybir as mybir
from concourse.bass_utils import run_bass_kernel_spmd

F32 = mybir.dt.float32
BF16 = mybir.dt.bfloat16
AF = mybir.ActivationFunctionType
ALU = mybir.AluOpType
AX = mybir.AxisListType

P = 128
T = 2048
NSEQ = 2
D = 1024
DC = D // P
FT = 512
NF = T // FT
NTILE = T // P
TPF = FT // P
DM = 256
DF = 4096
E = 8
H = 512
LNC = 16.11809565095832

PHASES = ("A", "B", "C", "D")
DEBUG_SCRATCH = False
LIMIT_NF = None
LIMIT_NSEQ = None
A_PARTS = ("tiles", "stats", "c2r", "diag", "evac", "kmem", "conv", "decay")


class Rec:
    __slots__ = ("eng", "fn", "reads", "writes", "dma", "flag", "semval", "idx", "deps",
                 "slot", "use")

    def __init__(self, eng, fn, reads, writes, dma):
        self.eng, self.fn = eng, fn
        self.reads, self.writes = list(reads), list(writes)
        self.dma = dma
        self.flag = dma
        self.semval = None
        self.deps = []
        self.slot = self.use = 0


NSLOT = 8


class Prog:
    ENGINES = ("sync", "gpsimd", "tensor", "vector", "scalar")

    def __init__(self, nc, es):
        self.nc = nc
        self.ops = []
        self.last_w = {}
        self.readers = {}
        self.c_sem = {e: es.enter_context(nc.semaphore(f"c_{e}")) for e in self.ENGINES}
        self.d_sem = {e: [es.enter_context(nc.semaphore(f"d_{e}{i}")) for i in range(NSLOT)]
                      for e in ("sync", "gpsimd", "scalar")}
        self.dma_count = {e: 0 for e in ("sync", "gpsimd", "scalar")}

    PSUMNAMES = {"pbig", "pbig2", "ps0", "ps1", "ps2", "prow"}

    def op(self, eng, fn, reads=(), writes=(), dma=False):
        o = Rec(eng, fn, reads, writes, dma)
        o.idx = len(self.ops)
        if dma:
            k = self.dma_count[eng]
            o.slot, o.use = k % NSLOT, k // NSLOT
            o.semval = 16 * (o.use + 1)
            self.dma_count[eng] = k + 1
        deps = {}
        for k in o.reads:
            d = self.last_w.get(k)
            if d is not None:
                deps[d.idx] = d
        for k in o.writes:
            for d in self.readers.get(k, ()):
                deps[d.idx] = d
            d = self.last_w.get(k)
            if d is not None:
                deps[d.idx] = d
        if not dma and eng in ("scalar", "vector"):
            if not hasattr(self, "_last_psum_by_key"):
                self._last_psum_by_key = {}
            for k in list(o.reads) + list(o.writes):
                if (k if isinstance(k, str) else k[0]) in self.PSUMNAMES:
                    prev = self._last_psum_by_key.get(k)
                    if prev is not None and prev.eng != eng:
                        deps[prev.idx] = prev
                    self._last_psum_by_key[k] = o
        o.deps = list(deps.values())
        for k in o.writes:
            self.last_w[k] = o
            self.readers[k] = []
        for k in o.reads:
            self.readers.setdefault(k, []).append(o)
        self.ops.append(o)
        return o

    def finalize(self):
        for o in self.ops:
            for d in o.deps:
                if d.eng == o.eng == "tensor" and not d.dma:
                    continue
                d.flag = True
        cnt = {}
        for o in self.ops:
            if o.flag and not o.dma:
                cnt[o.eng] = cnt.get(o.eng, 0) + 1
                o.semval = cnt[o.eng]

    def emit(self, eng_name, eng):
        watermark = {}

        def wait(sem, v):
            if watermark.get(id(sem), -1) < v:
                eng.wait_ge(sem, v)
                watermark[id(sem)] = v

        for o in self.ops:
            if o.eng != eng_name:
                continue
            waits = {}
            for d in o.deps:
                if d.eng == o.eng and not d.dma and (o.eng == "tensor" or not d.flag):
                    continue
                sem = self.d_sem[d.eng][d.slot] if d.dma else self.c_sem[d.eng]
                cur = waits.get(id(sem))
                if cur is None or d.semval > cur[1]:
                    waits[id(sem)] = (sem, d.semval)
            for sem, v in waits.values():
                wait(sem, v)
            if o.dma and o.use > 0:
                wait(self.d_sem[o.eng][o.slot], 16 * o.use)
            inst = o.fn(eng)
            if o.flag:
                if o.dma:
                    inst.then_inc(self.d_sem[o.eng][o.slot], 16)
                else:
                    inst.then_inc(self.c_sem[o.eng], 1)

    def fence(self, eng):
        for e2 in ("sync", "gpsimd", "scalar"):
            n = self.dma_count[e2]
            for i in range(NSLOT):
                uses = (n - i + NSLOT - 1) // NSLOT if n > i else 0
                if uses > 0:
                    eng.wait_ge(self.d_sem[e2][i], 16 * uses)


def build():
    nc = bass.Bass("TRN2", target_bir_lowering=False, debug=False)

    def din(name, shape, dt=F32):
        return nc.dram_tensor(name, list(shape), dt, kind="ExternalInput").ap()

    x_in = din("x", [NSEQ, T, D])
    y_out = nc.dram_tensor("y", [NSEQ, T, D], F32, kind="ExternalOutput").ap()

    Wdecay = din("Wdecay", [D, D]); Wgate = din("Wgate", [D, D])
    Wk = din("Wk", [D, DM]); MT = din("MT", [DM, DM]); Wmem = din("Wmem", [DM, D])
    Wc = din("Wc", [3 * D, D])
    Wg = din("Wg", [D, DF]); Wup = din("Wup", [D, DF]); Wdown = din("Wdown", [DF, D])
    W1 = din("W1", [E, D, H]); W2 = din("W2", [E, H, D])
    Ws1 = din("Ws1", [D, H]); Ws2 = din("Ws2", [H, D])
    gateW = din("gateW", [D, E])
    n1w_i = din("n1w_c", [P, DC]); n2w_i = din("n2w_c", [P, DC])
    bd_i = din("bd_c", [P, DC]); bg_i = din("bg_c", [P, DC])
    cw_i = din("cw_cols", [P, DC * 7])
    bc_i = din("bc_row", [1, D]); bs1_i = din("bs1_row", [1, H]); bs2_i = din("bs2_row", [1, D])
    b1_i = din("b1_rows", [1, E, H]); b2_i = din("b2_pm", [E, D])
    ss_i = din("ss", [1, 1]); ms_i = din("ms", [1, 1]); rs_i = din("rs", [1, 1])

    skind = "ExternalOutput" if DEBUG_SCRATCH else "Internal"
    scr_loc = nc.dram_tensor("scr_loc", [NSEQ, P, DC, T], BF16, kind=skind).ap()
    scr_st = nc.dram_tensor("scr_st", [NSEQ, P, DC, T], BF16, kind=skind).ap()
    scr_mk = nc.dram_tensor("scr_mk", [NSEQ, P, 2, T], BF16, kind=skind).ap()
    scr_mix = nc.dram_tensor("scr_mix", [NSEQ, T, D], F32, kind=skind).ap()
    scr_x1 = nc.dram_tensor("scr_x1", [NSEQ, T, D], F32, kind=skind).ap()
    scr_ffa = nc.dram_tensor("scr_ffa", [NSEQ, T, D], F32, kind=skind).ap()
    scr_fna = nc.dram_tensor("scr_fna", [NSEQ, T, D], F32, kind=skind).ap()
    scr_dbg = nc.dram_tensor("scr_dbg", [P, DC, T], BF16, kind=skind).ap()
    scr_dbg2 = nc.dram_tensor("scr_dbg2", [P, DC, FT], F32, kind=skind).ap()
    scr_dbg3 = nc.dram_tensor("scr_dbg3", [P, 4, NTILE], F32, kind=skind).ap()

    eye_i = nc.inline_tensor(np.eye(P, dtype=np.float32), "eye_i").ap()

    es = ExitStack()
    with es:
        def sb(name, shape, dt=F32):
            return es.enter_context(nc.sbuf_tensor(name, list(shape), dt))

        def psum(name, shape):
            return es.enter_context(nc.psum_tensor(name, list(shape), F32))

        eye_f = sb("eye_f", [P, P]); eye_b = sb("eye_b", [P, P], BF16)
        ones_f = sb("ones_f", [1, P]); ones_b = sb("ones_b", [1, P], BF16)
        zerosF = sb("zerosF", [P, FT])
        consts = sb("consts", [P, 8])
        n1w = sb("n1w", [P, DC]); n2w = sb("n2w", [P, DC])
        bd_c = sb("bdcol", [P, DC]); bg_c = sb("bgcol", [P, DC])
        cw_c = sb("cwcol", [P, DC * 7])
        diags = sb("diags", [P, DC * 7 * P], BF16)
        b1_r = diags[0:1, 0:E * H].rearrange("o (e2 h2) -> o e2 h2", e2=E)
        b1K = ["b1"] + [("diag", i) for i in range(32)]
        MT_b = sb("MT_b", [P, 2, DM], BF16)
        gateW_b = sb("gateW_b", [P, DC, E], BF16)
        sscol = sb("sscol", [P, 1]); mscol = sb("mscol", [P, 1]); rscol = sb("rscol", [P, 1])
        nss = sb("nss", [P, DC])
        carA = sb("carA", [P, DC]); carC = sb("carC", [P, DC])

        MUc = [sb(f"MUc{s}", [P, NTILE]) for s in range(NSEQ)]
        Rc = [sb(f"Rc{s}", [P, NTILE]) for s in range(NSEQ)]
        Sc = [sb(f"Sc{s}", [P, NTILE]) for s in range(NSEQ)]
        NMSc = [sb(f"NMSc{s}", [P, NTILE]) for s in range(NSEQ)]
        R2c = [sb(f"R2c{s}", [P, NTILE]) for s in range(NSEQ)]
        MU4c = [sb(f"MU4c{s}", [P, NTILE]) for s in range(NSEQ)]
        R4c = [sb(f"R4c{s}", [P, NTILE]) for s in range(NSEQ)]
        GF = sb("GFt", [P, NTILE, E])
        GCM = sb("GCMt", [E, T], BF16)
        stat_t = sb("stat_t", [P, 8])
        rowb = sb("rowb", [1, 2 * P])

        actT = sb("actT", [P, DC, T], BF16)
        wpool = sb("wpool", [P, 32768], BF16)

        g32a = sb("g32a", [P, DC, FT])
        ldc = sb("ldc", [P, FT]); Acb = sb("Acb", [P, FT]); cdc = sb("cdc", [P, FT])
        ecb = sb("ecb", [P, FT]); ucb = sb("ucb", [P, FT]); Ccb = sb("Ccb", [P, FT])
        sluws = sb("sluws", [P, 1, FT])
        cma = [sb(f"cma{i}", [P, DC, FT], BF16) for i in range(2)]
        kbf = cma[1]
        cmb0 = sb("cmb0", [P, 2, FT], BF16)
        xws = sb("xws", [P, 2, D])
        sqh = sb("sqh", [P, FT])
        diagws = sb("diagws", [P, 1, P])
        mixws = sb("mixws", [P, 2, D])
        accws = sb("accws", [P, 2, D])
        fows = sb("fows", [P, D])
        x1ws = sb("x1ws", [P, D])
        gscb = sb("gscb", [P, 2, H], BF16)
        gcmw = sb("gcmw", [P, 1, 4, P], BF16)
        scb = sb("scb", [P, E])
        scb2 = sb("scb2", [P, E])

        pbig = psum("pbig", [P, 1024])
        pbig2 = psum("pbig2", [P, 1024])
        ps0 = psum("ps0", [P, FT]); ps1 = psum("ps1", [P, FT]); ps2 = psum("ps2", [P, FT])
        prow = psum("prow", [E, P])

        cvals = [1e-5, 1e-6, -0.5, -1.0]
        for i, cv in enumerate(cvals):
            nc.const_aps.register(F32, cv, consts[:, i:i + 1])

        prog = Prog(nc, es)
        OP = prog.op

        def wstrips(off, n):
            return [f"wp{j}" for j in range(off // 4096, (off + n - 1) // 4096 + 1)]

        def wview(off, shape):
            n = int(np.prod(shape))
            ap = wpool[:, off:off + n]
            if len(shape) > 1:
                pat = "p (" + " ".join(f"d{i}" for i in range(len(shape))) + ") -> p " + \
                      " ".join(f"d{i}" for i in range(len(shape)))
                ap = ap.rearrange(pat, **{f"d{i}": s for i, s in enumerate(shape)})
            return ap, wstrips(off, n)

        WdV, WdK = wview(0, [DC, D])
        WgV, WgK = wview(8192, [DC, D])
        WkV, WkK = wview(16384, [DC, DM])
        WcV, WcK = wview(0, [24, D])
        WmV, WmK = wview(24576, [2, D])
        bcV = wpool[0:1, 26624:27648]; bcK = wstrips(26624, 1024)
        WgqV, WgqK = wview(0, [DC, 1024])
        WuqV, WuqK = wview(8192, [DC, 1024])
        WdqV, WdqK = wview(16384, [DC, 1024])
        Ws1V, Ws1K = wview(0, [DC, H])
        Ws2V, Ws2K = wview(4096, [4, D])
        bs1V = wpool[0:1, 8192:8704]; bs1K = wstrips(8192, 512)
        bs2V = wpool[0:1, 8704:9728]; bs2K = wstrips(8704, 1024)
        b2V = wpool[0:8, 9728:10752]; b2K = wstrips(9728, 1024)
        W1hV, W1hK = wview(0, [4, DC, H])
        W2hV, W2hK = wview(16384, [4, 4, D])

        membf = actT[:, :, 0:FT]
        membfK = [("actT", q) for q in range(TPF)]

        def load_w(dst, src_ap, keys, eng="gpsimd"):
            OP(eng, lambda e, d=dst, s=src_ap: e.dma_start(out=d, in_=s),
               reads=[], writes=keys, dma=True)

        def setup():
            def ld(dst, src, keys, eng="sync"):
                OP(eng, lambda e, d=dst, s=src: e.dma_start(out=d, in_=s),
                   reads=[], writes=keys, dma=True)

            ld(eye_f[:], eye_i, ["eye_f"])
            ld(n1w[:], n1w_i, ["n1w"]); ld(n2w[:], n2w_i, ["n2w"])
            ld(bd_c[:], bd_i, ["bd"]); ld(bg_c[:], bg_i, ["bg"])
            ld(cw_c[:], cw_i, ["cw"])
            ld(MT_b[:], MT.rearrange("(c p) n -> p c n", p=P), ["MT"], eng="gpsimd")
            ld(gateW_b[:], gateW.rearrange("(c p) n -> p c n", p=P), ["gateW"], eng="gpsimd")
            ld(sscol[0:1, :], ss_i, ["ssv"]); ld(mscol[0:1, :], ms_i, ["msv"])
            ld(rscol[0:1, :], rs_i, ["rsv"])

            OP("vector", lambda e: e.memset(ones_f[:], 1.0), writes=["ones_f"])
            OP("vector", lambda e: e.memset(ones_b[:], 1.0), writes=["ones_b"])
            OP("vector", lambda e: e.memset(zerosF[:], 0.0), writes=["zerosF"])
            for i, cv in enumerate(cvals):
                OP("vector", lambda e, i=i, cv=cv: e.memset(consts[:, i:i + 1], cv),
                   writes=["consts"])
            OP("vector", lambda e: e.tensor_copy(eye_b[:], eye_f[:]),
               reads=["eye_f"], writes=["eye_b"])
            for name, col in (("ssv", sscol), ("msv", mscol), ("rsv", rscol)):
                OP("tensor", lambda e, c=col: e.matmul(ps2[:, 0:1], ones_f[:], c[0:1, :],
                                                       start=True, stop=True),
                   reads=[name, "ones_f"], writes=["ps2"])
                OP("vector", lambda e, c=col: e.tensor_copy(c[:], ps2[:, 0:1]),
                   reads=["ps2"], writes=[name])
            OP("vector", lambda e: e.tensor_scalar_mul(nss[:], n1w[:], sscol[:]),
               reads=["n1w", "ssv"], writes=["nss"])
            for c in range(DC):
                for j in range(7):
                    idx = c * 7 + j
                    OP("vector", lambda e, idx=idx: e.tensor_scalar_mul(
                        diags[:, idx * P:(idx + 1) * P], eye_f[:], cw_c[:, idx:idx + 1]),
                       reads=["eye_f", "cw"], writes=[("diag", idx)])

        def stats_pass(src_key, src_ap, mu_ap, mu_key, r_ap, r_key):
            OP("vector", lambda e: e.reduce_sum(stat_t[:, 0:1], src_ap, axis=AX.X),
               reads=[src_key], writes=["stat"])
            for hh in range(2):
                OP("scalar", lambda e, hh=hh: e.activation(
                    sqh[:], src_ap[:, hh * FT:(hh + 1) * FT], AF.Square),
                   reads=[src_key], writes=["sqh"])
                OP("vector", lambda e, hh=hh: e.reduce_sum(stat_t[:, 1 + hh:2 + hh], sqh[:],
                                                           axis=AX.X),
                   reads=["sqh"], writes=["stat"])
            OP("vector", lambda e: e.tensor_add(stat_t[:, 1:2], stat_t[:, 1:2], stat_t[:, 2:3]),
               reads=["stat"], writes=["stat"])
            OP("vector", lambda e: e.tensor_scalar_mul(mu_ap, stat_t[:, 0:1], 1.0 / D),
               reads=["stat"], writes=[mu_key])
            OP("vector", lambda e: e.tensor_tensor(stat_t[:, 3:4], mu_ap, mu_ap, ALU.mult),
               reads=[mu_key], writes=["stat"])
            OP("vector", lambda e: e.tensor_scalar_mul(stat_t[:, 2:3], stat_t[:, 1:2], 1.0 / D),
               reads=["stat"], writes=["stat"])
            OP("vector", lambda e: e.tensor_tensor(stat_t[:, 4:5], stat_t[:, 2:3],
                                                   stat_t[:, 3:4], ALU.subtract),
               reads=["stat"], writes=["stat"])
            OP("scalar", lambda e: e.activation(stat_t[:, 3:4], stat_t[:, 4:5], AF.Ln, bias=1e-5),
               reads=["stat"], writes=["stat"])
            OP("scalar", lambda e: e.activation(r_ap, stat_t[:, 3:4], AF.Exp, scale=-0.5),
               reads=["stat"], writes=[r_key])

        def col_to_row(col_ap, col_key, slot=0):
            OP("tensor", lambda e: e.matmul(prow[0:1, :], col_ap, eye_f[:], start=True, stop=True),
               reads=[col_key, "eye_f"], writes=["prow"])
            OP("scalar", lambda e, slot=slot: e.copy(rowb[:, slot * P:(slot + 1) * P],
                                                     prow[0:1, :]),
               reads=["prow"], writes=[("rowb", slot)])

        def diag_transpose(src_ap, src_key, scol_ap, scol_key, with_row,
                           pt=None, kb="pbig", slot=0):
            pt = pbig if pt is None else pt
            dw = diagws[:, 0]
            dwk = "diagws"
            OP("vector", lambda e, dw=dw: e.tensor_scalar_mul(dw, eye_f[:], scol_ap),
               reads=["eye_f", scol_key], writes=[dwk])
            for c in range(DC):
                OP("tensor", lambda e, c=c, pt=pt, dw=dw: e.matmul(
                    pt[:, c * P:(c + 1) * P], src_ap[:, c * P:(c + 1) * P], dw,
                    start=True, stop=not with_row),
                   reads=[src_key, dwk], writes=[(kb, c // 4)])
                if with_row:
                    OP("tensor", lambda e, c=c, pt=pt, slot=slot: e.matmul(
                        pt[:, c * P:(c + 1) * P], ones_f[:], rowb[:, slot * P:(slot + 1) * P],
                        start=False, stop=True),
                       reads=["ones_f", ("rowb", slot)], writes=[(kb, c // 4)])

        def phase_a(seq):
            OP("vector", lambda e: e.memset(carA[:], 0.0), writes=["carA"])
            OP("vector", lambda e: e.memset(carC[:], 0.0), writes=["carC"])
            for fi in range(LIMIT_NF or NF):
                t0 = fi * FT
                for tt in range(TPF):
                    g = fi * TPF + tt
                    xv = xws[:, g % 2]
                    xk = ("xws", g % 2)
                    OP("sync", lambda e, g=g, xv=xv: e.dma_start(
                        out=xv, in_=x_in[seq, g * P:(g + 1) * P, :]),
                       reads=[], writes=[xk], dma=True)
                    if "stats" not in A_PARTS:
                        continue
                    stats_pass(xk, xv, MUc[seq][:, g:g + 1], ("MUc", seq, g),
                               Rc[seq][:, g:g + 1], ("Rc", seq, g))
                    OP("vector", lambda e, g=g: e.tensor_scalar(
                        stat_t[:, 6:7], stat_t[:, 4:5], Rc[seq][:, g:g + 1],
                        Rc[seq][:, g:g + 1], ALU.mult, ALU.mult),
                       reads=["stat", ("Rc", seq, g)], writes=["stat"])
                    OP("scalar", lambda e: e.activation(stat_t[:, 6:7], stat_t[:, 6:7],
                                                        AF.Ln, bias=1e-6),
                       reads=["stat"], writes=["stat"])
                    OP("scalar", lambda e: e.activation(stat_t[:, 7:8], stat_t[:, 6:7],
                                                        AF.Exp, scale=-0.5),
                       reads=["stat"], writes=["stat"])
                    OP("vector", lambda e, g=g: e.tensor_tensor(
                        Sc[seq][:, g:g + 1], Rc[seq][:, g:g + 1], stat_t[:, 7:8], ALU.mult),
                       reads=["stat", ("Rc", seq, g)], writes=[("Sc", seq, g)])
                    OP("vector", lambda e, g=g: e.scalar_tensor_tensor(
                        NMSc[seq][:, g:g + 1], MUc[seq][:, g:g + 1], -1.0,
                        Sc[seq][:, g:g + 1], ALU.mult, ALU.mult),
                       reads=[("MUc", seq, g), ("Sc", seq, g)], writes=[("NMSc", seq, g)])
                    if "c2r" not in A_PARTS:
                        continue
                    col_to_row(NMSc[seq][:, g:g + 1], ("NMSc", seq, g), slot=g % 2)
                    if "diag" not in A_PARTS:
                        continue
                    diag_transpose(xv, xk, Sc[seq][:, g:g + 1], ("Sc", seq, g), True,
                                   slot=g % 2)
                    if "evac" not in A_PARTS:
                        continue
                    for hh in range(2):
                        OP("scalar", lambda e, tt=tt, hh=hh: e.copy(
                            g32a[:, hh * 4:(hh + 1) * 4, tt * P:(tt + 1) * P],
                            pbig[:, hh * FT:(hh + 1) * FT].rearrange("p (c q) -> p c q", q=P)),
                           reads=[("pbig", hh)], writes=["g32a"])
                    for c in range(DC):
                        OP("vector", lambda e, c=c, g=g: e.tensor_scalar_mul(
                            actT[:, c, g * P:(g + 1) * P], pbig[:, c * P:(c + 1) * P],
                            n1w[:, c:c + 1]),
                           reads=[("pbig", c // 4), "n1w"], writes=[("actT", g)])
                hbT_keys = [("actT", fi * TPF + q) for q in range(TPF)]
                hbT_prev = [("actT", fi * TPF - 1)] if fi > 0 else []

                for koc in range(2 if "kmem" in A_PARTS else 0):
                    for kc in range(DC):
                        OP("tensor", lambda e, koc=koc, kc=kc, fi=fi: e.matmul(
                            ps2[:], WkV[:, kc, koc * P:(koc + 1) * P],
                            actT[:, kc, fi * FT:(fi + 1) * FT],
                            start=(kc == 0), stop=(kc == DC - 1)),
                           reads=WkK + hbT_keys, writes=["ps2"])
                    OP("vector", lambda e, koc=koc: e.tensor_copy(kbf[:, koc], ps2[:]),
                       reads=["ps2"], writes=["cma1"])
                for ioc in range(2 if "kmem" in A_PARTS else 0):
                    for jc in range(2):
                        OP("tensor", lambda e, ioc=ioc, jc=jc: e.matmul(
                            ps2[:], MT_b[:, jc, ioc * P:(ioc + 1) * P], kbf[:, jc],
                            start=(jc == 0), stop=(jc == 1)),
                           reads=["MT", "cma1"], writes=["ps2"])
                    OP("vector", lambda e, ioc=ioc: e.tensor_scalar_mul(
                        cmb0[:, ioc], ps2[:], mscol[:]),
                       reads=["ps2", "msv"], writes=["cmb0"])
                if "kmem" in A_PARTS:
                    OP("gpsimd", lambda e, t0=t0: e.dma_start(
                        out=scr_mk[seq, :, :, t0:t0 + FT], in_=cmb0[:]),
                       reads=["cmb0"], writes=[("scr_mk", seq, t0)], dma=True)

                for c in range(DC if "conv" in A_PARTS else 0):
                    pb, pk = (ps0, "ps0") if c % 2 == 0 else (ps1, "ps1")
                    for s in range(7):
                        idx = c * 7 + (6 - s)
                        if fi == 0 and s > 0:
                            OP("tensor", lambda e, c=c, s=s, idx=idx, pb=pb: e.matmul(
                                pb[:, s:FT], diags[:, idx * P:(idx + 1) * P],
                                actT[:, c, 0:FT - s], start=False, stop=(s == 6)),
                               reads=[("diag", idx)] + hbT_keys, writes=[pk])
                        else:
                            OP("tensor", lambda e, c=c, s=s, idx=idx, pb=pb, fi=fi: e.matmul(
                                pb[:], diags[:, idx * P:(idx + 1) * P],
                                actT[:, c, fi * FT - s:(fi + 1) * FT - s],
                                start=(s == 0), stop=(s == 6)),
                               reads=[("diag", idx)] + hbT_keys + hbT_prev, writes=[pk])
                    OP("scalar", lambda e, c=c, pb=pb: e.copy(cma[0][:, c, :], pb[:]),
                       reads=[pk], writes=["cma0"])
                if "conv" in A_PARTS:
                    OP("gpsimd", lambda e, t0=t0: e.dma_start(
                        out=scr_loc[seq, :, :, t0:t0 + FT], in_=cma[0][:]),
                       reads=["cma0"], writes=[("scr_loc", seq, t0)], dma=True)

                for c in range(DC if "decay" in A_PARTS else 0):
                    pzd, pzdk = (ps0, "ps0") if c % 2 == 0 else (ps2, "ps2")
                    if c % 2 == 0:
                        bldc, bAcb, bcdc, becb, bucb, bCcb = (
                            ldc[:], Acb[:], cdc[:], ecb[:], ucb[:], Ccb[:])
                        kldc = kAcb = "ldc_a"
                        kcdc = kecb = "cde_a"
                        kucb = kCcb = "uc_a"
                    else:
                        bldc, bAcb = mixws[:, 0, 0:FT], mixws[:, 0, FT:2 * FT]
                        bcdc, becb = mixws[:, 1, 0:FT], mixws[:, 1, FT:2 * FT]
                        bucb, bCcb = accws[:, 0, 0:FT], accws[:, 0, FT:2 * FT]
                        kldc = kAcb = ("mixws", 0)
                        kcdc = kecb = ("mixws", 1)
                        kucb = kCcb = ("accws", 0)
                    for kc in range(DC):
                        OP("tensor", lambda e, c=c, kc=kc, fi=fi, pzd=pzd: e.matmul(
                            pzd[:], WdV[:, kc, c * P:(c + 1) * P],
                            actT[:, kc, fi * FT:(fi + 1) * FT],
                            start=(kc == 0), stop=(kc == DC - 1)),
                           reads=WdK + hbT_keys, writes=[pzdk])
                    pzg, pzgk = (ps1[:], "ps1") if c % 2 == 0 else \
                        (pbig2[:, 0:FT], ("pbig2", 0))
                    OP("scalar", lambda e, c=c, pzd=pzd, bldc=bldc: e.activation(
                        bldc, pzd[:], AF.Sigmoid, bias=bd_c[:, c:c + 1]),
                       reads=[pzdk, "bd"], writes=[kldc])
                    for kc in range(DC):
                        OP("tensor", lambda e, c=c, kc=kc, fi=fi, pzg=pzg: e.matmul(
                            pzg, WgV[:, kc, c * P:(c + 1) * P],
                            actT[:, kc, fi * FT:(fi + 1) * FT],
                            start=(kc == 0), stop=(kc == DC - 1)),
                           reads=WgK + hbT_keys, writes=[pzgk])
                    OP("scalar", lambda e, c=c, pzg=pzg, bucb=bucb: e.activation(
                        bucb, pzg, AF.Sigmoid, bias=bg_c[:, c:c + 1]),
                       reads=[pzgk, "bg"], writes=[kucb])
                    OP("vector", lambda e, bldc=bldc: e.tensor_scalar(
                        bldc, bldc, 0.95, 0.05, ALU.min, ALU.max),
                       reads=[kldc], writes=[kldc])
                    OP("scalar", lambda e, bldc=bldc: e.activation(bldc, bldc, AF.Ln),
                       reads=[kldc], writes=[kldc])
                    OP("vector", lambda e, c=c, bldc=bldc, bAcb=bAcb: e.tensor_tensor_scan(
                        bAcb, bldc, zerosF[:], carA[:, c:c + 1], ALU.add, ALU.add),
                       reads=[kldc, "zerosF", "carA"], writes=[kAcb])
                    OP("vector", lambda e, c=c, bldc=bldc: e.tensor_scalar(
                        bldc[:, 0:1], carA[:, c:c + 1], -1.0, LNC, ALU.mult, ALU.min),
                       reads=["carA"], writes=[kldc])
                    OP("vector", lambda e, bldc=bldc, bAcb=bAcb: e.tensor_scalar(
                        bldc[:, 1:FT], bAcb[:, 0:FT - 1], -1.0, LNC, ALU.mult, ALU.min),
                       reads=[kAcb], writes=[kldc])
                    OP("vector", lambda e, c=c, bAcb=bAcb: e.tensor_copy(
                        carA[:, c:c + 1], bAcb[:, FT - 1:FT]),
                       reads=[kAcb], writes=["carA"])
                    OP("scalar", lambda e, bcdc=bcdc, bAcb=bAcb: e.activation(bcdc, bAcb, AF.Exp),
                       reads=[kAcb], writes=[kcdc])
                    OP("scalar", lambda e, becb=becb, bldc=bldc: e.activation(becb, bldc, AF.Exp),
                       reads=[kldc], writes=[kecb])
                    OP("vector", lambda e, c=c, bucb=bucb: e.tensor_mul(
                        bucb, bucb, g32a[:, c, :]),
                       reads=[kucb, "g32a"], writes=[kucb])
                    OP("vector", lambda e, bldc=bldc, bucb=bucb, becb=becb: e.tensor_mul(
                        bldc, bucb, becb),
                       reads=[kucb, kecb], writes=[kldc])
                    OP("vector", lambda e, c=c, bCcb=bCcb, bldc=bldc: e.tensor_tensor_scan(
                        bCcb, bldc, zerosF[:], carC[:, c:c + 1], ALU.add, ALU.add),
                       reads=[kldc, "zerosF", "carC"], writes=[kCcb])
                    OP("vector", lambda e, c=c, bCcb=bCcb: e.tensor_copy(
                        carC[:, c:c + 1], bCcb[:, FT - 1:FT]),
                       reads=[kCcb], writes=["carC"])
                    OP("vector", lambda e, bucb=bucb, bCcb=bCcb, bcdc=bcdc: e.tensor_mul(
                        bucb, bCcb, bcdc),
                       reads=[kCcb, kcdc], writes=[kucb])
                    OP("vector", lambda e, c=c, bucb=bucb: e.tensor_scalar_mul(
                        cma[1][:, c, :], bucb, nss[:, c:c + 1]),
                       reads=[kucb, "nss"], writes=["cma1"])
                if "decay" in A_PARTS:
                    OP("gpsimd", lambda e, t0=t0: e.dma_start(
                        out=scr_st[seq, :, :, t0:t0 + FT], in_=cma[1][:]),
                       reads=["cma1"], writes=[("scr_st", seq, t0)], dma=True)
            if DEBUG_SCRATCH and seq == 0:
                OP("gpsimd", lambda e: e.dma_start(out=scr_dbg, in_=actT[:]),
                   reads=[("actT", q) for q in range(NTILE)], writes=[], dma=True)
                OP("gpsimd", lambda e: e.dma_start(out=scr_dbg2, in_=g32a[:]),
                   reads=["g32a"], writes=[], dma=True)
                for i3, (tn, tk) in enumerate([(MUc[0], "MUc"), (Rc[0], "Rc"),
                                               (Sc[0], "Sc"), (NMSc[0], "NMSc")]):
                    OP("gpsimd", lambda e, i3=i3, tn=tn: e.dma_start(
                        out=scr_dbg3[:, i3, :], in_=tn[:]),
                       reads=[(tk, 0, q) for q in range(NTILE)], writes=[], dma=True)

        def phase_b(seq):
            for fi in range(LIMIT_NF or NF):
                t0 = fi * FT
                OP("sync", lambda e, t0=t0: e.dma_start(
                    out=cma[0][:], in_=scr_loc[seq, :, :, t0:t0 + FT]),
                   reads=[("scr_loc", seq, t0)], writes=["cma0"], dma=True)
                OP("sync", lambda e, t0=t0: e.dma_start(
                    out=cma[1][:], in_=scr_st[seq, :, :, t0:t0 + FT]),
                   reads=[("scr_st", seq, t0)], writes=["cma1"], dma=True)
                OP("sync", lambda e, t0=t0: e.dma_start(
                    out=cmb0[:], in_=scr_mk[seq, :, :, t0:t0 + FT]),
                   reads=[("scr_mk", seq, t0)], writes=["cmb0"], dma=True)
                for moc in range(DC):
                    pb, pk = (ps0, "ps0") if moc % 2 == 0 else (ps1, "ps1")
                    for jc in range(2):
                        OP("tensor", lambda e, moc=moc, jc=jc, pb=pb: e.matmul(
                            pb[:], WmV[:, jc, moc * P:(moc + 1) * P], cmb0[:, jc],
                            start=(jc == 0), stop=(jc == 1)),
                           reads=WmK + ["cmb0"], writes=[pk])
                    OP("vector", lambda e, moc=moc, pb=pb: e.tensor_copy(membf[:, moc], pb[:]),
                       reads=[pk], writes=membfK)
                for tt in range(TPF):
                    g = fi * TPF + tt
                    xv, xk = xws[:, g % 2], ("xws", g % 2)
                    pt, kb = (pbig, "pbig") if g % 2 == 0 else (pbig2, "pbig2")
                    mv, mk2 = mixws[:, g % 2], ("mixws", g % 2)
                    OP("sync", lambda e, g=g, xv=xv: e.dma_start(
                        out=xv, in_=x_in[seq, g * P:(g + 1) * P, :]),
                       writes=[xk], dma=True)
                    for half in range(2):
                        OP("tensor", lambda e, half=half, pt=pt: e.matmul(
                            pt[:, half * FT:(half + 1) * FT], ones_b[:],
                            bcV[:, half * FT:(half + 1) * FT], start=True, stop=False),
                           reads=["ones_b"] + bcK, writes=[(kb, half)])
                    for kc in range(24):
                        if kc < 8:
                            opnd, rk = cma[0][:, kc, tt * P:(tt + 1) * P], ["cma0"]
                        elif kc < 16:
                            opnd, rk = cma[1][:, kc - 8, tt * P:(tt + 1) * P], ["cma1"]
                        else:
                            opnd, rk = membf[:, kc - 16, tt * P:(tt + 1) * P], membfK
                        for half in range(2):
                            OP("tensor", lambda e, kc=kc, half=half, opnd=opnd, pt=pt: e.matmul(
                                pt[:, half * FT:(half + 1) * FT], opnd,
                                WcV[:, kc, half * FT:(half + 1) * FT],
                                start=False, stop=(kc == 23)),
                               reads=rk + WcK, writes=[(kb, half)])
                    OP("vector", lambda e, g=g, xv=xv: e.tensor_scalar(
                        fows[:], xv, MUc[seq][:, g:g + 1], Rc[seq][:, g:g + 1],
                        ALU.subtract, ALU.mult),
                       reads=[xk, ("MUc", seq, g), ("Rc", seq, g)], writes=["fows"])
                    for hh in range(2):
                        OP("vector", lambda e, hh=hh, pt=pt, mv=mv: e.tensor_add(
                            mv[:, hh * FT:(hh + 1) * FT], pt[:, hh * FT:(hh + 1) * FT],
                            fows[:, hh * FT:(hh + 1) * FT]),
                           reads=[(kb, hh), "fows"], writes=[mk2])
                    for hh in range(2):
                        OP("scalar", lambda e, hh=hh, mv=mv: e.activation(
                            sqh[:], mv[:, hh * FT:(hh + 1) * FT], AF.Square),
                           reads=[mk2], writes=["sqh"])
                        OP("vector", lambda e, hh=hh: e.reduce_sum(
                            stat_t[:, 1 + hh:2 + hh], sqh[:], axis=AX.X),
                           reads=["sqh"], writes=["stat"])
                    OP("vector", lambda e: e.tensor_add(stat_t[:, 1:2], stat_t[:, 1:2],
                                                        stat_t[:, 2:3]),
                       reads=["stat"], writes=["stat"])
                    OP("vector", lambda e: e.tensor_scalar_mul(stat_t[:, 4:5], stat_t[:, 1:2],
                                                               1.0 / D),
                       reads=["stat"], writes=["stat"])
                    OP("scalar", lambda e: e.activation(stat_t[:, 3:4], stat_t[:, 4:5],
                                                        AF.Ln, bias=1e-6),
                       reads=["stat"], writes=["stat"])
                    OP("scalar", lambda e, g=g: e.activation(
                        R2c[seq][:, g:g + 1], stat_t[:, 3:4], AF.Exp, scale=-0.5),
                       reads=["stat"], writes=[("R2c", seq, g)])
                    OP("gpsimd", lambda e, g=g, mv=mv: e.dma_start(
                        out=scr_mix[seq, g * P:(g + 1) * P, :], in_=mv),
                       reads=[mk2], writes=[("scr_mix", seq, g)], dma=True)

        def phase_c(seq):
            for g in range(NTILE):
                mv, mk2 = mixws[:, g % 2], ("mixws", g % 2)
                pt, kb = (pbig, "pbig") if g % 2 == 0 else (pbig2, "pbig2")
                OP("sync", lambda e, g=g, mv=mv: e.dma_start(
                    out=mv, in_=scr_mix[seq, g * P:(g + 1) * P, :]),
                   reads=[("scr_mix", seq, g)], writes=[mk2], dma=True)
                diag_transpose(mv, mk2, R2c[seq][:, g:g + 1], ("R2c", seq, g), False,
                               pt=pt, kb=kb, slot=g % 2)
                for c in range(DC):
                    OP("vector", lambda e, c=c, g=g, pt=pt: e.tensor_scalar_mul(
                        actT[:, c, g * P:(g + 1) * P], pt[:, c * P:(c + 1) * P],
                        n2w[:, c:c + 1]),
                       reads=[(kb, c // 4), "n2w"], writes=[("actT", g)])

            for q in range(4):
                hc0 = q * 1024
                load_w(WgqV, Wg[:, hc0:hc0 + 1024].rearrange("(c p) n -> p c n", p=P), WgqK)
                load_w(WuqV, Wup[:, hc0:hc0 + 1024].rearrange("(c p) n -> p c n", p=P), WuqK)
                load_w(WdqV, Wdown[hc0:hc0 + 1024, :].rearrange("(c p) n -> p c n", p=P), WdqK)
                for fi in range(NF):
                    t0 = fi * FT
                    hbF = [("actT", fi * TPF + qq) for qq in range(TPF)]
                    for fc in range(8):
                        if fc % 2 == 0:
                            pg, pgk = ps0[:], "ps0"
                            pu, puk = ps1[:], "ps1"
                        else:
                            pg, pgk = pbig[:, 0:FT], ("pbig", 0)
                            pu, puk = pbig[:, FT:2 * FT], ("pbig", 1)
                        slu = sluws[:, 0]
                        sluk = "sluws"
                        for kc in range(DC):
                            OP("tensor", lambda e, fc=fc, kc=kc, t0=t0, pg=pg: e.matmul(
                                pg, WgqV[:, kc, fc * P:(fc + 1) * P],
                                actT[:, kc, t0:t0 + FT],
                                start=(kc == 0), stop=(kc == DC - 1)),
                               reads=WgqK + hbF, writes=[pgk])
                        for kc in range(DC):
                            OP("tensor", lambda e, fc=fc, kc=kc, t0=t0, pu=pu: e.matmul(
                                pu, WuqV[:, kc, fc * P:(fc + 1) * P],
                                actT[:, kc, t0:t0 + FT],
                                start=(kc == 0), stop=(kc == DC - 1)),
                               reads=WuqK + hbF, writes=[puk])
                        OP("scalar", lambda e, pg=pg, slu=slu: e.activation(slu, pg, AF.Silu),
                           reads=[pgk], writes=[sluk])
                        OP("vector", lambda e, fc=fc, pu=pu, slu=slu: e.tensor_mul(
                            cma[0][:, fc, :], slu, pu),
                           reads=[sluk, puk], writes=["cma0"])
                    for tt in range(TPF):
                        g = fi * TPF + tt
                        av, avk = accws[:, g % 2], ("accws", g % 2)
                        mv, mk2 = mixws[:, g % 2], ("mixws", g % 2)
                        for fc in range(8):
                            for hh in range(2):
                                OP("tensor", lambda e, fc=fc, hh=hh, tt=tt: e.matmul(
                                    pbig2[:, hh * FT:(hh + 1) * FT],
                                    cma[0][:, fc, tt * P:(tt + 1) * P],
                                    WdqV[:, fc, hh * FT:(hh + 1) * FT],
                                    start=(fc == 0), stop=(fc == 7)),
                                   reads=["cma0"] + WdqK, writes=[("pbig2", hh)])
                        if q < 3:
                            if q == 0:
                                for hh in range(2):
                                    OP("scalar", lambda e, hh=hh, av=av: e.copy(
                                        av[:, hh * FT:(hh + 1) * FT],
                                        pbig2[:, hh * FT:(hh + 1) * FT]),
                                       reads=[("pbig2", hh)], writes=[avk])
                            else:
                                OP("sync", lambda e, g=g, av=av: e.dma_start(
                                    out=av, in_=scr_ffa[seq, g * P:(g + 1) * P, :]),
                                   reads=[("scr_ffa", seq, g)], writes=[avk], dma=True)
                                for hh in range(2):
                                    OP("vector", lambda e, hh=hh, av=av: e.tensor_add(
                                        av[:, hh * FT:(hh + 1) * FT],
                                        av[:, hh * FT:(hh + 1) * FT],
                                        pbig2[:, hh * FT:(hh + 1) * FT]),
                                       reads=[("pbig2", hh), avk], writes=[avk])
                            OP("gpsimd", lambda e, g=g, av=av: e.dma_start(
                                out=scr_ffa[seq, g * P:(g + 1) * P, :], in_=av),
                               reads=[avk], writes=[("scr_ffa", seq, g)], dma=True)
                        else:
                            OP("sync", lambda e, g=g, av=av: e.dma_start(
                                out=av, in_=scr_ffa[seq, g * P:(g + 1) * P, :]),
                               reads=[("scr_ffa", seq, g)], writes=[avk], dma=True)
                            OP("sync", lambda e, g=g, mv=mv: e.dma_start(
                                out=mv, in_=scr_mix[seq, g * P:(g + 1) * P, :]),
                               reads=[("scr_mix", seq, g)], writes=[mk2], dma=True)
                            OP("sync", lambda e, g=g: e.dma_start(
                                out=x1ws[:], in_=x_in[seq, g * P:(g + 1) * P, :]),
                               writes=["x1ws"], dma=True)
                            for hh in range(2):
                                OP("vector", lambda e, hh=hh, av=av: e.tensor_add(
                                    av[:, hh * FT:(hh + 1) * FT],
                                    av[:, hh * FT:(hh + 1) * FT],
                                    pbig2[:, hh * FT:(hh + 1) * FT]),
                                   reads=[("pbig2", hh), avk], writes=[avk])
                            OP("vector", lambda e, av=av, mv=mv: e.tensor_add(fows[:], av, mv),
                               reads=[avk, mk2], writes=["fows"])
                            stats_pass("fows", fows[:], stat_t[:, 5:6], "stat",
                                       stat_t[:, 6:7], "stat")
                            OP("vector", lambda e: e.tensor_scalar(
                                fows[:], fows[:], stat_t[:, 5:6], stat_t[:, 6:7],
                                ALU.subtract, ALU.mult),
                               reads=["fows", "stat"], writes=["fows"])
                            OP("vector", lambda e: e.tensor_add(x1ws[:], x1ws[:], fows[:]),
                               reads=["fows", "x1ws"], writes=["x1ws"])
                            stats_pass("x1ws", x1ws[:], MU4c[seq][:, g:g + 1],
                                       ("MU4c", seq, g), R4c[seq][:, g:g + 1], ("R4c", seq, g))
                            OP("gpsimd", lambda e, g=g: e.dma_start(
                                out=scr_x1[seq, g * P:(g + 1) * P, :], in_=x1ws[:]),
                               reads=["x1ws"], writes=[("scr_x1", seq, g)], dma=True)

        def phase_d(seq):
            for g in range(NTILE):
                pt, kb = (pbig, "pbig") if g % 2 == 0 else (pbig2, "pbig2")
                OP("sync", lambda e, g=g: e.dma_start(
                    out=x1ws[:], in_=scr_x1[seq, g * P:(g + 1) * P, :]),
                   reads=[("scr_x1", seq, g)], writes=["x1ws"], dma=True)
                OP("vector", lambda e, g=g: e.scalar_tensor_tensor(
                    stat_t[:, 7:8], MU4c[seq][:, g:g + 1], -1.0, R4c[seq][:, g:g + 1],
                    ALU.mult, ALU.mult),
                   reads=[("MU4c", seq, g), ("R4c", seq, g)], writes=["stat"])
                col_to_row(stat_t[:, 7:8], "stat", slot=g % 2)
                diag_transpose(x1ws[:], "x1ws", R4c[seq][:, g:g + 1], ("R4c", seq, g), True,
                               pt=pt, kb=kb, slot=g % 2)
                for hh in range(2):
                    OP("scalar", lambda e, g=g, hh=hh, pt=pt: e.copy(
                        actT[:, hh * 4:(hh + 1) * 4, g * P:(g + 1) * P],
                        pt[:, hh * FT:(hh + 1) * FT].rearrange("p (c q) -> p c q", q=P)),
                       reads=[(kb, hh)], writes=[("actT", g)])
                for kc in range(DC):
                    OP("tensor", lambda e, kc=kc, g=g: e.matmul(
                        ps2[:, 0:E], actT[:, kc, g * P:(g + 1) * P], gateW_b[:, kc, :],
                        start=(kc == 0), stop=(kc == DC - 1)),
                       reads=["gateW", ("actT", g)], writes=["ps2"])
                OP("scalar", lambda e: e.activation(scb[:], ps2[:, 0:E], AF.Sigmoid),
                   reads=["ps2"], writes=["scb"])
                OP("vector", lambda e: e.tensor_scalar_mul(scb[:], scb[:], rscol[:]),
                   reads=["scb", "rsv"], writes=["scb"])
                bc_v1 = bass.broadcast_tensor_aps(scb[:], stat_t[:, 0:1])
                bc_v2 = bass.broadcast_tensor_aps(scb[:], stat_t[:, 1:2])
                OP("vector", lambda e: e.reduce_max(stat_t[:, 0:1], scb[:], axis=AX.X),
                   reads=["scb"], writes=["stat"])
                OP("vector", lambda e, bc=bc_v1: e.tensor_tensor(scb2[:], bc[0], bc[1], ALU.is_ge),
                   reads=["scb", "stat"], writes=["scb2"])
                OP("vector", lambda e: e.scalar_tensor_tensor(
                    scb2[:], scb2[:], -1e30, scb[:], ALU.mult, ALU.add),
                   reads=["scb", "scb2"], writes=["scb2"])
                OP("vector", lambda e: e.reduce_max(stat_t[:, 1:2], scb2[:], axis=AX.X),
                   reads=["scb2"], writes=["stat"])
                OP("vector", lambda e, bc=bc_v2: e.tensor_tensor(scb2[:], bc[0], bc[1], ALU.is_ge),
                   reads=["scb", "stat"], writes=["scb2"])
                OP("vector", lambda e: e.tensor_mul(scb2[:], scb2[:], scb[:]),
                   reads=["scb", "scb2"], writes=["scb2"])
                OP("vector", lambda e: e.reduce_sum(stat_t[:, 2:3], scb2[:], axis=AX.X),
                   reads=["scb2"], writes=["stat"])
                OP("vector", lambda e: e.reciprocal(stat_t[:, 3:4], stat_t[:, 2:3]),
                   reads=["stat"], writes=["stat"])
                OP("vector", lambda e, g=g: e.tensor_scalar_mul(
                    GF[:, g, :], scb2[:], stat_t[:, 3:4]),
                   reads=["scb2", "stat"], writes=[("GF", g)])
                OP("tensor", lambda e, g=g: e.matmul(
                    prow[:], GF[:, g, :], eye_f[:], start=True, stop=True),
                   reads=[("GF", g), "eye_f"], writes=["prow"])
                OP("vector", lambda e, g=g: e.tensor_copy(GCM[:, g * P:(g + 1) * P], prow[:]),
                   reads=["prow"], writes=[("GCM", g)])

            load_w(Ws1V, Ws1.rearrange("(c p) n -> p c n", p=P), Ws1K)
            load_w(Ws2V, Ws2.rearrange("(c p) n -> p c n", p=P), Ws2K)
            load_w(bs1V, bs1_i, bs1K)
            load_w(bs2V, bs2_i, bs2K)
            load_w(b2V, b2_i, b2K)
            for g in range(NTILE):
                gc0 = g * P
                pt, kb = (pbig2, "pbig2") if g % 2 == 0 else (pbig, "pbig")
                psh, pshk = (ps0, "ps0") if g % 2 == 0 else (ps2, "ps2")
                gsv, gsk = gscb[:, g % 2], ("gscb", g % 2)
                av, avk = accws[:, g % 2], ("accws", g % 2)
                for hh in range(2):
                    OP("tensor", lambda e, hh=hh, pt=pt: e.matmul(
                        pt[:, hh * FT:(hh + 1) * FT], ones_b[:],
                        bs2V[:, hh * FT:(hh + 1) * FT], start=True, stop=False),
                       reads=["ones_b"] + bs2K, writes=[(kb, hh)])
                    OP("tensor", lambda e, hh=hh, g=g, gc0=gc0, pt=pt: e.matmul(
                        pt[:, hh * FT:(hh + 1) * FT], GCM[:, gc0:gc0 + P],
                        b2V[:, hh * FT:(hh + 1) * FT], start=False, stop=False),
                       reads=[("GCM", g)] + b2K, writes=[(kb, hh)])
                OP("tensor", lambda e, psh=psh: e.matmul(
                    psh[:], ones_b[:], bs1V[:], start=True, stop=False),
                   reads=["ones_b"] + bs1K, writes=[pshk])
                for kc in range(DC):
                    OP("tensor", lambda e, kc=kc, g=g, gc0=gc0, psh=psh: e.matmul(
                        psh[:], actT[:, kc, gc0:gc0 + P], Ws1V[:, kc, :],
                        start=False, stop=(kc == DC - 1)),
                       reads=Ws1K + [("actT", g)], writes=[pshk])
                OP("scalar", lambda e, psh=psh, gsv=gsv: e.activation(gsv, psh[:], AF.Gelu),
                   reads=[pshk], writes=[gsk])
                for hcc in range(4):
                    OP("tensor", lambda e, hcc=hcc, gsv=gsv, g=g: e.matmul(
                        ps1[:, 0:P], gsv[:, hcc * P:(hcc + 1) * P], eye_b[:],
                        start=True, stop=True),
                       reads=[gsk, "eye_b"], writes=["ps1"])
                    OP("vector", lambda e, hcc=hcc, g=g: e.tensor_copy(
                        gcmw[:, 0, hcc], ps1[:, 0:P]),
                       reads=["ps1"], writes=[("gcmw", hcc)])
                for hcc in range(4):
                    for hh in range(2):
                        OP("tensor", lambda e, hcc=hcc, hh=hh, g=g, pt=pt: e.matmul(
                            pt[:, hh * FT:(hh + 1) * FT], gcmw[:, 0, hcc],
                            Ws2V[:, hcc, hh * FT:(hh + 1) * FT],
                            start=False, stop=(hcc == 3)),
                           reads=[("gcmw", hcc)] + Ws2K, writes=[(kb, hh)])
                for hh in range(2):
                    OP("scalar", lambda e, hh=hh, pt=pt, av=av: e.copy(
                        av[:, hh * FT:(hh + 1) * FT], pt[:, hh * FT:(hh + 1) * FT]),
                       reads=[(kb, hh)], writes=[avk])
                OP("gpsimd", lambda e, g=g, gc0=gc0, av=av: e.dma_start(
                    out=scr_fna[seq, gc0:gc0 + P, :], in_=av),
                   reads=[avk], writes=[("scr_fna", seq, g)], dma=True)

            for dh in range(2):
                for ei in range(4):
                    ex = dh * 4 + ei
                    load_w(W1hV[:, ei], W1[ex].rearrange("(c p) h -> p c h", p=P),
                           wstrips(ei * 4096, 4096))
                    load_w(W2hV[:, ei], W2[ex].rearrange("(c p) n -> p c n", p=P),
                           wstrips(16384 + ei * 4096, 4096))
                for g in range(NTILE):
                    gc0 = g * P
                    pt, kb = (pbig2, "pbig2") if g % 2 == 0 else (pbig, "pbig")
                    av, avk = accws[:, g % 2], ("accws", g % 2)
                    for ei in range(4):
                        ex = dh * 4 + ei
                        pe1, pe1k = (ps0, "ps0") if ei % 2 == 0 else (ps2, "ps2")
                        slu, sluk = sluws[:, 0], "sluws"
                        gsv, gsk = gscb[:, ei % 2], ("gscb", ei % 2)
                        OP("tensor", lambda e, ex=ex, pe1=pe1: e.matmul(
                            pe1[:], ones_b[:], b1_r[:, ex, :], start=True, stop=False),
                           reads=["ones_b", "b1"], writes=[pe1k])
                        for kc in range(DC):
                            OP("tensor", lambda e, kc=kc, ei=ei, g=g, gc0=gc0,
                               pe1=pe1: e.matmul(
                                pe1[:], actT[:, kc, gc0:gc0 + P], W1hV[:, ei, kc, :],
                                start=False, stop=(kc == DC - 1)),
                               reads=wstrips(ei * 4096, 4096) + [("actT", g)], writes=[pe1k])
                        OP("scalar", lambda e, pe1=pe1, slu=slu: e.activation(
                            slu, pe1[:], AF.Gelu),
                           reads=[pe1k], writes=[sluk])
                        OP("vector", lambda e, ex=ex, g=g, slu=slu, gsv=gsv: e.tensor_scalar_mul(
                            gsv, slu, GF[:, g, ex:ex + 1]),
                           reads=[sluk, ("GF", g)], writes=[gsk])
                        for hcc in range(4):
                            OP("tensor", lambda e, hcc=hcc, gsv=gsv: e.matmul(
                                ps1[:, 0:P], gsv[:, hcc * P:(hcc + 1) * P], eye_b[:],
                                start=True, stop=True),
                               reads=[gsk, "eye_b"], writes=["ps1"])
                            OP("vector", lambda e, hcc=hcc, ei=ei: e.tensor_copy(
                                gcmw[:, 0, hcc], ps1[:, 0:P]),
                               reads=["ps1"], writes=[("gcmw", hcc)])
                        for hcc in range(4):
                            for hh in range(2):
                                st_f = (ei == 0 and hcc == 0)
                                OP("tensor", lambda e, hcc=hcc, hh=hh, ei=ei, pt=pt,
                                   st=st_f: e.matmul(
                                    pt[:, hh * FT:(hh + 1) * FT], gcmw[:, 0, hcc],
                                    W2hV[:, ei, hcc, hh * FT:(hh + 1) * FT],
                                    start=st, stop=(ei == 3 and hcc == 3)),
                                   reads=[("gcmw", hcc)]
                                       + wstrips(16384 + ei * 4096, 4096),
                                   writes=[(kb, hh)])
                    OP("sync", lambda e, g=g, gc0=gc0, av=av: e.dma_start(
                        out=av, in_=scr_fna[seq, gc0:gc0 + P, :]),
                       reads=[("scr_fna", seq, g)], writes=[avk], dma=True)
                    for hh in range(2):
                        OP("vector", lambda e, hh=hh, pt=pt, av=av: e.tensor_add(
                            av[:, hh * FT:(hh + 1) * FT], av[:, hh * FT:(hh + 1) * FT],
                            pt[:, hh * FT:(hh + 1) * FT]),
                           reads=[(kb, hh), avk], writes=[avk])
                    if dh == 0:
                        OP("gpsimd", lambda e, g=g, gc0=gc0, av=av: e.dma_start(
                            out=scr_fna[seq, gc0:gc0 + P, :], in_=av),
                           reads=[avk], writes=[("scr_fna", seq, g)], dma=True)
                    else:
                        OP("sync", lambda e, g=g, gc0=gc0: e.dma_start(
                            out=x1ws[:], in_=scr_x1[seq, gc0:gc0 + P, :]),
                           reads=[("scr_x1", seq, g)], writes=["x1ws"], dma=True)
                        stats_pass(avk, av, stat_t[:, 5:6], "stat",
                                   stat_t[:, 6:7], "stat")
                        OP("vector", lambda e, av=av: e.tensor_scalar(
                            av, av, stat_t[:, 5:6], stat_t[:, 6:7],
                            ALU.subtract, ALU.mult),
                           reads=[avk, "stat"], writes=[avk])
                        OP("vector", lambda e, av=av: e.tensor_add(av, av, x1ws[:]),
                           reads=["x1ws", avk], writes=[avk])
                        OP("gpsimd", lambda e, g=g, gc0=gc0, av=av: e.dma_start(
                            out=y_out[seq, gc0:gc0 + P, :], in_=av),
                           reads=[avk], writes=[], dma=True)

        setup()
        if "A" in PHASES:
            load_w(WdV, Wdecay.rearrange("(c p) n -> p c n", p=P), WdK)
            load_w(WgV, Wgate.rearrange("(c p) n -> p c n", p=P), WgK)
            load_w(WkV, Wk.rearrange("(c p) n -> p c n", p=P), WkK)
            for seq in range(LIMIT_NSEQ or NSEQ):
                phase_a(seq)
        if "B" in PHASES:
            load_w(WcV, Wc.rearrange("(c p) n -> p c n", p=P), WcK)
            load_w(WmV, Wmem.rearrange("(c p) n -> p c n", p=P), WmK)
            load_w(bcV, bc_i, bcK)
            for seq in range(LIMIT_NSEQ or NSEQ):
                phase_b(seq)
        if "C" in PHASES:
            for seq in range(LIMIT_NSEQ or NSEQ):
                phase_c(seq)
        if "D" in PHASES:
            OP("gpsimd", lambda e: e.dma_start(out=b1_r, in_=b1_i),
               reads=[], writes=b1K, dma=True)
            for seq in range(LIMIT_NSEQ or NSEQ):
                phase_d(seq)

        prog.finalize()

        with nc.Block() as block:
            @block.sync
            def _(eng):
                prog.emit("sync", eng)

            @block.gpsimd
            def _(eng):
                prog.emit("gpsimd", eng)
                prog.fence(eng)

            @block.tensor
            def _(eng):
                prog.emit("tensor", eng)

            @block.vector
            def _(eng):
                prog.emit("vector", eng)

            @block.scalar
            def _(eng):
                prog.emit("scalar", eng)

    return nc


_nc_cache = None


def kernel(**inputs):
    global _nc_cache
    f32 = np.float32

    def cont(a):
        return np.ascontiguousarray(a, dtype=f32)

    x = cont(inputs["L0_attn_x"])
    conv_w = cont(inputs["L0_attn_conv_w"])
    aux = {
        "Wdecay": cont(inputs["L0_attn_Wdecay"]),
        "Wgate": cont(inputs["L0_attn_Wgate"]),
        "Wk": cont(inputs["L0_attn_Wk"]),
        "MT": cont(inputs["L0_attn_M"].T),
        "Wmem": cont(inputs["L0_attn_Wmem"]),
        "Wc": cont(inputs["L0_attn_Wc"]),
        "Wg": cont(inputs["L0_attn_Wg"]),
        "Wup": cont(inputs["L0_attn_Wup"]),
        "Wdown": cont(inputs["L0_attn_Wdown"]),
        "W1": cont(inputs["L0_ffn_W1"]),
        "W2": cont(inputs["L0_ffn_W2"]),
        "Ws1": cont(inputs["L0_ffn_Ws1"]),
        "Ws2": cont(inputs["L0_ffn_Ws2"]),
        "gateW": cont(inputs["L0_ffn_gate_W"]),
        "n1w_c": cont(inputs["L0_attn_norm1_w"].reshape(DC, P).T),
        "n2w_c": cont(inputs["L0_attn_norm2_w"].reshape(DC, P).T),
        "bd_c": cont(inputs["L0_attn_bdecay"].reshape(DC, P).T),
        "bg_c": cont(inputs["L0_attn_bgate"].reshape(DC, P).T),
        "cw_cols": cont(conv_w.reshape(DC, P, 7).transpose(1, 0, 2).reshape(P, DC * 7)),
        "bc_row": cont(inputs["L0_attn_bc"].reshape(1, D)),
        "bs1_row": cont(inputs["L0_ffn_bs1"].reshape(1, H)),
        "bs2_row": cont(inputs["L0_ffn_bs2"].reshape(1, D)),
        "b1_rows": cont(inputs["L0_ffn_b1"].reshape(1, E, H)),
        "b2_pm": cont(inputs["L0_ffn_b2"]),
        "ss": cont(np.reshape(inputs["L0_attn_state_scale"], (1, 1))),
        "ms": cont(np.reshape(inputs["L0_attn_mem_scale"], (1, 1))),
        "rs": cont(np.reshape(inputs["L0_ffn_route_scale"], (1, 1))),
    }
    if _nc_cache is None:
        _nc_cache = build()
    nc = _nc_cache
    in_maps = []
    for c in range(8):
        m = dict(aux)
        m["x"] = x[c * NSEQ:(c + 1) * NSEQ]
        in_maps.append(m)
    res = run_bass_kernel_spmd(nc, in_maps, core_ids=list(range(8)))
    return np.concatenate([r["y"] for r in res.results], axis=0)


# version 08
# speedup vs baseline: 1.0654x; 1.0608x over previous
import numpy as np
from contextlib import ExitStack

import concourse.bass as bass
import concourse.mybir as mybir
from concourse.bass_utils import run_bass_kernel_spmd

F32 = mybir.dt.float32
BF16 = mybir.dt.bfloat16
AF = mybir.ActivationFunctionType
ALU = mybir.AluOpType
AX = mybir.AxisListType

P = 128
T = 2048
NSEQ = 2
D = 1024
DC = D // P
FT = 512
NF = T // FT
NTILE = T // P
TPF = FT // P
DM = 256
DF = 4096
E = 8
H = 512
LNC = 16.11809565095832

PHASES = ("A", "B", "C", "D")
DEBUG_SCRATCH = False
LIMIT_NF = None
LIMIT_NSEQ = None
A_PARTS = ("tiles", "stats", "c2r", "diag", "evac", "kmem", "conv", "decay")


class Rec:
    __slots__ = ("eng", "fn", "reads", "writes", "dma", "flag", "semval", "idx", "deps",
                 "slot", "use")

    def __init__(self, eng, fn, reads, writes, dma):
        self.eng, self.fn = eng, fn
        self.reads, self.writes = list(reads), list(writes)
        self.dma = dma
        self.flag = dma
        self.semval = None
        self.deps = []
        self.slot = self.use = 0


NSLOT = 8


class Prog:
    ENGINES = ("sync", "gpsimd", "tensor", "vector", "scalar")

    def __init__(self, nc, es):
        self.nc = nc
        self.ops = []
        self.last_w = {}
        self.readers = {}
        self.c_sem = {e: es.enter_context(nc.semaphore(f"c_{e}")) for e in self.ENGINES}
        self.d_sem = {e: [es.enter_context(nc.semaphore(f"d_{e}{i}")) for i in range(NSLOT)]
                      for e in ("sync", "gpsimd", "scalar")}
        self.dma_count = {e: 0 for e in ("sync", "gpsimd", "scalar")}

    PSUMNAMES = {"pbig", "pbig2", "ps0", "ps1", "ps2", "prow"}

    def op(self, eng, fn, reads=(), writes=(), dma=False):
        o = Rec(eng, fn, reads, writes, dma)
        o.idx = len(self.ops)
        if dma:
            k = self.dma_count[eng]
            o.slot, o.use = k % NSLOT, k // NSLOT
            o.semval = 16 * (o.use + 1)
            self.dma_count[eng] = k + 1
        deps = {}
        for k in o.reads:
            d = self.last_w.get(k)
            if d is not None:
                deps[d.idx] = d
        for k in o.writes:
            for d in self.readers.get(k, ()):
                deps[d.idx] = d
            d = self.last_w.get(k)
            if d is not None:
                deps[d.idx] = d
        if not dma and eng in ("scalar", "vector"):
            if not hasattr(self, "_last_psum_by_key"):
                self._last_psum_by_key = {}
            for k in list(o.reads) + list(o.writes):
                if (k if isinstance(k, str) else k[0]) in self.PSUMNAMES:
                    prev = self._last_psum_by_key.get(k)
                    if prev is not None and prev.eng != eng:
                        deps[prev.idx] = prev
                    self._last_psum_by_key[k] = o
        o.deps = list(deps.values())
        for k in o.writes:
            self.last_w[k] = o
            self.readers[k] = []
        for k in o.reads:
            self.readers.setdefault(k, []).append(o)
        self.ops.append(o)
        return o

    def finalize(self):
        for o in self.ops:
            for d in o.deps:
                if d.eng == o.eng == "tensor" and not d.dma:
                    continue
                d.flag = True
        cnt = {}
        for o in self.ops:
            if o.flag and not o.dma:
                cnt[o.eng] = cnt.get(o.eng, 0) + 1
                o.semval = cnt[o.eng]

    def emit(self, eng_name, eng):
        watermark = {}

        def wait(sem, v):
            if watermark.get(id(sem), -1) < v:
                eng.wait_ge(sem, v)
                watermark[id(sem)] = v

        for o in self.ops:
            if o.eng != eng_name:
                continue
            waits = {}
            for d in o.deps:
                if d.eng == o.eng and not d.dma and (o.eng == "tensor" or not d.flag):
                    continue
                sem = self.d_sem[d.eng][d.slot] if d.dma else self.c_sem[d.eng]
                cur = waits.get(id(sem))
                if cur is None or d.semval > cur[1]:
                    waits[id(sem)] = (sem, d.semval)
            for sem, v in waits.values():
                wait(sem, v)
            if o.dma and o.use > 0:
                wait(self.d_sem[o.eng][o.slot], 16 * o.use)
            inst = o.fn(eng)
            if o.flag:
                if o.dma:
                    inst.then_inc(self.d_sem[o.eng][o.slot], 16)
                else:
                    inst.then_inc(self.c_sem[o.eng], 1)

    def fence(self, eng):
        for e2 in ("sync", "gpsimd", "scalar"):
            n = self.dma_count[e2]
            for i in range(NSLOT):
                uses = (n - i + NSLOT - 1) // NSLOT if n > i else 0
                if uses > 0:
                    eng.wait_ge(self.d_sem[e2][i], 16 * uses)


def build():
    nc = bass.Bass("TRN2", target_bir_lowering=False, debug=False)

    def din(name, shape, dt=F32):
        return nc.dram_tensor(name, list(shape), dt, kind="ExternalInput").ap()

    x_in = din("x", [NSEQ, T, D])
    y_out = nc.dram_tensor("y", [NSEQ, T, D], F32, kind="ExternalOutput").ap()

    Wdecay = din("Wdecay", [D, D]); Wgate = din("Wgate", [D, D])
    Wk = din("Wk", [D, DM]); MT = din("MT", [DM, DM]); Wmem = din("Wmem", [DM, D])
    Wc = din("Wc", [3 * D, D])
    Wg = din("Wg", [D, DF]); Wup = din("Wup", [D, DF]); Wdown = din("Wdown", [DF, D])
    W1 = din("W1", [E, D, H]); W2 = din("W2", [E, H, D])
    Ws1 = din("Ws1", [D, H]); Ws2 = din("Ws2", [H, D])
    gateW = din("gateW", [D, E])
    n1w_i = din("n1w_c", [P, DC]); n2w_i = din("n2w_c", [P, DC])
    bd_i = din("bd_c", [P, DC]); bg_i = din("bg_c", [P, DC])
    cw_i = din("cw_cols", [P, DC * 7])
    bc_i = din("bc_row", [1, D]); bs1_i = din("bs1_row", [1, H]); bs2_i = din("bs2_row", [1, D])
    b1_i = din("b1_rows", [1, E, H]); b2_i = din("b2_pm", [E, D])
    ss_i = din("ss", [1, 1]); ms_i = din("ms", [1, 1]); rs_i = din("rs", [1, 1])

    skind = "ExternalOutput" if DEBUG_SCRATCH else "Internal"
    scr_loc = nc.dram_tensor("scr_loc", [NSEQ, P, DC, T], BF16, kind=skind).ap()
    scr_st = nc.dram_tensor("scr_st", [NSEQ, P, DC, T], BF16, kind=skind).ap()
    scr_mk = nc.dram_tensor("scr_mk", [NSEQ, P, 2, T], BF16, kind=skind).ap()
    scr_mix = nc.dram_tensor("scr_mix", [NSEQ, T, D], F32, kind=skind).ap()
    scr_x1 = nc.dram_tensor("scr_x1", [NSEQ, T, D], F32, kind=skind).ap()
    scr_ffa = nc.dram_tensor("scr_ffa", [NSEQ, T, D], F32, kind=skind).ap()
    scr_fna = nc.dram_tensor("scr_fna", [NSEQ, T, D], F32, kind=skind).ap()
    scr_dbg = nc.dram_tensor("scr_dbg", [P, DC, T], BF16, kind=skind).ap()
    scr_dbg2 = nc.dram_tensor("scr_dbg2", [P, DC, FT], F32, kind=skind).ap()
    scr_dbg3 = nc.dram_tensor("scr_dbg3", [P, 4, NTILE], F32, kind=skind).ap()

    eye_i = nc.inline_tensor(np.eye(P, dtype=np.float32), "eye_i").ap()

    es = ExitStack()
    with es:
        def sb(name, shape, dt=F32):
            return es.enter_context(nc.sbuf_tensor(name, list(shape), dt))

        def psum(name, shape):
            return es.enter_context(nc.psum_tensor(name, list(shape), F32))

        eye_f = sb("eye_f", [P, P]); eye_b = sb("eye_b", [P, P], BF16)
        ones_f = sb("ones_f", [1, P]); ones_b = sb("ones_b", [1, P], BF16)
        zerosF = sb("zerosF", [P, FT])
        consts = sb("consts", [P, 8])
        n1w = sb("n1w", [P, DC]); n2w = sb("n2w", [P, DC])
        bd_c = sb("bdcol", [P, DC]); bg_c = sb("bgcol", [P, DC])
        cw_c = sb("cwcol", [P, DC * 7])
        diags = sb("diags", [P, DC * 7 * P], BF16)
        b1_r = diags[0:1, 0:E * H].rearrange("o (e2 h2) -> o e2 h2", e2=E)
        b1K = ["b1"] + [("diag", i) for i in range(32)]
        MT_b = sb("MT_b", [P, 2, DM], BF16)
        gateW_b = sb("gateW_b", [P, DC, E], BF16)
        sscol = sb("sscol", [P, 1]); mscol = sb("mscol", [P, 1]); rscol = sb("rscol", [P, 1])
        nss = sb("nss", [P, DC])
        carA = sb("carA", [P, DC]); carC = sb("carC", [P, DC])

        MUc = [sb(f"MUc{s}", [P, NTILE]) for s in range(NSEQ)]
        Rc = [sb(f"Rc{s}", [P, NTILE]) for s in range(NSEQ)]
        Sc = [sb(f"Sc{s}", [P, NTILE]) for s in range(NSEQ)]
        NMSc = [sb(f"NMSc{s}", [P, NTILE]) for s in range(NSEQ)]
        R2c = [sb(f"R2c{s}", [P, NTILE]) for s in range(NSEQ)]
        MU4c = [sb(f"MU4c{s}", [P, NTILE]) for s in range(NSEQ)]
        R4c = [sb(f"R4c{s}", [P, NTILE]) for s in range(NSEQ)]
        GF = sb("GFt", [P, NTILE, E])
        GCM = sb("GCMt", [E, T], BF16)
        stat_t = sb("stat_t", [P, 8])
        rowb = sb("rowb", [1, 2 * P], BF16)

        actT = sb("actT", [P, DC, T], BF16)
        wpool = sb("wpool", [P, 32768], BF16)

        g32a = sb("g32a", [P, DC, FT])
        ldc = sb("ldc", [P, FT]); Acb = sb("Acb", [P, FT]); cdc = sb("cdc", [P, FT])
        ecb = sb("ecb", [P, FT]); ucb = sb("ucb", [P, FT]); Ccb = sb("Ccb", [P, FT])
        sluws = sb("sluws", [P, 1, FT])
        cma = [sb(f"cma{i}", [P, DC, FT], BF16) for i in range(2)]
        kbf = cma[1]
        cmb0 = sb("cmb0", [P, 2, FT], BF16)
        xws = sb("xws", [P, 2, D])
        sqh = sb("sqh", [P, FT])
        diagws = sb("diagws", [P, 1, P], BF16)
        mixws = sb("mixws", [P, 2, D])
        accws = sb("accws", [P, 2, D])
        fows = sb("fows", [P, D])
        x1ws = sb("x1ws", [P, D])
        gscb = sb("gscb", [P, 2, H], BF16)
        gcmw = sb("gcmw", [P, 1, 4, P], BF16)
        scb = sb("scb", [P, E])
        scb2 = sb("scb2", [P, E])

        pbig = psum("pbig", [P, 1024])
        pbig2 = psum("pbig2", [P, 1024])
        ps0 = psum("ps0", [P, FT]); ps1 = psum("ps1", [P, FT]); ps2 = psum("ps2", [P, FT])
        prow = psum("prow", [E, P])

        cvals = [1e-5, 1e-6, -0.5, -1.0]
        for i, cv in enumerate(cvals):
            nc.const_aps.register(F32, cv, consts[:, i:i + 1])

        prog = Prog(nc, es)
        OP = prog.op

        def wstrips(off, n):
            return [f"wp{j}" for j in range(off // 4096, (off + n - 1) // 4096 + 1)]

        def wview(off, shape):
            n = int(np.prod(shape))
            ap = wpool[:, off:off + n]
            if len(shape) > 1:
                pat = "p (" + " ".join(f"d{i}" for i in range(len(shape))) + ") -> p " + \
                      " ".join(f"d{i}" for i in range(len(shape)))
                ap = ap.rearrange(pat, **{f"d{i}": s for i, s in enumerate(shape)})
            return ap, wstrips(off, n)

        WdV, WdK = wview(0, [DC, D])
        WgV, WgK = wview(8192, [DC, D])
        WkV, WkK = wview(16384, [DC, DM])
        WcV, WcK = wview(0, [24, D])
        WmV, WmK = wview(24576, [2, D])
        bcV = wpool[0:1, 26624:27648]; bcK = wstrips(26624, 1024)
        WgqV, WgqK = wview(0, [DC, 1024])
        WuqV, WuqK = wview(8192, [DC, 1024])
        WdqV, WdqK = wview(16384, [DC, 1024])
        Ws1V, Ws1K = wview(0, [DC, H])
        Ws2V, Ws2K = wview(4096, [4, D])
        bs1V = wpool[0:1, 8192:8704]; bs1K = wstrips(8192, 512)
        bs2V = wpool[0:1, 8704:9728]; bs2K = wstrips(8704, 1024)
        b2V = wpool[0:8, 9728:10752]; b2K = wstrips(9728, 1024)
        W1hV, W1hK = wview(0, [4, DC, H])
        W2hV, W2hK = wview(16384, [4, 4, D])

        membf = actT[:, :, 0:FT]
        membfK = [("actT", q) for q in range(TPF)]

        def load_w(dst, src_ap, keys, eng="gpsimd"):
            OP(eng, lambda e, d=dst, s=src_ap: e.dma_start(out=d, in_=s),
               reads=[], writes=keys, dma=True)

        def setup():
            def ld(dst, src, keys, eng="sync"):
                OP(eng, lambda e, d=dst, s=src: e.dma_start(out=d, in_=s),
                   reads=[], writes=keys, dma=True)

            ld(eye_f[:], eye_i, ["eye_f"])
            ld(n1w[:], n1w_i, ["n1w"]); ld(n2w[:], n2w_i, ["n2w"])
            ld(bd_c[:], bd_i, ["bd"]); ld(bg_c[:], bg_i, ["bg"])
            ld(cw_c[:], cw_i, ["cw"])
            ld(MT_b[:], MT.rearrange("(c p) n -> p c n", p=P), ["MT"], eng="gpsimd")
            ld(gateW_b[:], gateW.rearrange("(c p) n -> p c n", p=P), ["gateW"], eng="gpsimd")
            ld(sscol[0:1, :], ss_i, ["ssv"]); ld(mscol[0:1, :], ms_i, ["msv"])
            ld(rscol[0:1, :], rs_i, ["rsv"])

            OP("vector", lambda e: e.memset(ones_f[:], 1.0), writes=["ones_f"])
            OP("vector", lambda e: e.memset(ones_b[:], 1.0), writes=["ones_b"])
            OP("vector", lambda e: e.memset(zerosF[:], 0.0), writes=["zerosF"])
            for i, cv in enumerate(cvals):
                OP("vector", lambda e, i=i, cv=cv: e.memset(consts[:, i:i + 1], cv),
                   writes=["consts"])
            OP("vector", lambda e: e.tensor_copy(eye_b[:], eye_f[:]),
               reads=["eye_f"], writes=["eye_b"])
            for name, col in (("ssv", sscol), ("msv", mscol), ("rsv", rscol)):
                OP("tensor", lambda e, c=col: e.matmul(ps2[:, 0:1], ones_f[:], c[0:1, :],
                                                       start=True, stop=True),
                   reads=[name, "ones_f"], writes=["ps2"])
                OP("vector", lambda e, c=col: e.tensor_copy(c[:], ps2[:, 0:1]),
                   reads=["ps2"], writes=[name])
            OP("vector", lambda e: e.tensor_scalar_mul(nss[:], n1w[:], sscol[:]),
               reads=["n1w", "ssv"], writes=["nss"])
            for c in range(DC):
                for j in range(7):
                    idx = c * 7 + j
                    OP("vector", lambda e, idx=idx: e.tensor_scalar_mul(
                        diags[:, idx * P:(idx + 1) * P], eye_f[:], cw_c[:, idx:idx + 1]),
                       reads=["eye_f", "cw"], writes=[("diag", idx)])

        def stats_pass(src_key, src_ap, mu_ap, mu_key, r_ap, r_key):
            OP("vector", lambda e: e.reduce_sum(stat_t[:, 0:1], src_ap, axis=AX.X),
               reads=[src_key], writes=["stat"])
            for hh in range(2):
                OP("scalar", lambda e, hh=hh: e.activation(
                    sqh[:], src_ap[:, hh * FT:(hh + 1) * FT], AF.Square),
                   reads=[src_key], writes=["sqh"])
                OP("vector", lambda e, hh=hh: e.reduce_sum(stat_t[:, 1 + hh:2 + hh], sqh[:],
                                                           axis=AX.X),
                   reads=["sqh"], writes=["stat"])
            OP("vector", lambda e: e.tensor_add(stat_t[:, 1:2], stat_t[:, 1:2], stat_t[:, 2:3]),
               reads=["stat"], writes=["stat"])
            OP("vector", lambda e: e.tensor_scalar_mul(mu_ap, stat_t[:, 0:1], 1.0 / D),
               reads=["stat"], writes=[mu_key])
            OP("vector", lambda e: e.tensor_tensor(stat_t[:, 3:4], mu_ap, mu_ap, ALU.mult),
               reads=[mu_key], writes=["stat"])
            OP("vector", lambda e: e.tensor_scalar_mul(stat_t[:, 2:3], stat_t[:, 1:2], 1.0 / D),
               reads=["stat"], writes=["stat"])
            OP("vector", lambda e: e.tensor_tensor(stat_t[:, 4:5], stat_t[:, 2:3],
                                                   stat_t[:, 3:4], ALU.subtract),
               reads=["stat"], writes=["stat"])
            OP("scalar", lambda e: e.activation(stat_t[:, 3:4], stat_t[:, 4:5], AF.Ln, bias=1e-5),
               reads=["stat"], writes=["stat"])
            OP("scalar", lambda e: e.activation(r_ap, stat_t[:, 3:4], AF.Exp, scale=-0.5),
               reads=["stat"], writes=[r_key])

        def col_to_row(col_ap, col_key, slot=0):
            OP("tensor", lambda e: e.matmul(prow[0:1, :], col_ap, eye_f[:], start=True, stop=True),
               reads=[col_key, "eye_f"], writes=["prow"])
            OP("scalar", lambda e, slot=slot: e.copy(rowb[:, slot * P:(slot + 1) * P],
                                                     prow[0:1, :]),
               reads=["prow"], writes=[("rowb", slot)])

        def diag_transpose(src_ap, src_key, scol_ap, scol_key, with_row,
                           pt=None, kb="pbig", slot=0):
            pt = pbig if pt is None else pt
            dw = diagws[:, 0]
            dwk = "diagws"
            OP("vector", lambda e, dw=dw: e.tensor_scalar_mul(dw, eye_b[:], scol_ap),
               reads=["eye_b", scol_key], writes=[dwk])
            for c in range(DC):
                OP("tensor", lambda e, c=c, pt=pt, dw=dw: e.matmul(
                    pt[:, c * P:(c + 1) * P], src_ap[:, c * P:(c + 1) * P], dw,
                    start=True, stop=not with_row),
                   reads=[src_key, dwk], writes=[(kb, c // 4)])
                if with_row:
                    OP("tensor", lambda e, c=c, pt=pt, slot=slot: e.matmul(
                        pt[:, c * P:(c + 1) * P], ones_b[:], rowb[:, slot * P:(slot + 1) * P],
                        start=False, stop=True),
                       reads=["ones_b", ("rowb", slot)], writes=[(kb, c // 4)])

        def phase_a(seq):
            OP("vector", lambda e: e.memset(carA[:], 0.0), writes=["carA"])
            OP("vector", lambda e: e.memset(carC[:], 0.0), writes=["carC"])
            for fi in range(LIMIT_NF or NF):
                t0 = fi * FT
                for tt in range(TPF):
                    g = fi * TPF + tt
                    xv = xws[:, g % 2]
                    xk = ("xws", g % 2)
                    xbv = cma[0][:, 2 * (g % 2):2 * (g % 2) + 2, :].rearrange(
                        "p a b -> p (a b)")
                    OP("sync", lambda e, g=g, xv=xv: e.dma_start(
                        out=xv, in_=x_in[seq, g * P:(g + 1) * P, :]),
                       reads=[], writes=[xk], dma=True)
                    OP("gpsimd", lambda e, g=g, xbv=xbv: e.dma_start(
                        out=xbv, in_=x_in[seq, g * P:(g + 1) * P, :]),
                       reads=[], writes=["cma0"], dma=True)
                    if "stats" not in A_PARTS:
                        continue
                    stats_pass(xk, xv, MUc[seq][:, g:g + 1], ("MUc", seq, g),
                               Rc[seq][:, g:g + 1], ("Rc", seq, g))
                    OP("vector", lambda e, g=g: e.tensor_scalar(
                        stat_t[:, 6:7], stat_t[:, 4:5], Rc[seq][:, g:g + 1],
                        Rc[seq][:, g:g + 1], ALU.mult, ALU.mult),
                       reads=["stat", ("Rc", seq, g)], writes=["stat"])
                    OP("scalar", lambda e: e.activation(stat_t[:, 6:7], stat_t[:, 6:7],
                                                        AF.Ln, bias=1e-6),
                       reads=["stat"], writes=["stat"])
                    OP("scalar", lambda e: e.activation(stat_t[:, 7:8], stat_t[:, 6:7],
                                                        AF.Exp, scale=-0.5),
                       reads=["stat"], writes=["stat"])
                    OP("vector", lambda e, g=g: e.tensor_tensor(
                        Sc[seq][:, g:g + 1], Rc[seq][:, g:g + 1], stat_t[:, 7:8], ALU.mult),
                       reads=["stat", ("Rc", seq, g)], writes=[("Sc", seq, g)])
                    OP("vector", lambda e, g=g: e.scalar_tensor_tensor(
                        NMSc[seq][:, g:g + 1], MUc[seq][:, g:g + 1], -1.0,
                        Sc[seq][:, g:g + 1], ALU.mult, ALU.mult),
                       reads=[("MUc", seq, g), ("Sc", seq, g)], writes=[("NMSc", seq, g)])
                    if "c2r" not in A_PARTS:
                        continue
                    col_to_row(NMSc[seq][:, g:g + 1], ("NMSc", seq, g), slot=g % 2)
                    if "diag" not in A_PARTS:
                        continue
                    diag_transpose(xbv, "cma0", Sc[seq][:, g:g + 1], ("Sc", seq, g), True,
                                   slot=g % 2)
                    if "evac" not in A_PARTS:
                        continue
                    for hh in range(2):
                        OP("scalar", lambda e, tt=tt, hh=hh: e.copy(
                            g32a[:, hh * 4:(hh + 1) * 4, tt * P:(tt + 1) * P],
                            pbig[:, hh * FT:(hh + 1) * FT].rearrange("p (c q) -> p c q", q=P)),
                           reads=[("pbig", hh)], writes=["g32a"])
                    for c in range(DC):
                        OP("vector", lambda e, c=c, g=g: e.tensor_scalar_mul(
                            actT[:, c, g * P:(g + 1) * P], pbig[:, c * P:(c + 1) * P],
                            n1w[:, c:c + 1]),
                           reads=[("pbig", c // 4), "n1w"], writes=[("actT", g)])
                hbT_keys = [("actT", fi * TPF + q) for q in range(TPF)]
                hbT_prev = [("actT", fi * TPF - 1)] if fi > 0 else []

                for koc in range(2 if "kmem" in A_PARTS else 0):
                    for kc in range(DC):
                        OP("tensor", lambda e, koc=koc, kc=kc, fi=fi: e.matmul(
                            ps2[:], WkV[:, kc, koc * P:(koc + 1) * P],
                            actT[:, kc, fi * FT:(fi + 1) * FT],
                            start=(kc == 0), stop=(kc == DC - 1)),
                           reads=WkK + hbT_keys, writes=["ps2"])
                    OP("vector", lambda e, koc=koc: e.tensor_copy(kbf[:, koc], ps2[:]),
                       reads=["ps2"], writes=["cma1"])
                for ioc in range(2 if "kmem" in A_PARTS else 0):
                    for jc in range(2):
                        OP("tensor", lambda e, ioc=ioc, jc=jc: e.matmul(
                            ps2[:], MT_b[:, jc, ioc * P:(ioc + 1) * P], kbf[:, jc],
                            start=(jc == 0), stop=(jc == 1)),
                           reads=["MT", "cma1"], writes=["ps2"])
                    OP("vector", lambda e, ioc=ioc: e.tensor_scalar_mul(
                        cmb0[:, ioc], ps2[:], mscol[:]),
                       reads=["ps2", "msv"], writes=["cmb0"])
                if "kmem" in A_PARTS:
                    OP("gpsimd", lambda e, t0=t0: e.dma_start(
                        out=scr_mk[seq, :, :, t0:t0 + FT], in_=cmb0[:]),
                       reads=["cmb0"], writes=[("scr_mk", seq, t0)], dma=True)

                for c in range(DC if "conv" in A_PARTS else 0):
                    pb, pk = (ps0, "ps0") if c % 2 == 0 else (ps1, "ps1")
                    for s in range(7):
                        idx = c * 7 + (6 - s)
                        if fi == 0 and s > 0:
                            OP("tensor", lambda e, c=c, s=s, idx=idx, pb=pb: e.matmul(
                                pb[:, s:FT], diags[:, idx * P:(idx + 1) * P],
                                actT[:, c, 0:FT - s], start=False, stop=(s == 6)),
                               reads=[("diag", idx)] + hbT_keys, writes=[pk])
                        else:
                            OP("tensor", lambda e, c=c, s=s, idx=idx, pb=pb, fi=fi: e.matmul(
                                pb[:], diags[:, idx * P:(idx + 1) * P],
                                actT[:, c, fi * FT - s:(fi + 1) * FT - s],
                                start=(s == 0), stop=(s == 6)),
                               reads=[("diag", idx)] + hbT_keys + hbT_prev, writes=[pk])
                    OP("scalar", lambda e, c=c, pb=pb: e.copy(cma[0][:, c, :], pb[:]),
                       reads=[pk], writes=["cma0"])
                if "conv" in A_PARTS:
                    OP("gpsimd", lambda e, t0=t0: e.dma_start(
                        out=scr_loc[seq, :, :, t0:t0 + FT], in_=cma[0][:]),
                       reads=["cma0"], writes=[("scr_loc", seq, t0)], dma=True)

                for c in range(DC if "decay" in A_PARTS else 0):
                    pzd, pzdk = (ps0, "ps0") if c % 2 == 0 else (ps2, "ps2")
                    if c % 2 == 0:
                        bldc, bAcb, bcdc, becb, bucb, bCcb = (
                            ldc[:], Acb[:], cdc[:], ecb[:], ucb[:], Ccb[:])
                        kldc = kAcb = "ldc_a"
                        kcdc = kecb = "cde_a"
                        kucb = kCcb = "uc_a"
                    else:
                        bldc, bAcb = mixws[:, 0, 0:FT], mixws[:, 0, FT:2 * FT]
                        bcdc, becb = mixws[:, 1, 0:FT], mixws[:, 1, FT:2 * FT]
                        bucb, bCcb = accws[:, 0, 0:FT], accws[:, 0, FT:2 * FT]
                        kldc = kAcb = ("mixws", 0)
                        kcdc = kecb = ("mixws", 1)
                        kucb = kCcb = ("accws", 0)
                    for kc in range(DC):
                        OP("tensor", lambda e, c=c, kc=kc, fi=fi, pzd=pzd: e.matmul(
                            pzd[:], WdV[:, kc, c * P:(c + 1) * P],
                            actT[:, kc, fi * FT:(fi + 1) * FT],
                            start=(kc == 0), stop=(kc == DC - 1)),
                           reads=WdK + hbT_keys, writes=[pzdk])
                    pzg, pzgk = (ps1[:], "ps1") if c % 2 == 0 else \
                        (pbig2[:, 0:FT], ("pbig2", 0))
                    OP("scalar", lambda e, c=c, pzd=pzd, bldc=bldc: e.activation(
                        bldc, pzd[:], AF.Sigmoid, bias=bd_c[:, c:c + 1]),
                       reads=[pzdk, "bd"], writes=[kldc])
                    for kc in range(DC):
                        OP("tensor", lambda e, c=c, kc=kc, fi=fi, pzg=pzg: e.matmul(
                            pzg, WgV[:, kc, c * P:(c + 1) * P],
                            actT[:, kc, fi * FT:(fi + 1) * FT],
                            start=(kc == 0), stop=(kc == DC - 1)),
                           reads=WgK + hbT_keys, writes=[pzgk])
                    OP("scalar", lambda e, c=c, pzg=pzg, bucb=bucb: e.activation(
                        bucb, pzg, AF.Sigmoid, bias=bg_c[:, c:c + 1]),
                       reads=[pzgk, "bg"], writes=[kucb])
                    OP("vector", lambda e, bldc=bldc: e.tensor_scalar(
                        bldc, bldc, 0.95, 0.05, ALU.min, ALU.max),
                       reads=[kldc], writes=[kldc])
                    OP("scalar", lambda e, bldc=bldc: e.activation(bldc, bldc, AF.Ln),
                       reads=[kldc], writes=[kldc])
                    OP("vector", lambda e, c=c, bldc=bldc, bAcb=bAcb: e.tensor_tensor_scan(
                        bAcb, bldc, zerosF[:], carA[:, c:c + 1], ALU.add, ALU.add),
                       reads=[kldc, "zerosF", "carA"], writes=[kAcb])
                    OP("vector", lambda e, c=c, bldc=bldc: e.tensor_scalar(
                        bldc[:, 0:1], carA[:, c:c + 1], -1.0, LNC, ALU.mult, ALU.min),
                       reads=["carA"], writes=[kldc])
                    OP("vector", lambda e, bldc=bldc, bAcb=bAcb: e.tensor_scalar(
                        bldc[:, 1:FT], bAcb[:, 0:FT - 1], -1.0, LNC, ALU.mult, ALU.min),
                       reads=[kAcb], writes=[kldc])
                    OP("vector", lambda e, c=c, bAcb=bAcb: e.tensor_copy(
                        carA[:, c:c + 1], bAcb[:, FT - 1:FT]),
                       reads=[kAcb], writes=["carA"])
                    OP("scalar", lambda e, bcdc=bcdc, bAcb=bAcb: e.activation(bcdc, bAcb, AF.Exp),
                       reads=[kAcb], writes=[kcdc])
                    OP("scalar", lambda e, becb=becb, bldc=bldc: e.activation(becb, bldc, AF.Exp),
                       reads=[kldc], writes=[kecb])
                    OP("vector", lambda e, c=c, bucb=bucb: e.tensor_mul(
                        bucb, bucb, g32a[:, c, :]),
                       reads=[kucb, "g32a"], writes=[kucb])
                    OP("vector", lambda e, bldc=bldc, bucb=bucb, becb=becb: e.tensor_mul(
                        bldc, bucb, becb),
                       reads=[kucb, kecb], writes=[kldc])
                    OP("vector", lambda e, c=c, bCcb=bCcb, bldc=bldc: e.tensor_tensor_scan(
                        bCcb, bldc, zerosF[:], carC[:, c:c + 1], ALU.add, ALU.add),
                       reads=[kldc, "zerosF", "carC"], writes=[kCcb])
                    OP("vector", lambda e, c=c, bCcb=bCcb: e.tensor_copy(
                        carC[:, c:c + 1], bCcb[:, FT - 1:FT]),
                       reads=[kCcb], writes=["carC"])
                    OP("vector", lambda e, bucb=bucb, bCcb=bCcb, bcdc=bcdc: e.tensor_mul(
                        bucb, bCcb, bcdc),
                       reads=[kCcb, kcdc], writes=[kucb])
                    OP("vector", lambda e, c=c, bucb=bucb: e.tensor_scalar_mul(
                        cma[1][:, c, :], bucb, nss[:, c:c + 1]),
                       reads=[kucb, "nss"], writes=["cma1"])
                if "decay" in A_PARTS:
                    OP("gpsimd", lambda e, t0=t0: e.dma_start(
                        out=scr_st[seq, :, :, t0:t0 + FT], in_=cma[1][:]),
                       reads=["cma1"], writes=[("scr_st", seq, t0)], dma=True)
            if DEBUG_SCRATCH and seq == 0:
                OP("gpsimd", lambda e: e.dma_start(out=scr_dbg, in_=actT[:]),
                   reads=[("actT", q) for q in range(NTILE)], writes=[], dma=True)
                OP("gpsimd", lambda e: e.dma_start(out=scr_dbg2, in_=g32a[:]),
                   reads=["g32a"], writes=[], dma=True)
                for i3, (tn, tk) in enumerate([(MUc[0], "MUc"), (Rc[0], "Rc"),
                                               (Sc[0], "Sc"), (NMSc[0], "NMSc")]):
                    OP("gpsimd", lambda e, i3=i3, tn=tn: e.dma_start(
                        out=scr_dbg3[:, i3, :], in_=tn[:]),
                       reads=[(tk, 0, q) for q in range(NTILE)], writes=[], dma=True)

        def phase_b(seq):
            for fi in range(LIMIT_NF or NF):
                t0 = fi * FT
                OP("sync", lambda e, t0=t0: e.dma_start(
                    out=cma[0][:], in_=scr_loc[seq, :, :, t0:t0 + FT]),
                   reads=[("scr_loc", seq, t0)], writes=["cma0"], dma=True)
                OP("sync", lambda e, t0=t0: e.dma_start(
                    out=cma[1][:], in_=scr_st[seq, :, :, t0:t0 + FT]),
                   reads=[("scr_st", seq, t0)], writes=["cma1"], dma=True)
                OP("sync", lambda e, t0=t0: e.dma_start(
                    out=cmb0[:], in_=scr_mk[seq, :, :, t0:t0 + FT]),
                   reads=[("scr_mk", seq, t0)], writes=["cmb0"], dma=True)
                for moc in range(DC):
                    pb, pk = (ps0, "ps0") if moc % 2 == 0 else (ps1, "ps1")
                    for jc in range(2):
                        OP("tensor", lambda e, moc=moc, jc=jc, pb=pb: e.matmul(
                            pb[:], WmV[:, jc, moc * P:(moc + 1) * P], cmb0[:, jc],
                            start=(jc == 0), stop=(jc == 1)),
                           reads=WmK + ["cmb0"], writes=[pk])
                    OP("vector", lambda e, moc=moc, pb=pb: e.tensor_copy(membf[:, moc], pb[:]),
                       reads=[pk], writes=membfK)
                for tt in range(TPF):
                    g = fi * TPF + tt
                    xv, xk = xws[:, g % 2], ("xws", g % 2)
                    pt, kb = (pbig, "pbig") if g % 2 == 0 else (pbig2, "pbig2")
                    mv, mk2 = mixws[:, g % 2], ("mixws", g % 2)
                    OP("sync", lambda e, g=g, xv=xv: e.dma_start(
                        out=xv, in_=x_in[seq, g * P:(g + 1) * P, :]),
                       writes=[xk], dma=True)
                    for half in range(2):
                        OP("tensor", lambda e, half=half, pt=pt: e.matmul(
                            pt[:, half * FT:(half + 1) * FT], ones_b[:],
                            bcV[:, half * FT:(half + 1) * FT], start=True, stop=False),
                           reads=["ones_b"] + bcK, writes=[(kb, half)])
                    for kc in range(24):
                        if kc < 8:
                            opnd, rk = cma[0][:, kc, tt * P:(tt + 1) * P], ["cma0"]
                        elif kc < 16:
                            opnd, rk = cma[1][:, kc - 8, tt * P:(tt + 1) * P], ["cma1"]
                        else:
                            opnd, rk = membf[:, kc - 16, tt * P:(tt + 1) * P], membfK
                        for half in range(2):
                            OP("tensor", lambda e, kc=kc, half=half, opnd=opnd, pt=pt: e.matmul(
                                pt[:, half * FT:(half + 1) * FT], opnd,
                                WcV[:, kc, half * FT:(half + 1) * FT],
                                start=False, stop=(kc == 23)),
                               reads=rk + WcK, writes=[(kb, half)])
                    OP("vector", lambda e, g=g, xv=xv: e.tensor_scalar(
                        fows[:], xv, MUc[seq][:, g:g + 1], Rc[seq][:, g:g + 1],
                        ALU.subtract, ALU.mult),
                       reads=[xk, ("MUc", seq, g), ("Rc", seq, g)], writes=["fows"])
                    for hh in range(2):
                        OP("vector", lambda e, hh=hh, pt=pt, mv=mv: e.tensor_add(
                            mv[:, hh * FT:(hh + 1) * FT], pt[:, hh * FT:(hh + 1) * FT],
                            fows[:, hh * FT:(hh + 1) * FT]),
                           reads=[(kb, hh), "fows"], writes=[mk2])
                    for hh in range(2):
                        OP("scalar", lambda e, hh=hh, mv=mv: e.activation(
                            sqh[:], mv[:, hh * FT:(hh + 1) * FT], AF.Square),
                           reads=[mk2], writes=["sqh"])
                        OP("vector", lambda e, hh=hh: e.reduce_sum(
                            stat_t[:, 1 + hh:2 + hh], sqh[:], axis=AX.X),
                           reads=["sqh"], writes=["stat"])
                    OP("vector", lambda e: e.tensor_add(stat_t[:, 1:2], stat_t[:, 1:2],
                                                        stat_t[:, 2:3]),
                       reads=["stat"], writes=["stat"])
                    OP("vector", lambda e: e.tensor_scalar_mul(stat_t[:, 4:5], stat_t[:, 1:2],
                                                               1.0 / D),
                       reads=["stat"], writes=["stat"])
                    OP("scalar", lambda e: e.activation(stat_t[:, 3:4], stat_t[:, 4:5],
                                                        AF.Ln, bias=1e-6),
                       reads=["stat"], writes=["stat"])
                    OP("scalar", lambda e, g=g: e.activation(
                        R2c[seq][:, g:g + 1], stat_t[:, 3:4], AF.Exp, scale=-0.5),
                       reads=["stat"], writes=[("R2c", seq, g)])
                    OP("gpsimd", lambda e, g=g, mv=mv: e.dma_start(
                        out=scr_mix[seq, g * P:(g + 1) * P, :], in_=mv),
                       reads=[mk2], writes=[("scr_mix", seq, g)], dma=True)

        def phase_c(seq):
            for g in range(NTILE):
                pt, kb = (pbig, "pbig") if g % 2 == 0 else (pbig2, "pbig2")
                xbv = cma[0][:, 2 * (g % 2):2 * (g % 2) + 2, :].rearrange("p a b -> p (a b)")
                OP("gpsimd", lambda e, g=g, xbv=xbv: e.dma_start(
                    out=xbv, in_=scr_mix[seq, g * P:(g + 1) * P, :]),
                   reads=[("scr_mix", seq, g)], writes=["cma0"], dma=True)
                diag_transpose(xbv, "cma0", R2c[seq][:, g:g + 1], ("R2c", seq, g), False,
                               pt=pt, kb=kb, slot=g % 2)
                for c in range(DC):
                    OP("vector", lambda e, c=c, g=g, pt=pt: e.tensor_scalar_mul(
                        actT[:, c, g * P:(g + 1) * P], pt[:, c * P:(c + 1) * P],
                        n2w[:, c:c + 1]),
                       reads=[(kb, c // 4), "n2w"], writes=[("actT", g)])

            for q in range(4):
                hc0 = q * 1024
                load_w(WgqV, Wg[:, hc0:hc0 + 1024].rearrange("(c p) n -> p c n", p=P), WgqK)
                load_w(WuqV, Wup[:, hc0:hc0 + 1024].rearrange("(c p) n -> p c n", p=P), WuqK)
                load_w(WdqV, Wdown[hc0:hc0 + 1024, :].rearrange("(c p) n -> p c n", p=P), WdqK)
                for fi in range(NF):
                    t0 = fi * FT
                    hbF = [("actT", fi * TPF + qq) for qq in range(TPF)]
                    for fc in range(8):
                        if fc % 2 == 0:
                            pg, pgk = ps0[:], "ps0"
                            pu, puk = ps1[:], "ps1"
                        else:
                            pg, pgk = pbig[:, 0:FT], ("pbig", 0)
                            pu, puk = pbig[:, FT:2 * FT], ("pbig", 1)
                        slu = sluws[:, 0]
                        sluk = "sluws"
                        for kc in range(DC):
                            OP("tensor", lambda e, fc=fc, kc=kc, t0=t0, pg=pg: e.matmul(
                                pg, WgqV[:, kc, fc * P:(fc + 1) * P],
                                actT[:, kc, t0:t0 + FT],
                                start=(kc == 0), stop=(kc == DC - 1)),
                               reads=WgqK + hbF, writes=[pgk])
                        for kc in range(DC):
                            OP("tensor", lambda e, fc=fc, kc=kc, t0=t0, pu=pu: e.matmul(
                                pu, WuqV[:, kc, fc * P:(fc + 1) * P],
                                actT[:, kc, t0:t0 + FT],
                                start=(kc == 0), stop=(kc == DC - 1)),
                               reads=WuqK + hbF, writes=[puk])
                        OP("scalar", lambda e, pg=pg, slu=slu: e.activation(slu, pg, AF.Silu),
                           reads=[pgk], writes=[sluk])
                        OP("vector", lambda e, fc=fc, pu=pu, slu=slu: e.tensor_mul(
                            cma[0][:, fc, :], slu, pu),
                           reads=[sluk, puk], writes=["cma0"])
                    for tt in range(TPF):
                        g = fi * TPF + tt
                        av, avk = accws[:, g % 2], ("accws", g % 2)
                        mv, mk2 = mixws[:, g % 2], ("mixws", g % 2)
                        for fc in range(8):
                            for hh in range(2):
                                OP("tensor", lambda e, fc=fc, hh=hh, tt=tt: e.matmul(
                                    pbig2[:, hh * FT:(hh + 1) * FT],
                                    cma[0][:, fc, tt * P:(tt + 1) * P],
                                    WdqV[:, fc, hh * FT:(hh + 1) * FT],
                                    start=(fc == 0), stop=(fc == 7)),
                                   reads=["cma0"] + WdqK, writes=[("pbig2", hh)])
                        if q < 3:
                            if q == 0:
                                for hh in range(2):
                                    OP("scalar", lambda e, hh=hh, av=av: e.copy(
                                        av[:, hh * FT:(hh + 1) * FT],
                                        pbig2[:, hh * FT:(hh + 1) * FT]),
                                       reads=[("pbig2", hh)], writes=[avk])
                            else:
                                OP("sync", lambda e, g=g, av=av: e.dma_start(
                                    out=av, in_=scr_ffa[seq, g * P:(g + 1) * P, :]),
                                   reads=[("scr_ffa", seq, g)], writes=[avk], dma=True)
                                for hh in range(2):
                                    OP("vector", lambda e, hh=hh, av=av: e.tensor_add(
                                        av[:, hh * FT:(hh + 1) * FT],
                                        av[:, hh * FT:(hh + 1) * FT],
                                        pbig2[:, hh * FT:(hh + 1) * FT]),
                                       reads=[("pbig2", hh), avk], writes=[avk])
                            OP("gpsimd", lambda e, g=g, av=av: e.dma_start(
                                out=scr_ffa[seq, g * P:(g + 1) * P, :], in_=av),
                               reads=[avk], writes=[("scr_ffa", seq, g)], dma=True)
                        else:
                            OP("sync", lambda e, g=g, av=av: e.dma_start(
                                out=av, in_=scr_ffa[seq, g * P:(g + 1) * P, :]),
                               reads=[("scr_ffa", seq, g)], writes=[avk], dma=True)
                            OP("sync", lambda e, g=g, mv=mv: e.dma_start(
                                out=mv, in_=scr_mix[seq, g * P:(g + 1) * P, :]),
                               reads=[("scr_mix", seq, g)], writes=[mk2], dma=True)
                            OP("sync", lambda e, g=g: e.dma_start(
                                out=x1ws[:], in_=x_in[seq, g * P:(g + 1) * P, :]),
                               writes=["x1ws"], dma=True)
                            for hh in range(2):
                                OP("vector", lambda e, hh=hh, av=av: e.tensor_add(
                                    av[:, hh * FT:(hh + 1) * FT],
                                    av[:, hh * FT:(hh + 1) * FT],
                                    pbig2[:, hh * FT:(hh + 1) * FT]),
                                   reads=[("pbig2", hh), avk], writes=[avk])
                            OP("vector", lambda e, av=av, mv=mv: e.tensor_add(fows[:], av, mv),
                               reads=[avk, mk2], writes=["fows"])
                            stats_pass("fows", fows[:], stat_t[:, 5:6], "stat",
                                       stat_t[:, 6:7], "stat")
                            OP("vector", lambda e: e.tensor_scalar(
                                fows[:], fows[:], stat_t[:, 5:6], stat_t[:, 6:7],
                                ALU.subtract, ALU.mult),
                               reads=["fows", "stat"], writes=["fows"])
                            OP("vector", lambda e: e.tensor_add(x1ws[:], x1ws[:], fows[:]),
                               reads=["fows", "x1ws"], writes=["x1ws"])
                            stats_pass("x1ws", x1ws[:], MU4c[seq][:, g:g + 1],
                                       ("MU4c", seq, g), R4c[seq][:, g:g + 1], ("R4c", seq, g))
                            OP("gpsimd", lambda e, g=g: e.dma_start(
                                out=scr_x1[seq, g * P:(g + 1) * P, :], in_=x1ws[:]),
                               reads=["x1ws"], writes=[("scr_x1", seq, g)], dma=True)

        def phase_d(seq):
            for g in range(NTILE):
                pt, kb = (pbig, "pbig") if g % 2 == 0 else (pbig2, "pbig2")
                xbv = cma[0][:, 2 * (g % 2):2 * (g % 2) + 2, :].rearrange("p a b -> p (a b)")
                OP("gpsimd", lambda e, g=g, xbv=xbv: e.dma_start(
                    out=xbv, in_=scr_x1[seq, g * P:(g + 1) * P, :]),
                   reads=[("scr_x1", seq, g)], writes=["cma0"], dma=True)
                OP("vector", lambda e, g=g: e.scalar_tensor_tensor(
                    stat_t[:, 7:8], MU4c[seq][:, g:g + 1], -1.0, R4c[seq][:, g:g + 1],
                    ALU.mult, ALU.mult),
                   reads=[("MU4c", seq, g), ("R4c", seq, g)], writes=["stat"])
                col_to_row(stat_t[:, 7:8], "stat", slot=g % 2)
                diag_transpose(xbv, "cma0", R4c[seq][:, g:g + 1], ("R4c", seq, g), True,
                               pt=pt, kb=kb, slot=g % 2)
                for hh in range(2):
                    OP("scalar", lambda e, g=g, hh=hh, pt=pt: e.copy(
                        actT[:, hh * 4:(hh + 1) * 4, g * P:(g + 1) * P],
                        pt[:, hh * FT:(hh + 1) * FT].rearrange("p (c q) -> p c q", q=P)),
                       reads=[(kb, hh)], writes=[("actT", g)])
                for kc in range(DC):
                    OP("tensor", lambda e, kc=kc, g=g: e.matmul(
                        ps2[:, 0:E], actT[:, kc, g * P:(g + 1) * P], gateW_b[:, kc, :],
                        start=(kc == 0), stop=(kc == DC - 1)),
                       reads=["gateW", ("actT", g)], writes=["ps2"])
                OP("scalar", lambda e: e.activation(scb[:], ps2[:, 0:E], AF.Sigmoid),
                   reads=["ps2"], writes=["scb"])
                OP("vector", lambda e: e.tensor_scalar_mul(scb[:], scb[:], rscol[:]),
                   reads=["scb", "rsv"], writes=["scb"])
                bc_v1 = bass.broadcast_tensor_aps(scb[:], stat_t[:, 0:1])
                bc_v2 = bass.broadcast_tensor_aps(scb[:], stat_t[:, 1:2])
                OP("vector", lambda e: e.reduce_max(stat_t[:, 0:1], scb[:], axis=AX.X),
                   reads=["scb"], writes=["stat"])
                OP("vector", lambda e, bc=bc_v1: e.tensor_tensor(scb2[:], bc[0], bc[1], ALU.is_ge),
                   reads=["scb", "stat"], writes=["scb2"])
                OP("vector", lambda e: e.scalar_tensor_tensor(
                    scb2[:], scb2[:], -1e30, scb[:], ALU.mult, ALU.add),
                   reads=["scb", "scb2"], writes=["scb2"])
                OP("vector", lambda e: e.reduce_max(stat_t[:, 1:2], scb2[:], axis=AX.X),
                   reads=["scb2"], writes=["stat"])
                OP("vector", lambda e, bc=bc_v2: e.tensor_tensor(scb2[:], bc[0], bc[1], ALU.is_ge),
                   reads=["scb", "stat"], writes=["scb2"])
                OP("vector", lambda e: e.tensor_mul(scb2[:], scb2[:], scb[:]),
                   reads=["scb", "scb2"], writes=["scb2"])
                OP("vector", lambda e: e.reduce_sum(stat_t[:, 2:3], scb2[:], axis=AX.X),
                   reads=["scb2"], writes=["stat"])
                OP("vector", lambda e: e.reciprocal(stat_t[:, 3:4], stat_t[:, 2:3]),
                   reads=["stat"], writes=["stat"])
                OP("vector", lambda e, g=g: e.tensor_scalar_mul(
                    GF[:, g, :], scb2[:], stat_t[:, 3:4]),
                   reads=["scb2", "stat"], writes=[("GF", g)])
                OP("tensor", lambda e, g=g: e.matmul(
                    prow[:], GF[:, g, :], eye_f[:], start=True, stop=True),
                   reads=[("GF", g), "eye_f"], writes=["prow"])
                OP("vector", lambda e, g=g: e.tensor_copy(GCM[:, g * P:(g + 1) * P], prow[:]),
                   reads=["prow"], writes=[("GCM", g)])

            load_w(Ws1V, Ws1.rearrange("(c p) n -> p c n", p=P), Ws1K)
            load_w(Ws2V, Ws2.rearrange("(c p) n -> p c n", p=P), Ws2K)
            load_w(bs1V, bs1_i, bs1K)
            load_w(bs2V, bs2_i, bs2K)
            load_w(b2V, b2_i, b2K)
            for g in range(NTILE):
                gc0 = g * P
                pt, kb = (pbig2, "pbig2") if g % 2 == 0 else (pbig, "pbig")
                psh, pshk = (ps0, "ps0") if g % 2 == 0 else (ps2, "ps2")
                gsv, gsk = gscb[:, g % 2], ("gscb", g % 2)
                av, avk = accws[:, g % 2], ("accws", g % 2)
                for hh in range(2):
                    OP("tensor", lambda e, hh=hh, pt=pt: e.matmul(
                        pt[:, hh * FT:(hh + 1) * FT], ones_b[:],
                        bs2V[:, hh * FT:(hh + 1) * FT], start=True, stop=False),
                       reads=["ones_b"] + bs2K, writes=[(kb, hh)])
                    OP("tensor", lambda e, hh=hh, g=g, gc0=gc0, pt=pt: e.matmul(
                        pt[:, hh * FT:(hh + 1) * FT], GCM[:, gc0:gc0 + P],
                        b2V[:, hh * FT:(hh + 1) * FT], start=False, stop=False),
                       reads=[("GCM", g)] + b2K, writes=[(kb, hh)])
                OP("tensor", lambda e, psh=psh: e.matmul(
                    psh[:], ones_b[:], bs1V[:], start=True, stop=False),
                   reads=["ones_b"] + bs1K, writes=[pshk])
                for kc in range(DC):
                    OP("tensor", lambda e, kc=kc, g=g, gc0=gc0, psh=psh: e.matmul(
                        psh[:], actT[:, kc, gc0:gc0 + P], Ws1V[:, kc, :],
                        start=False, stop=(kc == DC - 1)),
                       reads=Ws1K + [("actT", g)], writes=[pshk])
                OP("scalar", lambda e, psh=psh, gsv=gsv: e.activation(gsv, psh[:], AF.Gelu),
                   reads=[pshk], writes=[gsk])
                for hcc in range(4):
                    OP("tensor", lambda e, hcc=hcc, gsv=gsv, g=g: e.matmul(
                        ps1[:, 0:P], gsv[:, hcc * P:(hcc + 1) * P], eye_b[:],
                        start=True, stop=True),
                       reads=[gsk, "eye_b"], writes=["ps1"])
                    OP("vector", lambda e, hcc=hcc, g=g: e.tensor_copy(
                        gcmw[:, 0, hcc], ps1[:, 0:P]),
                       reads=["ps1"], writes=[("gcmw", hcc)])
                for hcc in range(4):
                    for hh in range(2):
                        OP("tensor", lambda e, hcc=hcc, hh=hh, g=g, pt=pt: e.matmul(
                            pt[:, hh * FT:(hh + 1) * FT], gcmw[:, 0, hcc],
                            Ws2V[:, hcc, hh * FT:(hh + 1) * FT],
                            start=False, stop=(hcc == 3)),
                           reads=[("gcmw", hcc)] + Ws2K, writes=[(kb, hh)])
                for hh in range(2):
                    OP("scalar", lambda e, hh=hh, pt=pt, av=av: e.copy(
                        av[:, hh * FT:(hh + 1) * FT], pt[:, hh * FT:(hh + 1) * FT]),
                       reads=[(kb, hh)], writes=[avk])
                OP("gpsimd", lambda e, g=g, gc0=gc0, av=av: e.dma_start(
                    out=scr_fna[seq, gc0:gc0 + P, :], in_=av),
                   reads=[avk], writes=[("scr_fna", seq, g)], dma=True)

            for dh in range(2):
                for ei in range(4):
                    ex = dh * 4 + ei
                    load_w(W1hV[:, ei], W1[ex].rearrange("(c p) h -> p c h", p=P),
                           wstrips(ei * 4096, 4096))
                    load_w(W2hV[:, ei], W2[ex].rearrange("(c p) n -> p c n", p=P),
                           wstrips(16384 + ei * 4096, 4096))
                for g in range(NTILE):
                    gc0 = g * P
                    pt, kb = (pbig2, "pbig2") if g % 2 == 0 else (pbig, "pbig")
                    av, avk = accws[:, g % 2], ("accws", g % 2)
                    for ei in range(4):
                        ex = dh * 4 + ei
                        pe1, pe1k = (ps0, "ps0") if ei % 2 == 0 else (ps2, "ps2")
                        slu, sluk = sluws[:, 0], "sluws"
                        gsv, gsk = gscb[:, ei % 2], ("gscb", ei % 2)
                        OP("tensor", lambda e, ex=ex, pe1=pe1: e.matmul(
                            pe1[:], ones_b[:], b1_r[:, ex, :], start=True, stop=False),
                           reads=["ones_b", "b1"], writes=[pe1k])
                        for kc in range(DC):
                            OP("tensor", lambda e, kc=kc, ei=ei, g=g, gc0=gc0,
                               pe1=pe1: e.matmul(
                                pe1[:], actT[:, kc, gc0:gc0 + P], W1hV[:, ei, kc, :],
                                start=False, stop=(kc == DC - 1)),
                               reads=wstrips(ei * 4096, 4096) + [("actT", g)], writes=[pe1k])
                        OP("scalar", lambda e, pe1=pe1, slu=slu: e.activation(
                            slu, pe1[:], AF.Gelu),
                           reads=[pe1k], writes=[sluk])
                        OP("vector", lambda e, ex=ex, g=g, slu=slu, gsv=gsv: e.tensor_scalar_mul(
                            gsv, slu, GF[:, g, ex:ex + 1]),
                           reads=[sluk, ("GF", g)], writes=[gsk])
                        for hcc in range(4):
                            OP("tensor", lambda e, hcc=hcc, gsv=gsv: e.matmul(
                                ps1[:, 0:P], gsv[:, hcc * P:(hcc + 1) * P], eye_b[:],
                                start=True, stop=True),
                               reads=[gsk, "eye_b"], writes=["ps1"])
                            OP("vector", lambda e, hcc=hcc, ei=ei: e.tensor_copy(
                                gcmw[:, 0, hcc], ps1[:, 0:P]),
                               reads=["ps1"], writes=[("gcmw", hcc)])
                        for hcc in range(4):
                            for hh in range(2):
                                st_f = (ei == 0 and hcc == 0)
                                OP("tensor", lambda e, hcc=hcc, hh=hh, ei=ei, pt=pt,
                                   st=st_f: e.matmul(
                                    pt[:, hh * FT:(hh + 1) * FT], gcmw[:, 0, hcc],
                                    W2hV[:, ei, hcc, hh * FT:(hh + 1) * FT],
                                    start=st, stop=(ei == 3 and hcc == 3)),
                                   reads=[("gcmw", hcc)]
                                       + wstrips(16384 + ei * 4096, 4096),
                                   writes=[(kb, hh)])
                    OP("sync", lambda e, g=g, gc0=gc0, av=av: e.dma_start(
                        out=av, in_=scr_fna[seq, gc0:gc0 + P, :]),
                       reads=[("scr_fna", seq, g)], writes=[avk], dma=True)
                    for hh in range(2):
                        OP("vector", lambda e, hh=hh, pt=pt, av=av: e.tensor_add(
                            av[:, hh * FT:(hh + 1) * FT], av[:, hh * FT:(hh + 1) * FT],
                            pt[:, hh * FT:(hh + 1) * FT]),
                           reads=[(kb, hh), avk], writes=[avk])
                    if dh == 0:
                        OP("gpsimd", lambda e, g=g, gc0=gc0, av=av: e.dma_start(
                            out=scr_fna[seq, gc0:gc0 + P, :], in_=av),
                           reads=[avk], writes=[("scr_fna", seq, g)], dma=True)
                    else:
                        OP("sync", lambda e, g=g, gc0=gc0: e.dma_start(
                            out=x1ws[:], in_=scr_x1[seq, gc0:gc0 + P, :]),
                           reads=[("scr_x1", seq, g)], writes=["x1ws"], dma=True)
                        stats_pass(avk, av, stat_t[:, 5:6], "stat",
                                   stat_t[:, 6:7], "stat")
                        OP("vector", lambda e, av=av: e.tensor_scalar(
                            av, av, stat_t[:, 5:6], stat_t[:, 6:7],
                            ALU.subtract, ALU.mult),
                           reads=[avk, "stat"], writes=[avk])
                        OP("vector", lambda e, av=av: e.tensor_add(av, av, x1ws[:]),
                           reads=["x1ws", avk], writes=[avk])
                        OP("gpsimd", lambda e, g=g, gc0=gc0, av=av: e.dma_start(
                            out=y_out[seq, gc0:gc0 + P, :], in_=av),
                           reads=[avk], writes=[], dma=True)

        setup()
        if "A" in PHASES:
            load_w(WdV, Wdecay.rearrange("(c p) n -> p c n", p=P), WdK)
            load_w(WgV, Wgate.rearrange("(c p) n -> p c n", p=P), WgK)
            load_w(WkV, Wk.rearrange("(c p) n -> p c n", p=P), WkK)
            for seq in range(LIMIT_NSEQ or NSEQ):
                phase_a(seq)
        if "B" in PHASES:
            load_w(WcV, Wc.rearrange("(c p) n -> p c n", p=P), WcK)
            load_w(WmV, Wmem.rearrange("(c p) n -> p c n", p=P), WmK)
            load_w(bcV, bc_i, bcK)
            for seq in range(LIMIT_NSEQ or NSEQ):
                phase_b(seq)
        if "C" in PHASES:
            for seq in range(LIMIT_NSEQ or NSEQ):
                phase_c(seq)
        if "D" in PHASES:
            OP("gpsimd", lambda e: e.dma_start(out=b1_r, in_=b1_i),
               reads=[], writes=b1K, dma=True)
            for seq in range(LIMIT_NSEQ or NSEQ):
                phase_d(seq)

        prog.finalize()

        with nc.Block() as block:
            @block.sync
            def _(eng):
                prog.emit("sync", eng)

            @block.gpsimd
            def _(eng):
                prog.emit("gpsimd", eng)
                prog.fence(eng)

            @block.tensor
            def _(eng):
                prog.emit("tensor", eng)

            @block.vector
            def _(eng):
                prog.emit("vector", eng)

            @block.scalar
            def _(eng):
                prog.emit("scalar", eng)

    return nc


_nc_cache = None


def kernel(**inputs):
    global _nc_cache
    f32 = np.float32

    def cont(a):
        return np.ascontiguousarray(a, dtype=f32)

    x = cont(inputs["L0_attn_x"])
    conv_w = cont(inputs["L0_attn_conv_w"])
    aux = {
        "Wdecay": cont(inputs["L0_attn_Wdecay"]),
        "Wgate": cont(inputs["L0_attn_Wgate"]),
        "Wk": cont(inputs["L0_attn_Wk"]),
        "MT": cont(inputs["L0_attn_M"].T),
        "Wmem": cont(inputs["L0_attn_Wmem"]),
        "Wc": cont(inputs["L0_attn_Wc"]),
        "Wg": cont(inputs["L0_attn_Wg"]),
        "Wup": cont(inputs["L0_attn_Wup"]),
        "Wdown": cont(inputs["L0_attn_Wdown"]),
        "W1": cont(inputs["L0_ffn_W1"]),
        "W2": cont(inputs["L0_ffn_W2"]),
        "Ws1": cont(inputs["L0_ffn_Ws1"]),
        "Ws2": cont(inputs["L0_ffn_Ws2"]),
        "gateW": cont(inputs["L0_ffn_gate_W"]),
        "n1w_c": cont(inputs["L0_attn_norm1_w"].reshape(DC, P).T),
        "n2w_c": cont(inputs["L0_attn_norm2_w"].reshape(DC, P).T),
        "bd_c": cont(inputs["L0_attn_bdecay"].reshape(DC, P).T),
        "bg_c": cont(inputs["L0_attn_bgate"].reshape(DC, P).T),
        "cw_cols": cont(conv_w.reshape(DC, P, 7).transpose(1, 0, 2).reshape(P, DC * 7)),
        "bc_row": cont(inputs["L0_attn_bc"].reshape(1, D)),
        "bs1_row": cont(inputs["L0_ffn_bs1"].reshape(1, H)),
        "bs2_row": cont(inputs["L0_ffn_bs2"].reshape(1, D)),
        "b1_rows": cont(inputs["L0_ffn_b1"].reshape(1, E, H)),
        "b2_pm": cont(inputs["L0_ffn_b2"]),
        "ss": cont(np.reshape(inputs["L0_attn_state_scale"], (1, 1))),
        "ms": cont(np.reshape(inputs["L0_attn_mem_scale"], (1, 1))),
        "rs": cont(np.reshape(inputs["L0_ffn_route_scale"], (1, 1))),
    }
    if _nc_cache is None:
        _nc_cache = build()
    nc = _nc_cache
    in_maps = []
    for c in range(8):
        m = dict(aux)
        m["x"] = x[c * NSEQ:(c + 1) * NSEQ]
        in_maps.append(m)
    res = run_bass_kernel_spmd(nc, in_maps, core_ids=list(range(8)))
    return np.concatenate([r["y"] for r in res.results], axis=0)


# version 12
# speedup vs baseline: 1.2363x; 1.1605x over previous
import numpy as np
from contextlib import ExitStack

import concourse.bass as bass
import concourse.mybir as mybir
from concourse.bass_utils import run_bass_kernel_spmd

F32 = mybir.dt.float32
BF16 = mybir.dt.bfloat16
AF = mybir.ActivationFunctionType
ALU = mybir.AluOpType
AX = mybir.AxisListType

P = 128
T = 2048
NSEQ = 2
D = 1024
DC = D // P
FT = 512
NF = T // FT
NTILE = T // P
TPF = FT // P
DM = 256
DF = 4096
E = 8
H = 512
LNC = 16.11809565095832

PHASES = ("A", "B", "C", "D")
DEBUG_SCRATCH = False
LIMIT_NF = None
LIMIT_NSEQ = None
A_PARTS = ("tiles", "stats", "c2r", "diag", "evac", "kmem", "conv", "decay")


class Rec:
    __slots__ = ("eng", "fn", "reads", "writes", "dma", "flag", "semval", "idx", "deps",
                 "slot", "use")

    def __init__(self, eng, fn, reads, writes, dma):
        self.eng, self.fn = eng, fn
        self.reads, self.writes = list(reads), list(writes)
        self.dma = dma
        self.flag = dma
        self.semval = None
        self.deps = []
        self.slot = self.use = 0


NSLOT = 8


class Prog:
    ENGINES = ("sync", "gpsimd", "tensor", "vector", "scalar")

    def __init__(self, nc, es):
        self.nc = nc
        self.ops = []
        self.last_w = {}
        self.readers = {}
        self.c_sem = {e: es.enter_context(nc.semaphore(f"c_{e}")) for e in self.ENGINES}
        self.d_sem = {e: [es.enter_context(nc.semaphore(f"d_{e}{i}")) for i in range(NSLOT)]
                      for e in ("sync", "gpsimd", "scalar")}
        self.dma_count = {e: 0 for e in ("sync", "gpsimd", "scalar")}

    PSUMNAMES = {"pbig", "pbig2", "ps0", "ps1", "ps2", "ps3"}

    def op(self, eng, fn, reads=(), writes=(), dma=False):
        o = Rec(eng, fn, reads, writes, dma)
        o.idx = len(self.ops)
        if dma:
            k = self.dma_count[eng]
            o.slot, o.use = k % NSLOT, k // NSLOT
            o.semval = 16 * (o.use + 1)
            self.dma_count[eng] = k + 1
        deps = {}
        for k in o.reads:
            d = self.last_w.get(k)
            if d is not None:
                deps[d.idx] = d
        for k in o.writes:
            for d in self.readers.get(k, ()):
                deps[d.idx] = d
            d = self.last_w.get(k)
            if d is not None:
                deps[d.idx] = d
        if not dma and eng in ("scalar", "vector"):
            if not hasattr(self, "_last_psum_by_key"):
                self._last_psum_by_key = {}
            for k in list(o.reads) + list(o.writes):
                base = k if isinstance(k, str) else k[0]
                if base in self.PSUMNAMES:
                    if base in ("pbig", "pbig2"):
                        banks = [(base, k[1])] if isinstance(k, tuple) else \
                            [(base, 0), (base, 1)]
                    else:
                        banks = [base]
                    for b in banks:
                        prev = self._last_psum_by_key.get(b)
                        if prev is not None and prev.eng != eng:
                            deps[prev.idx] = prev
                        self._last_psum_by_key[b] = o
        o.deps = list(deps.values())
        for k in o.writes:
            self.last_w[k] = o
            self.readers[k] = []
        for k in o.reads:
            self.readers.setdefault(k, []).append(o)
        self.ops.append(o)
        return o

    def finalize(self):
        for o in self.ops:
            for d in o.deps:
                if d.eng == o.eng == "tensor" and not d.dma:
                    continue
                d.flag = True
        cnt = {}
        for o in self.ops:
            if o.flag and not o.dma:
                cnt[o.eng] = cnt.get(o.eng, 0) + 1
                o.semval = cnt[o.eng]

    def emit(self, eng_name, eng):
        watermark = {}

        def wait(sem, v):
            if watermark.get(id(sem), -1) < v:
                eng.wait_ge(sem, v)
                watermark[id(sem)] = v

        for o in self.ops:
            if o.eng != eng_name:
                continue
            waits = {}
            for d in o.deps:
                if d.eng == o.eng and not d.dma and (o.eng == "tensor" or not d.flag):
                    continue
                sem = self.d_sem[d.eng][d.slot] if d.dma else self.c_sem[d.eng]
                cur = waits.get(id(sem))
                if cur is None or d.semval > cur[1]:
                    waits[id(sem)] = (sem, d.semval)
            for sem, v in waits.values():
                wait(sem, v)
            if o.dma and o.use > 0:
                wait(self.d_sem[o.eng][o.slot], 16 * o.use)
            inst = o.fn(eng)
            if o.flag:
                if o.dma:
                    inst.then_inc(self.d_sem[o.eng][o.slot], 16)
                else:
                    inst.then_inc(self.c_sem[o.eng], 1)

    def fence(self, eng):
        for e2 in ("sync", "gpsimd", "scalar"):
            n = self.dma_count[e2]
            for i in range(NSLOT):
                uses = (n - i + NSLOT - 1) // NSLOT if n > i else 0
                if uses > 0:
                    eng.wait_ge(self.d_sem[e2][i], 16 * uses)


def build():
    nc = bass.Bass("TRN2", target_bir_lowering=False, debug=False)

    def din(name, shape, dt=F32):
        return nc.dram_tensor(name, list(shape), dt, kind="ExternalInput").ap()

    x_in = din("x", [NSEQ, T, D])
    y_out = nc.dram_tensor("y", [NSEQ, T, D], F32, kind="ExternalOutput").ap()

    Wdecay = din("Wdecay", [D, D]); Wgate = din("Wgate", [D, D])
    Wk = din("Wk", [D, DM]); MT = din("MT", [DM, DM]); Wmem = din("Wmem", [DM, D])
    Wc = din("Wc", [3 * D, D])
    Wg = din("Wg", [D, DF]); Wup = din("Wup", [D, DF]); Wdown = din("Wdown", [DF, D])
    W1 = din("W1", [E, D, H]); W2 = din("W2", [E, H, D])
    Ws1 = din("Ws1", [D, H]); Ws2 = din("Ws2", [H, D])
    gateW = din("gateW", [D, E])
    n1w_i = din("n1w_c", [P, DC]); n2w_i = din("n2w_c", [P, DC])
    bd_i = din("bd_c", [P, DC]); bg_i = din("bg_c", [P, DC])
    cw_i = din("cw_cols", [P, DC * 7])
    bc_i = din("bc_row", [1, D]); bs1_i = din("bs1_row", [1, H]); bs2_i = din("bs2_row", [1, D])
    b1_i = din("b1_rows", [1, E, H]); b2_i = din("b2_pm", [E, D])
    ss_i = din("ss", [1, 1]); ms_i = din("ms", [1, 1]); rs_i = din("rs", [1, 1])

    skind = "ExternalOutput" if DEBUG_SCRATCH else "Internal"
    scr_loc = nc.dram_tensor("scr_loc", [NSEQ, P, DC, T], BF16, kind=skind).ap()
    scr_st = nc.dram_tensor("scr_st", [NSEQ, P, DC, T], BF16, kind=skind).ap()
    scr_mk = nc.dram_tensor("scr_mk", [NSEQ, P, 2, T], BF16, kind=skind).ap()
    scr_mix = nc.dram_tensor("scr_mix", [NSEQ, T, D], F32, kind=skind).ap()
    scr_x1 = nc.dram_tensor("scr_x1", [NSEQ, T, D], F32, kind=skind).ap()
    scr_ffa = nc.dram_tensor("scr_ffa", [NSEQ, T, D], F32, kind=skind).ap()
    scr_fna = nc.dram_tensor("scr_fna", [NSEQ, T, D], F32, kind=skind).ap()
    scr_dbg = nc.dram_tensor("scr_dbg", [P, DC, T], BF16, kind=skind).ap()
    scr_dbg2 = nc.dram_tensor("scr_dbg2", [P, DC, FT], F32, kind=skind).ap()
    scr_dbg3 = nc.dram_tensor("scr_dbg3", [P, 4, NTILE], F32, kind=skind).ap()

    eye_i = nc.inline_tensor(np.eye(P, dtype=np.float32), "eye_i").ap()

    es = ExitStack()
    with es:
        def sb(name, shape, dt=F32):
            return es.enter_context(nc.sbuf_tensor(name, list(shape), dt))

        def psum(name, shape):
            return es.enter_context(nc.psum_tensor(name, list(shape), F32))

        eye_f = sb("eye_f", [P, P]); eye_b = sb("eye_b", [P, P], BF16)
        ones_f = sb("ones_f", [1, P]); ones_b = sb("ones_b", [1, P], BF16)
        zerosF = sb("zerosF", [P, FT])
        consts = sb("consts", [P, 8])
        n1w = sb("n1w", [P, DC]); n2w = sb("n2w", [P, DC])
        bd_c = sb("bdcol", [P, DC]); bg_c = sb("bgcol", [P, DC])
        cw_c = sb("cwcol", [P, DC * 7])
        diags = sb("diags", [P, DC * 7 * P], BF16)
        b1_r = diags[0:1, 0:E * H].rearrange("o (e2 h2) -> o e2 h2", e2=E)
        b1K = ["b1"] + [("diag", i) for i in range(32)]
        MT_b = sb("MT_b", [P, 2, DM], BF16)
        gateW_b = sb("gateW_b", [P, DC, E], BF16)
        sscol = sb("sscol", [P, 1]); mscol = sb("mscol", [P, 1]); rscol = sb("rscol", [P, 1])
        nss = sb("nss", [P, DC])
        carA = sb("carA", [P, DC]); carC = sb("carC", [P, DC])

        MUc = [sb(f"MUc{s}", [P, NTILE]) for s in range(NSEQ)]
        Rc = [sb(f"Rc{s}", [P, NTILE]) for s in range(NSEQ)]
        Sc = [sb(f"Sc{s}", [P, NTILE]) for s in range(NSEQ)]
        NMSc = [sb(f"NMSc{s}", [P, NTILE]) for s in range(NSEQ)]
        R2c = [sb(f"R2c{s}", [P, NTILE]) for s in range(NSEQ)]
        MU4c = [sb(f"MU4c{s}", [P, NTILE]) for s in range(NSEQ)]
        R4c = [sb(f"R4c{s}", [P, NTILE]) for s in range(NSEQ)]
        GF = sb("GFt", [P, NTILE, E])
        GCM = sb("GCMt", [E, T], BF16)
        stat_t = sb("stat_t", [P, 8])
        rowb = sb("rowb", [1, 2 * P], BF16)

        actT = sb("actT", [P, DC, T], BF16)
        wpool = sb("wpool", [P, 32768], BF16)

        g32a = sb("g32a", [P, DC, FT])
        ldc = sb("ldc", [P, FT]); Acb = sb("Acb", [P, FT]); cdc = sb("cdc", [P, FT])
        ecb = sb("ecb", [P, FT]); ucb = sb("ucb", [P, FT]); Ccb = sb("Ccb", [P, FT])
        sluws = sb("sluws", [P, 1, FT])
        cma = [sb(f"cma{i}", [P, DC, FT], BF16) for i in range(2)]
        kbf = cma[1]
        cmb0 = sb("cmb0", [P, 2, FT], BF16)
        xws = sb("xws", [P, 2, D])
        sqh = sb("sqh", [P, FT])
        diagws = sb("diagws", [P, 1, P], BF16)
        mixws = sb("mixws", [P, 2, D])
        accws = sb("accws", [P, 2, D])
        fows = sb("fows", [P, D])
        x1ws = sb("x1ws", [P, D])
        gscb = sb("gscb", [P, 2, H], BF16)
        gcmw = sb("gcmw", [P, 1, 4, P], BF16)
        scb = sb("scb", [P, E])
        scb2 = sb("scb2", [P, E])

        pbig = psum("pbig", [P, 1024])
        pbig2 = psum("pbig2", [P, 1024])
        ps0 = psum("ps0", [P, FT]); ps1 = psum("ps1", [P, FT]); ps2 = psum("ps2", [P, FT])
        ps3 = psum("ps3", [P, FT])
        prow = ps3

        cvals = [1e-5, 1e-6, -0.5, -1.0]
        for i, cv in enumerate(cvals):
            nc.const_aps.register(F32, cv, consts[:, i:i + 1])

        prog = Prog(nc, es)
        OP = prog.op

        def wstrips(off, n):
            return [f"wp{j}" for j in range(off // 4096, (off + n - 1) // 4096 + 1)]

        def wview(off, shape):
            n = int(np.prod(shape))
            ap = wpool[:, off:off + n]
            if len(shape) > 1:
                pat = "p (" + " ".join(f"d{i}" for i in range(len(shape))) + ") -> p " + \
                      " ".join(f"d{i}" for i in range(len(shape)))
                ap = ap.rearrange(pat, **{f"d{i}": s for i, s in enumerate(shape)})
            return ap, wstrips(off, n)

        WdV, WdK = wview(0, [DC, D])
        WgV, WgK = wview(8192, [DC, D])
        WkV, WkK = wview(16384, [DC, DM])
        WcV, WcK = wview(0, [24, D])
        WmV, WmK = wview(24576, [2, D])
        bcV = wpool[0:1, 26624:27648]; bcK = wstrips(26624, 1024)
        WgqV, WgqK = wview(0, [DC, 1024])
        WuqV, WuqK = wview(8192, [DC, 1024])
        WdqV, WdqK = wview(16384, [DC, 1024])
        Ws1V, Ws1K = wview(0, [DC, H])
        Ws2V, Ws2K = wview(4096, [4, D])
        bs1V = wpool[0:1, 8192:8704]; bs1K = wstrips(8192, 512)
        bs2V = wpool[0:1, 8704:9728]; bs2K = wstrips(8704, 1024)
        b2V = wpool[0:8, 9728:10752]; b2K = wstrips(9728, 1024)
        W1hV, W1hK = wview(0, [4, DC, H])
        W2hV, W2hK = wview(16384, [4, 4, D])

        membf = actT[:, :, 0:FT]
        membfK = [("actT", q) for q in range(TPF)]

        def load_w(dst, src_ap, keys, eng="gpsimd"):
            OP(eng, lambda e, d=dst, s=src_ap: e.dma_start(out=d, in_=s),
               reads=[], writes=keys, dma=True)

        def setup():
            def ld(dst, src, keys, eng="sync"):
                OP(eng, lambda e, d=dst, s=src: e.dma_start(out=d, in_=s),
                   reads=[], writes=keys, dma=True)

            ld(eye_f[:], eye_i, ["eye_f"])
            ld(n1w[:], n1w_i, ["n1w"]); ld(n2w[:], n2w_i, ["n2w"])
            ld(bd_c[:], bd_i, ["bd"]); ld(bg_c[:], bg_i, ["bg"])
            ld(cw_c[:], cw_i, ["cw"])
            ld(MT_b[:], MT.rearrange("(c p) n -> p c n", p=P), ["MT"], eng="gpsimd")
            ld(gateW_b[:], gateW.rearrange("(c p) n -> p c n", p=P), ["gateW"], eng="gpsimd")
            ld(sscol[0:1, :], ss_i, ["ssv"]); ld(mscol[0:1, :], ms_i, ["msv"])
            ld(rscol[0:1, :], rs_i, ["rsv"])

            OP("vector", lambda e: e.memset(ones_f[:], 1.0), writes=["ones_f"])
            OP("vector", lambda e: e.memset(ones_b[:], 1.0), writes=["ones_b"])
            OP("vector", lambda e: e.memset(zerosF[:], 0.0), writes=["zerosF"])
            for i, cv in enumerate(cvals):
                OP("vector", lambda e, i=i, cv=cv: e.memset(consts[:, i:i + 1], cv),
                   writes=["consts"])
            OP("vector", lambda e: e.tensor_copy(eye_b[:], eye_f[:]),
               reads=["eye_f"], writes=["eye_b"])
            for name, col in (("ssv", sscol), ("msv", mscol), ("rsv", rscol)):
                OP("tensor", lambda e, c=col: e.matmul(ps2[:, 0:1], ones_f[:], c[0:1, :],
                                                       start=True, stop=True),
                   reads=[name, "ones_f"], writes=["ps2"])
                OP("vector", lambda e, c=col: e.tensor_copy(c[:], ps2[:, 0:1]),
                   reads=["ps2"], writes=[name])
            OP("vector", lambda e: e.tensor_scalar_mul(nss[:], n1w[:], sscol[:]),
               reads=["n1w", "ssv"], writes=["nss"])
            for c in range(DC):
                for j in range(7):
                    idx = c * 7 + j
                    OP("vector", lambda e, idx=idx: e.tensor_scalar_mul(
                        diags[:, idx * P:(idx + 1) * P], eye_f[:], cw_c[:, idx:idx + 1]),
                       reads=["eye_f", "cw"], writes=[("diag", idx)])

        def stats_pass(src_key, src_ap, mu_ap, mu_key, r_ap, r_key):
            OP("vector", lambda e: e.reduce_sum(stat_t[:, 0:1], src_ap, axis=AX.X),
               reads=[src_key], writes=["stat"])
            for hh in range(2):
                OP("scalar", lambda e, hh=hh: e.activation(
                    sqh[:], src_ap[:, hh * FT:(hh + 1) * FT], AF.Square),
                   reads=[src_key], writes=["sqh"])
                OP("vector", lambda e, hh=hh: e.reduce_sum(stat_t[:, 1 + hh:2 + hh], sqh[:],
                                                           axis=AX.X),
                   reads=["sqh"], writes=["stat"])
            OP("vector", lambda e: e.tensor_add(stat_t[:, 1:2], stat_t[:, 1:2], stat_t[:, 2:3]),
               reads=["stat"], writes=["stat"])
            OP("vector", lambda e: e.tensor_scalar_mul(mu_ap, stat_t[:, 0:1], 1.0 / D),
               reads=["stat"], writes=[mu_key])
            OP("vector", lambda e: e.tensor_tensor(stat_t[:, 3:4], mu_ap, mu_ap, ALU.mult),
               reads=[mu_key], writes=["stat"])
            OP("vector", lambda e: e.tensor_scalar_mul(stat_t[:, 2:3], stat_t[:, 1:2], 1.0 / D),
               reads=["stat"], writes=["stat"])
            OP("vector", lambda e: e.tensor_tensor(stat_t[:, 4:5], stat_t[:, 2:3],
                                                   stat_t[:, 3:4], ALU.subtract),
               reads=["stat"], writes=["stat"])
            OP("scalar", lambda e: e.activation(stat_t[:, 3:4], stat_t[:, 4:5], AF.Ln, bias=1e-5),
               reads=["stat"], writes=["stat"])
            OP("scalar", lambda e: e.activation(r_ap, stat_t[:, 3:4], AF.Exp, scale=-0.5),
               reads=["stat"], writes=[r_key])

        def col_to_row(col_ap, col_key, slot=0):
            OP("tensor", lambda e: e.matmul(ps3[0:1, 0:P], col_ap, eye_f[:], start=True, stop=True),
               reads=[col_key, "eye_f"], writes=["ps3"])
            OP("scalar", lambda e, slot=slot: e.copy(rowb[:, slot * P:(slot + 1) * P],
                                                     ps3[0:1, 0:P]),
               reads=["ps3"], writes=[("rowb", slot)])

        def diag_transpose(src_ap, src_key, scol_ap, scol_key, with_row,
                           pt=None, kb="pbig", slot=0):
            pt = pbig if pt is None else pt
            dw = diagws[:, 0]
            dwk = "diagws"
            OP("vector", lambda e, dw=dw: e.tensor_scalar_mul(dw, eye_b[:], scol_ap),
               reads=["eye_b", scol_key], writes=[dwk])
            for c in range(DC):
                OP("tensor", lambda e, c=c, pt=pt, dw=dw: e.matmul(
                    pt[:, c * P:(c + 1) * P], src_ap[:, c * P:(c + 1) * P], dw,
                    start=True, stop=not with_row),
                   reads=[src_key, dwk], writes=[(kb, c // 4)])
                if with_row:
                    OP("tensor", lambda e, c=c, pt=pt, slot=slot: e.matmul(
                        pt[:, c * P:(c + 1) * P], ones_b[:], rowb[:, slot * P:(slot + 1) * P],
                        start=False, stop=True),
                       reads=["ones_b", ("rowb", slot)], writes=[(kb, c // 4)])

        def phase_a(seq):
            OP("vector", lambda e: e.memset(carA[:], 0.0),
               writes=[("carA", c2) for c2 in range(DC)])
            OP("vector", lambda e: e.memset(carC[:], 0.0),
               writes=[("carC", c2) for c2 in range(DC)])
            for fi in range(LIMIT_NF or NF):
                t0 = fi * FT
                for tt in range(TPF):
                    g = fi * TPF + tt
                    xv = xws[:, g % 2]
                    xk = ("xws", g % 2)
                    xbv = cma[0][:, 2 * (g % 2):2 * (g % 2) + 2, :].rearrange(
                        "p a b -> p (a b)")
                    OP("sync", lambda e, g=g, xv=xv: e.dma_start(
                        out=xv, in_=x_in[seq, g * P:(g + 1) * P, :]),
                       reads=[], writes=[xk], dma=True)
                    OP("gpsimd", lambda e, g=g, xbv=xbv: e.dma_start(
                        out=xbv, in_=x_in[seq, g * P:(g + 1) * P, :]),
                       reads=[], writes=["cma0"], dma=True)
                    if "stats" not in A_PARTS:
                        continue
                    stats_pass(xk, xv, MUc[seq][:, g:g + 1], ("MUc", seq, g),
                               Rc[seq][:, g:g + 1], ("Rc", seq, g))
                    OP("vector", lambda e, g=g: e.tensor_scalar(
                        stat_t[:, 6:7], stat_t[:, 4:5], Rc[seq][:, g:g + 1],
                        Rc[seq][:, g:g + 1], ALU.mult, ALU.mult),
                       reads=["stat", ("Rc", seq, g)], writes=["stat"])
                    OP("scalar", lambda e: e.activation(stat_t[:, 6:7], stat_t[:, 6:7],
                                                        AF.Ln, bias=1e-6),
                       reads=["stat"], writes=["stat"])
                    OP("scalar", lambda e: e.activation(stat_t[:, 7:8], stat_t[:, 6:7],
                                                        AF.Exp, scale=-0.5),
                       reads=["stat"], writes=["stat"])
                    OP("vector", lambda e, g=g: e.tensor_tensor(
                        Sc[seq][:, g:g + 1], Rc[seq][:, g:g + 1], stat_t[:, 7:8], ALU.mult),
                       reads=["stat", ("Rc", seq, g)], writes=[("Sc", seq, g)])
                    OP("vector", lambda e, g=g: e.scalar_tensor_tensor(
                        NMSc[seq][:, g:g + 1], MUc[seq][:, g:g + 1], -1.0,
                        Sc[seq][:, g:g + 1], ALU.mult, ALU.mult),
                       reads=[("MUc", seq, g), ("Sc", seq, g)], writes=[("NMSc", seq, g)])
                    if "c2r" not in A_PARTS:
                        continue
                    col_to_row(NMSc[seq][:, g:g + 1], ("NMSc", seq, g), slot=g % 2)
                    if "diag" not in A_PARTS:
                        continue
                    diag_transpose(xbv, "cma0", Sc[seq][:, g:g + 1], ("Sc", seq, g), True,
                                   slot=g % 2)
                    if "evac" not in A_PARTS:
                        continue
                    for hh in range(2):
                        OP("scalar", lambda e, tt=tt, hh=hh: e.copy(
                            g32a[:, hh * 4:(hh + 1) * 4, tt * P:(tt + 1) * P],
                            pbig[:, hh * FT:(hh + 1) * FT].rearrange("p (c q) -> p c q", q=P)),
                           reads=[("pbig", hh)], writes=["g32a"])
                    for c in range(DC):
                        OP("vector", lambda e, c=c, g=g: e.tensor_scalar_mul(
                            actT[:, c, g * P:(g + 1) * P], pbig[:, c * P:(c + 1) * P],
                            n1w[:, c:c + 1]),
                           reads=[("pbig", c // 4), "n1w"], writes=[("actT", g)])
                hbT_keys = [("actT", fi * TPF + q) for q in range(TPF)]
                hbT_prev = [("actT", fi * TPF - 1)] if fi > 0 else []

                for koc in range(2 if "kmem" in A_PARTS else 0):
                    for kc in range(DC):
                        OP("tensor", lambda e, koc=koc, kc=kc, fi=fi: e.matmul(
                            ps2[:], WkV[:, kc, koc * P:(koc + 1) * P],
                            actT[:, kc, fi * FT:(fi + 1) * FT],
                            start=(kc == 0), stop=(kc == DC - 1)),
                           reads=WkK + hbT_keys, writes=["ps2"])
                    OP("vector", lambda e, koc=koc: e.tensor_copy(kbf[:, koc], ps2[:]),
                       reads=["ps2"], writes=["cma1"])
                for ioc in range(2 if "kmem" in A_PARTS else 0):
                    for jc in range(2):
                        OP("tensor", lambda e, ioc=ioc, jc=jc: e.matmul(
                            ps2[:], MT_b[:, jc, ioc * P:(ioc + 1) * P], kbf[:, jc],
                            start=(jc == 0), stop=(jc == 1)),
                           reads=["MT", "cma1"], writes=["ps2"])
                    OP("vector", lambda e, ioc=ioc: e.tensor_scalar_mul(
                        cmb0[:, ioc], ps2[:], mscol[:]),
                       reads=["ps2", "msv"], writes=["cmb0"])
                if "kmem" in A_PARTS:
                    OP("gpsimd", lambda e, t0=t0: e.dma_start(
                        out=scr_mk[seq, :, :, t0:t0 + FT], in_=cmb0[:]),
                       reads=["cmb0"], writes=[("scr_mk", seq, t0)], dma=True)

                for c in range(DC if "conv" in A_PARTS else 0):
                    pb, pk = (ps0, ["ps0"]) if c % 2 == 0 else \
                        (ps1, [("ps1", i) for i in range(4)])
                    for s in range(7):
                        idx = c * 7 + (6 - s)
                        if fi == 0 and s > 0:
                            OP("tensor", lambda e, c=c, s=s, idx=idx, pb=pb: e.matmul(
                                pb[:, s:FT], diags[:, idx * P:(idx + 1) * P],
                                actT[:, c, 0:FT - s], start=False, stop=(s == 6)),
                               reads=[("diag", idx)] + hbT_keys, writes=pk)
                        else:
                            OP("tensor", lambda e, c=c, s=s, idx=idx, pb=pb, fi=fi: e.matmul(
                                pb[:], diags[:, idx * P:(idx + 1) * P],
                                actT[:, c, fi * FT - s:(fi + 1) * FT - s],
                                start=(s == 0), stop=(s == 6)),
                               reads=[("diag", idx)] + hbT_keys + hbT_prev, writes=pk)
                    OP("scalar", lambda e, c=c, pb=pb: e.copy(cma[0][:, c, :], pb[:]),
                       reads=pk, writes=["cma0"])
                if "conv" in A_PARTS:
                    OP("gpsimd", lambda e, t0=t0: e.dma_start(
                        out=scr_loc[seq, :, :, t0:t0 + FT], in_=cma[0][:]),
                       reads=["cma0"], writes=[("scr_loc", seq, t0)], dma=True)

                def dk_bufs(c):
                    if c % 2 == 0:
                        return ((ldc[:], Acb[:], cdc[:], ecb[:], ucb[:], Ccb[:]),
                                ("ldc_a", "ldc_a", "cde_a", "cde_a", "uc_a", "uc_a"))
                    return ((mixws[:, 0, 0:FT], mixws[:, 0, FT:2 * FT],
                             mixws[:, 1, 0:FT], mixws[:, 1, FT:2 * FT],
                             accws[:, 0, 0:FT], accws[:, 0, FT:2 * FT]),
                            (("mixws", 0), ("mixws", 0), ("mixws", 1), ("mixws", 1),
                             ("accws", 0), ("accws", 0)))

                def decay_mm_sig(c, fi):
                    (bldc, bAcb, bcdc, becb, bucb, bCcb), \
                        (kldc, kAcb, kcdc, kecb, kucb, kCcb) = dk_bufs(c)
                    pzd, pzdk = (ps0, "ps0") if c % 2 == 0 else (ps2, "ps2")
                    pzg, pzgk = (ps1[:], [("ps1", i) for i in range(4)]) \
                        if c % 2 == 0 else (pbig2[:, 0:FT], [("pbig2", 0)])
                    for kc in range(DC):
                        OP("tensor", lambda e, c=c, kc=kc, fi=fi, pzd=pzd: e.matmul(
                            pzd[:], WdV[:, kc, c * P:(c + 1) * P],
                            actT[:, kc, fi * FT:(fi + 1) * FT],
                            start=(kc == 0), stop=(kc == DC - 1)),
                           reads=WdK + hbT_keys, writes=[pzdk])
                    for kc in range(DC):
                        OP("tensor", lambda e, c=c, kc=kc, fi=fi, pzg=pzg: e.matmul(
                            pzg, WgV[:, kc, c * P:(c + 1) * P],
                            actT[:, kc, fi * FT:(fi + 1) * FT],
                            start=(kc == 0), stop=(kc == DC - 1)),
                           reads=WgK + hbT_keys, writes=pzgk)
                    OP("scalar", lambda e, c=c, pzd=pzd, bldc=bldc: e.activation(
                        bldc, pzd[:], AF.Sigmoid, bias=bd_c[:, c:c + 1]),
                       reads=[pzdk, "bd"], writes=[kldc])
                    OP("scalar", lambda e, c=c, pzg=pzg, bucb=bucb: e.activation(
                        bucb, pzg, AF.Sigmoid, bias=bg_c[:, c:c + 1]),
                       reads=pzgk + ["bg"], writes=[kucb])

                def decay_rest(c):
                    (bldc, bAcb, bcdc, becb, bucb, bCcb), \
                        (kldc, kAcb, kcdc, kecb, kucb, kCcb) = dk_bufs(c)
                    OP("vector", lambda e, bldc=bldc: e.tensor_scalar(
                        bldc, bldc, 0.95, 0.05, ALU.min, ALU.max),
                       reads=[kldc], writes=[kldc])
                    OP("scalar", lambda e, bldc=bldc: e.activation(bldc, bldc, AF.Ln),
                       reads=[kldc], writes=[kldc])
                    OP("vector", lambda e, c=c, bldc=bldc, bAcb=bAcb: e.tensor_tensor_scan(
                        bAcb, bldc, zerosF[:], carA[:, c:c + 1], ALU.add, ALU.add),
                       reads=[kldc, "zerosF", ("carA", c)], writes=[kAcb])
                    OP("vector", lambda e, c=c, bldc=bldc: e.tensor_scalar(
                        bldc[:, 0:1], carA[:, c:c + 1], -1.0, LNC, ALU.mult, ALU.min),
                       reads=[("carA", c)], writes=[kldc])
                    OP("vector", lambda e, bldc=bldc, bAcb=bAcb: e.tensor_scalar(
                        bldc[:, 1:FT], bAcb[:, 0:FT - 1], -1.0, LNC, ALU.mult, ALU.min),
                       reads=[kAcb], writes=[kldc])
                    OP("vector", lambda e, c=c, bAcb=bAcb: e.tensor_copy(
                        carA[:, c:c + 1], bAcb[:, FT - 1:FT]),
                       reads=[kAcb], writes=[("carA", c)])
                    OP("scalar", lambda e, bcdc=bcdc, bAcb=bAcb: e.activation(
                        bcdc, bAcb, AF.Exp),
                       reads=[kAcb], writes=[kcdc])
                    OP("scalar", lambda e, becb=becb, bldc=bldc: e.activation(
                        becb, bldc, AF.Exp),
                       reads=[kldc], writes=[kecb])
                    OP("vector", lambda e, c=c, bucb=bucb: e.tensor_mul(
                        bucb, bucb, g32a[:, c, :]),
                       reads=[kucb, "g32a"], writes=[kucb])
                    OP("vector", lambda e, bldc=bldc, bucb=bucb, becb=becb: e.tensor_mul(
                        bldc, bucb, becb),
                       reads=[kucb, kecb], writes=[kldc])
                    OP("vector", lambda e, c=c, bCcb=bCcb, bldc=bldc: e.tensor_tensor_scan(
                        bCcb, bldc, zerosF[:], carC[:, c:c + 1], ALU.add, ALU.add),
                       reads=[kldc, "zerosF", ("carC", c)], writes=[kCcb])
                    OP("vector", lambda e, c=c, bCcb=bCcb: e.tensor_copy(
                        carC[:, c:c + 1], bCcb[:, FT - 1:FT]),
                       reads=[kCcb], writes=[("carC", c)])
                    OP("vector", lambda e, bucb=bucb, bCcb=bCcb, bcdc=bcdc: e.tensor_mul(
                        bucb, bCcb, bcdc),
                       reads=[kCcb, kcdc], writes=[kucb])
                    OP("vector", lambda e, c=c, bucb=bucb: e.tensor_scalar_mul(
                        cma[1][:, c, :], bucb, nss[:, c:c + 1]),
                       reads=[kucb, "nss"], writes=["cma1"])

                if "decay" in A_PARTS:
                    for cp in range(DC // 2):
                        decay_mm_sig(2 * cp, fi)
                        decay_mm_sig(2 * cp + 1, fi)
                        decay_rest(2 * cp)
                        decay_rest(2 * cp + 1)
                if "decay" in A_PARTS:
                    OP("gpsimd", lambda e, t0=t0: e.dma_start(
                        out=scr_st[seq, :, :, t0:t0 + FT], in_=cma[1][:]),
                       reads=["cma1"], writes=[("scr_st", seq, t0)], dma=True)
            if DEBUG_SCRATCH and seq == 0:
                OP("gpsimd", lambda e: e.dma_start(out=scr_dbg, in_=actT[:]),
                   reads=[("actT", q) for q in range(NTILE)], writes=[], dma=True)
                OP("gpsimd", lambda e: e.dma_start(out=scr_dbg2, in_=g32a[:]),
                   reads=["g32a"], writes=[], dma=True)
                for i3, (tn, tk) in enumerate([(MUc[0], "MUc"), (Rc[0], "Rc"),
                                               (Sc[0], "Sc"), (NMSc[0], "NMSc")]):
                    OP("gpsimd", lambda e, i3=i3, tn=tn: e.dma_start(
                        out=scr_dbg3[:, i3, :], in_=tn[:]),
                       reads=[(tk, 0, q) for q in range(NTILE)], writes=[], dma=True)

        def phase_b(seq):
            for fi in range(LIMIT_NF or NF):
                t0 = fi * FT
                OP("sync", lambda e, t0=t0: e.dma_start(
                    out=cma[0][:], in_=scr_loc[seq, :, :, t0:t0 + FT]),
                   reads=[("scr_loc", seq, t0)], writes=["cma0"], dma=True)
                OP("sync", lambda e, t0=t0: e.dma_start(
                    out=cma[1][:], in_=scr_st[seq, :, :, t0:t0 + FT]),
                   reads=[("scr_st", seq, t0)], writes=["cma1"], dma=True)
                OP("sync", lambda e, t0=t0: e.dma_start(
                    out=cmb0[:], in_=scr_mk[seq, :, :, t0:t0 + FT]),
                   reads=[("scr_mk", seq, t0)], writes=["cmb0"], dma=True)
                for moc in range(DC):
                    pb, pk = (ps0, ["ps0"]) if moc % 2 == 0 else \
                        (ps1, [("ps1", i) for i in range(4)])
                    for jc in range(2):
                        OP("tensor", lambda e, moc=moc, jc=jc, pb=pb: e.matmul(
                            pb[:], WmV[:, jc, moc * P:(moc + 1) * P], cmb0[:, jc],
                            start=(jc == 0), stop=(jc == 1)),
                           reads=WmK + ["cmb0"], writes=pk)
                    OP("vector", lambda e, moc=moc, pb=pb: e.tensor_copy(membf[:, moc], pb[:]),
                       reads=pk, writes=membfK)
                for tt in range(TPF):
                    g = fi * TPF + tt
                    xv, xk = xws[:, g % 2], ("xws", g % 2)
                    pt, kb = (pbig, "pbig") if g % 2 == 0 else (pbig2, "pbig2")
                    mv, mk2 = mixws[:, g % 2], ("mixws", g % 2)
                    OP("sync", lambda e, g=g, xv=xv: e.dma_start(
                        out=xv, in_=x_in[seq, g * P:(g + 1) * P, :]),
                       writes=[xk], dma=True)
                    for half in range(2):
                        OP("tensor", lambda e, half=half, pt=pt: e.matmul(
                            pt[:, half * FT:(half + 1) * FT], ones_b[:],
                            bcV[:, half * FT:(half + 1) * FT], start=True, stop=False),
                           reads=["ones_b"] + bcK, writes=[(kb, half)])
                    for kc in range(24):
                        if kc < 8:
                            opnd, rk = cma[0][:, kc, tt * P:(tt + 1) * P], ["cma0"]
                        elif kc < 16:
                            opnd, rk = cma[1][:, kc - 8, tt * P:(tt + 1) * P], ["cma1"]
                        else:
                            opnd, rk = membf[:, kc - 16, tt * P:(tt + 1) * P], membfK
                        for half in range(2):
                            OP("tensor", lambda e, kc=kc, half=half, opnd=opnd, pt=pt: e.matmul(
                                pt[:, half * FT:(half + 1) * FT], opnd,
                                WcV[:, kc, half * FT:(half + 1) * FT],
                                start=False, stop=(kc == 23)),
                               reads=rk + WcK, writes=[(kb, half)])
                    OP("vector", lambda e, g=g, xv=xv: e.tensor_scalar(
                        fows[:], xv, MUc[seq][:, g:g + 1], Rc[seq][:, g:g + 1],
                        ALU.subtract, ALU.mult),
                       reads=[xk, ("MUc", seq, g), ("Rc", seq, g)], writes=["fows"])
                    for hh in range(2):
                        OP("vector", lambda e, hh=hh, pt=pt, mv=mv: e.tensor_add(
                            mv[:, hh * FT:(hh + 1) * FT], pt[:, hh * FT:(hh + 1) * FT],
                            fows[:, hh * FT:(hh + 1) * FT]),
                           reads=[(kb, hh), "fows"], writes=[mk2])
                    for hh in range(2):
                        OP("scalar", lambda e, hh=hh, mv=mv: e.activation(
                            sqh[:], mv[:, hh * FT:(hh + 1) * FT], AF.Square),
                           reads=[mk2], writes=["sqh"])
                        OP("vector", lambda e, hh=hh: e.reduce_sum(
                            stat_t[:, 1 + hh:2 + hh], sqh[:], axis=AX.X),
                           reads=["sqh"], writes=["stat"])
                    OP("vector", lambda e: e.tensor_add(stat_t[:, 1:2], stat_t[:, 1:2],
                                                        stat_t[:, 2:3]),
                       reads=["stat"], writes=["stat"])
                    OP("vector", lambda e: e.tensor_scalar_mul(stat_t[:, 4:5], stat_t[:, 1:2],
                                                               1.0 / D),
                       reads=["stat"], writes=["stat"])
                    OP("scalar", lambda e: e.activation(stat_t[:, 3:4], stat_t[:, 4:5],
                                                        AF.Ln, bias=1e-6),
                       reads=["stat"], writes=["stat"])
                    OP("scalar", lambda e, g=g: e.activation(
                        R2c[seq][:, g:g + 1], stat_t[:, 3:4], AF.Exp, scale=-0.5),
                       reads=["stat"], writes=[("R2c", seq, g)])
                    OP("gpsimd", lambda e, g=g, mv=mv: e.dma_start(
                        out=scr_mix[seq, g * P:(g + 1) * P, :], in_=mv),
                       reads=[mk2], writes=[("scr_mix", seq, g)], dma=True)

        def phase_c(seq):
            for g in range(NTILE):
                pt, kb = (pbig, "pbig") if g % 2 == 0 else (pbig2, "pbig2")
                xbv = cma[0][:, 2 * (g % 2):2 * (g % 2) + 2, :].rearrange("p a b -> p (a b)")
                OP("gpsimd", lambda e, g=g, xbv=xbv: e.dma_start(
                    out=xbv, in_=scr_mix[seq, g * P:(g + 1) * P, :]),
                   reads=[("scr_mix", seq, g)], writes=["cma0"], dma=True)
                diag_transpose(xbv, "cma0", R2c[seq][:, g:g + 1], ("R2c", seq, g), False,
                               pt=pt, kb=kb, slot=g % 2)
                for c in range(DC):
                    OP("vector", lambda e, c=c, g=g, pt=pt: e.tensor_scalar_mul(
                        actT[:, c, g * P:(g + 1) * P], pt[:, c * P:(c + 1) * P],
                        n2w[:, c:c + 1]),
                       reads=[(kb, c // 4), "n2w"], writes=[("actT", g)])

            for q in range(4):
                hc0 = q * 1024
                load_w(WgqV, Wg[:, hc0:hc0 + 1024].rearrange("(c p) n -> p c n", p=P), WgqK)
                load_w(WuqV, Wup[:, hc0:hc0 + 1024].rearrange("(c p) n -> p c n", p=P), WuqK)
                load_w(WdqV, Wdown[hc0:hc0 + 1024, :].rearrange("(c p) n -> p c n", p=P), WdqK)
                for fi in range(NF):
                    t0 = fi * FT
                    hbF = [("actT", fi * TPF + qq) for qq in range(TPF)]
                    for fc in range(8):
                        if fc % 2 == 0:
                            pg, pgk = ps0[:], ["ps0"]
                            pu, puk = ps1[:], [("ps1", i) for i in range(4)]
                        else:
                            pg, pgk = pbig[:, 0:FT], [("pbig", 0)]
                            pu, puk = pbig[:, FT:2 * FT], [("pbig", 1)]
                        slu = sluws[:, 0]
                        sluk = "sluws"
                        for kc in range(DC):
                            OP("tensor", lambda e, fc=fc, kc=kc, t0=t0, pg=pg: e.matmul(
                                pg, WgqV[:, kc, fc * P:(fc + 1) * P],
                                actT[:, kc, t0:t0 + FT],
                                start=(kc == 0), stop=(kc == DC - 1)),
                               reads=WgqK + hbF, writes=pgk)
                        for kc in range(DC):
                            OP("tensor", lambda e, fc=fc, kc=kc, t0=t0, pu=pu: e.matmul(
                                pu, WuqV[:, kc, fc * P:(fc + 1) * P],
                                actT[:, kc, t0:t0 + FT],
                                start=(kc == 0), stop=(kc == DC - 1)),
                               reads=WuqK + hbF, writes=puk)
                        OP("scalar", lambda e, pg=pg, slu=slu: e.activation(slu, pg, AF.Silu),
                           reads=pgk, writes=[sluk])
                        OP("vector", lambda e, fc=fc, pu=pu, slu=slu: e.tensor_mul(
                            cma[0][:, fc, :], slu, pu),
                           reads=[sluk] + puk, writes=["cma0"])
                    for tt in range(TPF):
                        g = fi * TPF + tt
                        av, avk = accws[:, g % 2], ("accws", g % 2)
                        mv, mk2 = mixws[:, g % 2], ("mixws", g % 2)
                        wd_ps = [(pbig2[:, 0:FT], ("pbig2", 0)),
                                 (pbig2[:, FT:2 * FT], ("pbig2", 1))] if g % 2 == 0 else \
                                [(ps2[:], "ps2"), (ps3[:], "ps3")]
                        for fc in range(8):
                            for hh in range(2):
                                wp_, wpk_ = wd_ps[hh]
                                OP("tensor", lambda e, fc=fc, hh=hh, tt=tt, wp_=wp_: e.matmul(
                                    wp_, cma[0][:, fc, tt * P:(tt + 1) * P],
                                    WdqV[:, fc, hh * FT:(hh + 1) * FT],
                                    start=(fc == 0), stop=(fc == 7)),
                                   reads=["cma0"] + WdqK, writes=[wpk_])
                        if q < 3:
                            if q == 0:
                                for hh in range(2):
                                    wp_, wpk_ = wd_ps[hh]
                                    OP("scalar", lambda e, hh=hh, av=av, wp_=wp_: e.copy(
                                        av[:, hh * FT:(hh + 1) * FT], wp_),
                                       reads=[wpk_], writes=[avk])
                            else:
                                OP("sync", lambda e, g=g, av=av: e.dma_start(
                                    out=av, in_=scr_ffa[seq, g * P:(g + 1) * P, :]),
                                   reads=[("scr_ffa", seq, g)], writes=[avk], dma=True)
                                for hh in range(2):
                                    wp_, wpk_ = wd_ps[hh]
                                    OP("vector", lambda e, hh=hh, av=av, wp_=wp_: e.tensor_add(
                                        av[:, hh * FT:(hh + 1) * FT],
                                        av[:, hh * FT:(hh + 1) * FT], wp_),
                                       reads=[wpk_, avk], writes=[avk])
                            OP("gpsimd", lambda e, g=g, av=av: e.dma_start(
                                out=scr_ffa[seq, g * P:(g + 1) * P, :], in_=av),
                               reads=[avk], writes=[("scr_ffa", seq, g)], dma=True)
                        else:
                            OP("sync", lambda e, g=g, av=av: e.dma_start(
                                out=av, in_=scr_ffa[seq, g * P:(g + 1) * P, :]),
                               reads=[("scr_ffa", seq, g)], writes=[avk], dma=True)
                            OP("sync", lambda e, g=g, mv=mv: e.dma_start(
                                out=mv, in_=scr_mix[seq, g * P:(g + 1) * P, :]),
                               reads=[("scr_mix", seq, g)], writes=[mk2], dma=True)
                            OP("sync", lambda e, g=g: e.dma_start(
                                out=x1ws[:], in_=x_in[seq, g * P:(g + 1) * P, :]),
                               writes=["x1ws"], dma=True)
                            for hh in range(2):
                                wp_, wpk_ = wd_ps[hh]
                                OP("vector", lambda e, hh=hh, av=av, wp_=wp_: e.tensor_add(
                                    av[:, hh * FT:(hh + 1) * FT],
                                    av[:, hh * FT:(hh + 1) * FT], wp_),
                                   reads=[wpk_, avk], writes=[avk])
                            OP("vector", lambda e, av=av, mv=mv: e.tensor_add(fows[:], av, mv),
                               reads=[avk, mk2], writes=["fows"])
                            stats_pass("fows", fows[:], stat_t[:, 5:6], "stat",
                                       stat_t[:, 6:7], "stat")
                            OP("vector", lambda e: e.tensor_scalar(
                                fows[:], fows[:], stat_t[:, 5:6], stat_t[:, 6:7],
                                ALU.subtract, ALU.mult),
                               reads=["fows", "stat"], writes=["fows"])
                            OP("vector", lambda e: e.tensor_add(x1ws[:], x1ws[:], fows[:]),
                               reads=["fows", "x1ws"], writes=["x1ws"])
                            stats_pass("x1ws", x1ws[:], MU4c[seq][:, g:g + 1],
                                       ("MU4c", seq, g), R4c[seq][:, g:g + 1], ("R4c", seq, g))
                            OP("gpsimd", lambda e, g=g: e.dma_start(
                                out=scr_x1[seq, g * P:(g + 1) * P, :], in_=x1ws[:]),
                               reads=["x1ws"], writes=[("scr_x1", seq, g)], dma=True)

        def phase_d(seq):
            for g in range(NTILE):
                pt, kb = (pbig, "pbig") if g % 2 == 0 else (pbig2, "pbig2")
                xbv = cma[0][:, 2 * (g % 2):2 * (g % 2) + 2, :].rearrange("p a b -> p (a b)")
                OP("gpsimd", lambda e, g=g, xbv=xbv: e.dma_start(
                    out=xbv, in_=scr_x1[seq, g * P:(g + 1) * P, :]),
                   reads=[("scr_x1", seq, g)], writes=["cma0"], dma=True)
                OP("vector", lambda e, g=g: e.scalar_tensor_tensor(
                    stat_t[:, 7:8], MU4c[seq][:, g:g + 1], -1.0, R4c[seq][:, g:g + 1],
                    ALU.mult, ALU.mult),
                   reads=[("MU4c", seq, g), ("R4c", seq, g)], writes=["stat"])
                col_to_row(stat_t[:, 7:8], "stat", slot=g % 2)
                diag_transpose(xbv, "cma0", R4c[seq][:, g:g + 1], ("R4c", seq, g), True,
                               pt=pt, kb=kb, slot=g % 2)
                for hh in range(2):
                    OP("scalar", lambda e, g=g, hh=hh, pt=pt: e.copy(
                        actT[:, hh * 4:(hh + 1) * 4, g * P:(g + 1) * P],
                        pt[:, hh * FT:(hh + 1) * FT].rearrange("p (c q) -> p c q", q=P)),
                       reads=[(kb, hh)], writes=[("actT", g)])
                for kc in range(DC):
                    OP("tensor", lambda e, kc=kc, g=g: e.matmul(
                        ps2[:, 0:E], actT[:, kc, g * P:(g + 1) * P], gateW_b[:, kc, :],
                        start=(kc == 0), stop=(kc == DC - 1)),
                       reads=["gateW", ("actT", g)], writes=["ps2"])
                OP("scalar", lambda e: e.activation(scb[:], ps2[:, 0:E], AF.Sigmoid),
                   reads=["ps2"], writes=["scb"])
                OP("vector", lambda e: e.tensor_scalar_mul(scb[:], scb[:], rscol[:]),
                   reads=["scb", "rsv"], writes=["scb"])
                bc_v1 = bass.broadcast_tensor_aps(scb[:], stat_t[:, 0:1])
                bc_v2 = bass.broadcast_tensor_aps(scb[:], stat_t[:, 1:2])
                OP("vector", lambda e: e.reduce_max(stat_t[:, 0:1], scb[:], axis=AX.X),
                   reads=["scb"], writes=["stat"])
                OP("vector", lambda e, bc=bc_v1: e.tensor_tensor(scb2[:], bc[0], bc[1], ALU.is_ge),
                   reads=["scb", "stat"], writes=["scb2"])
                OP("vector", lambda e: e.scalar_tensor_tensor(
                    scb2[:], scb2[:], -1e30, scb[:], ALU.mult, ALU.add),
                   reads=["scb", "scb2"], writes=["scb2"])
                OP("vector", lambda e: e.reduce_max(stat_t[:, 1:2], scb2[:], axis=AX.X),
                   reads=["scb2"], writes=["stat"])
                OP("vector", lambda e, bc=bc_v2: e.tensor_tensor(scb2[:], bc[0], bc[1], ALU.is_ge),
                   reads=["scb", "stat"], writes=["scb2"])
                OP("vector", lambda e: e.tensor_mul(scb2[:], scb2[:], scb[:]),
                   reads=["scb", "scb2"], writes=["scb2"])
                OP("vector", lambda e: e.reduce_sum(stat_t[:, 2:3], scb2[:], axis=AX.X),
                   reads=["scb2"], writes=["stat"])
                OP("vector", lambda e: e.reciprocal(stat_t[:, 3:4], stat_t[:, 2:3]),
                   reads=["stat"], writes=["stat"])
                OP("vector", lambda e, g=g: e.tensor_scalar_mul(
                    GF[:, g, :], scb2[:], stat_t[:, 3:4]),
                   reads=["scb2", "stat"], writes=[("GF", g)])
                OP("tensor", lambda e, g=g: e.matmul(
                    ps3[0:E, 0:P], GF[:, g, :], eye_f[:], start=True, stop=True),
                   reads=[("GF", g), "eye_f"], writes=["ps3"])
                OP("vector", lambda e, g=g: e.tensor_copy(GCM[:, g * P:(g + 1) * P],
                                                          ps3[0:E, 0:P]),
                   reads=["ps3"], writes=[("GCM", g)])

            load_w(Ws1V, Ws1.rearrange("(c p) n -> p c n", p=P), Ws1K)
            load_w(Ws2V, Ws2.rearrange("(c p) n -> p c n", p=P), Ws2K)
            load_w(bs1V, bs1_i, bs1K)
            load_w(bs2V, bs2_i, bs2K)
            load_w(b2V, b2_i, b2K)
            for g in range(NTILE):
                gc0 = g * P
                pt, kb = (pbig2, "pbig2") if g % 2 == 0 else (pbig, "pbig")
                psh, pshk = (ps0, "ps0") if g % 2 == 0 else (ps2, "ps2")
                gsv, gsk = gscb[:, g % 2], ("gscb", g % 2)
                av, avk = accws[:, g % 2], ("accws", g % 2)
                for hh in range(2):
                    OP("tensor", lambda e, hh=hh, pt=pt: e.matmul(
                        pt[:, hh * FT:(hh + 1) * FT], ones_b[:],
                        bs2V[:, hh * FT:(hh + 1) * FT], start=True, stop=False),
                       reads=["ones_b"] + bs2K, writes=[(kb, hh)])
                    OP("tensor", lambda e, hh=hh, g=g, gc0=gc0, pt=pt: e.matmul(
                        pt[:, hh * FT:(hh + 1) * FT], GCM[:, gc0:gc0 + P],
                        b2V[:, hh * FT:(hh + 1) * FT], start=False, stop=False),
                       reads=[("GCM", g)] + b2K, writes=[(kb, hh)])
                OP("tensor", lambda e, psh=psh: e.matmul(
                    psh[:], ones_b[:], bs1V[:], start=True, stop=False),
                   reads=["ones_b"] + bs1K, writes=[pshk])
                for kc in range(DC):
                    OP("tensor", lambda e, kc=kc, g=g, gc0=gc0, psh=psh: e.matmul(
                        psh[:], actT[:, kc, gc0:gc0 + P], Ws1V[:, kc, :],
                        start=False, stop=(kc == DC - 1)),
                       reads=Ws1K + [("actT", g)], writes=[pshk])
                OP("scalar", lambda e, psh=psh, gsv=gsv: e.activation(gsv, psh[:], AF.Gelu),
                   reads=[pshk], writes=[gsk])
                for hcc in range(4):
                    psq, psqk = (ps1, "ps1") if hcc % 2 == 0 else (ps3, "ps3")
                    OP("tensor", lambda e, hcc=hcc, gsv=gsv, g=g, psq=psq: e.matmul(
                        psq[:, 0:P], gsv[:, hcc * P:(hcc + 1) * P],
                        eye_b[:], start=True, stop=True),
                       reads=[gsk, "eye_b"], writes=[psqk])
                    OP("vector", lambda e, hcc=hcc, g=g, psq=psq: e.tensor_copy(
                        gcmw[:, 0, hcc], psq[:, 0:P]),
                       reads=[psqk], writes=[("gcmw", hcc)])
                for hcc in range(4):
                    for hh in range(2):
                        OP("tensor", lambda e, hcc=hcc, hh=hh, g=g, pt=pt: e.matmul(
                            pt[:, hh * FT:(hh + 1) * FT], gcmw[:, 0, hcc],
                            Ws2V[:, hcc, hh * FT:(hh + 1) * FT],
                            start=False, stop=(hcc == 3)),
                           reads=[("gcmw", hcc)] + Ws2K, writes=[(kb, hh)])
                for hh in range(2):
                    OP("scalar", lambda e, hh=hh, pt=pt, av=av: e.copy(
                        av[:, hh * FT:(hh + 1) * FT], pt[:, hh * FT:(hh + 1) * FT]),
                       reads=[(kb, hh)], writes=[avk])
                OP("gpsimd", lambda e, g=g, gc0=gc0, av=av: e.dma_start(
                    out=scr_fna[seq, gc0:gc0 + P, :], in_=av),
                   reads=[avk], writes=[("scr_fna", seq, g)], dma=True)

            for dh in range(2):
                for ei in range(4):
                    ex = dh * 4 + ei
                    load_w(W1hV[:, ei], W1[ex].rearrange("(c p) h -> p c h", p=P),
                           wstrips(ei * 4096, 4096))
                    load_w(W2hV[:, ei], W2[ex].rearrange("(c p) n -> p c n", p=P),
                           wstrips(16384 + ei * 4096, 4096))
                for g in range(NTILE):
                    gc0 = g * P
                    pt, kb = (pbig2, "pbig2") if g % 2 == 0 else (pbig, "pbig")
                    av, avk = accws[:, g % 2], ("accws", g % 2)

                    def e_front(ei, g=g, gc0=gc0):
                        ex = dh * 4 + ei
                        pe1, pe1k = (ps0, "ps0") if ei % 2 == 0 else (ps2, "ps2")
                        slu, sluk = sluws[:, 0], "sluws"
                        gsv, gsk = gscb[:, ei % 2], ("gscb", ei % 2)
                        OP("tensor", lambda e, ex=ex, pe1=pe1: e.matmul(
                            pe1[:], ones_b[:], b1_r[:, ex, :], start=True, stop=False),
                           reads=["ones_b", "b1"], writes=[pe1k])
                        for kc in range(DC):
                            OP("tensor", lambda e, kc=kc, ei=ei, g=g, gc0=gc0,
                               pe1=pe1: e.matmul(
                                pe1[:], actT[:, kc, gc0:gc0 + P], W1hV[:, ei, kc, :],
                                start=False, stop=(kc == DC - 1)),
                               reads=wstrips(ei * 4096, 4096) + [("actT", g)], writes=[pe1k])
                        OP("scalar", lambda e, pe1=pe1, slu=slu: e.activation(
                            slu, pe1[:], AF.Gelu),
                           reads=[pe1k], writes=[sluk])
                        OP("vector", lambda e, ex=ex, g=g, slu=slu, gsv=gsv: e.tensor_scalar_mul(
                            gsv, slu, GF[:, g, ex:ex + 1]),
                           reads=[sluk, ("GF", g)], writes=[gsk])

                    def e_back(ei, g=g):
                        gsv, gsk = gscb[:, ei % 2], ("gscb", ei % 2)
                        for hcc in range(4):
                            psq, psqk = (ps1, "ps1") if hcc % 2 == 0 else (ps3, "ps3")
                            OP("tensor", lambda e, hcc=hcc, gsv=gsv, psq=psq: e.matmul(
                                psq[:, 0:P],
                                gsv[:, hcc * P:(hcc + 1) * P], eye_b[:],
                                start=True, stop=True),
                               reads=[gsk, "eye_b"], writes=[psqk])
                            OP("vector", lambda e, hcc=hcc, ei=ei, psq=psq: e.tensor_copy(
                                gcmw[:, 0, hcc], psq[:, 0:P]),
                               reads=[psqk], writes=[("gcmw", hcc)])
                        for hcc in range(4):
                            for hh in range(2):
                                st_f = (ei == 0 and hcc == 0)
                                OP("tensor", lambda e, hcc=hcc, hh=hh, ei=ei, pt=pt,
                                   st=st_f: e.matmul(
                                    pt[:, hh * FT:(hh + 1) * FT], gcmw[:, 0, hcc],
                                    W2hV[:, ei, hcc, hh * FT:(hh + 1) * FT],
                                    start=st, stop=(ei == 3 and hcc == 3)),
                                   reads=[("gcmw", hcc)]
                                       + wstrips(16384 + ei * 4096, 4096),
                                   writes=[(kb, hh)])

                    e_front(0)
                    for ei in range(1, 4):
                        e_front(ei)
                        e_back(ei - 1)
                    e_back(3)
                    OP("sync", lambda e, g=g, gc0=gc0, av=av: e.dma_start(
                        out=av, in_=scr_fna[seq, gc0:gc0 + P, :]),
                       reads=[("scr_fna", seq, g)], writes=[avk], dma=True)
                    for hh in range(2):
                        OP("vector", lambda e, hh=hh, pt=pt, av=av: e.tensor_add(
                            av[:, hh * FT:(hh + 1) * FT], av[:, hh * FT:(hh + 1) * FT],
                            pt[:, hh * FT:(hh + 1) * FT]),
                           reads=[(kb, hh), avk], writes=[avk])
                    if dh == 0:
                        OP("gpsimd", lambda e, g=g, gc0=gc0, av=av: e.dma_start(
                            out=scr_fna[seq, gc0:gc0 + P, :], in_=av),
                           reads=[avk], writes=[("scr_fna", seq, g)], dma=True)
                    else:
                        OP("sync", lambda e, g=g, gc0=gc0: e.dma_start(
                            out=x1ws[:], in_=scr_x1[seq, gc0:gc0 + P, :]),
                           reads=[("scr_x1", seq, g)], writes=["x1ws"], dma=True)
                        stats_pass(avk, av, stat_t[:, 5:6], "stat",
                                   stat_t[:, 6:7], "stat")
                        OP("vector", lambda e, av=av: e.tensor_scalar(
                            av, av, stat_t[:, 5:6], stat_t[:, 6:7],
                            ALU.subtract, ALU.mult),
                           reads=[avk, "stat"], writes=[avk])
                        OP("vector", lambda e, av=av: e.tensor_add(av, av, x1ws[:]),
                           reads=["x1ws", avk], writes=[avk])
                        OP("gpsimd", lambda e, g=g, gc0=gc0, av=av: e.dma_start(
                            out=y_out[seq, gc0:gc0 + P, :], in_=av),
                           reads=[avk], writes=[], dma=True)

        setup()
        if "A" in PHASES:
            load_w(WdV, Wdecay.rearrange("(c p) n -> p c n", p=P), WdK)
            load_w(WgV, Wgate.rearrange("(c p) n -> p c n", p=P), WgK)
            load_w(WkV, Wk.rearrange("(c p) n -> p c n", p=P), WkK)
            for seq in range(LIMIT_NSEQ or NSEQ):
                phase_a(seq)
        if "B" in PHASES:
            load_w(WcV, Wc.rearrange("(c p) n -> p c n", p=P), WcK)
            load_w(WmV, Wmem.rearrange("(c p) n -> p c n", p=P), WmK)
            load_w(bcV, bc_i, bcK)
            for seq in range(LIMIT_NSEQ or NSEQ):
                phase_b(seq)
        if "C" in PHASES:
            for seq in range(LIMIT_NSEQ or NSEQ):
                phase_c(seq)
        if "D" in PHASES:
            OP("gpsimd", lambda e: e.dma_start(out=b1_r, in_=b1_i),
               reads=[], writes=b1K, dma=True)
            for seq in range(LIMIT_NSEQ or NSEQ):
                phase_d(seq)

        prog.finalize()

        with nc.Block() as block:
            @block.sync
            def _(eng):
                prog.emit("sync", eng)

            @block.gpsimd
            def _(eng):
                prog.emit("gpsimd", eng)
                prog.fence(eng)

            @block.tensor
            def _(eng):
                prog.emit("tensor", eng)

            @block.vector
            def _(eng):
                prog.emit("vector", eng)

            @block.scalar
            def _(eng):
                prog.emit("scalar", eng)

    return nc


_nc_cache = None


def kernel(**inputs):
    global _nc_cache
    f32 = np.float32

    def cont(a):
        return np.ascontiguousarray(a, dtype=f32)

    x = cont(inputs["L0_attn_x"])
    conv_w = cont(inputs["L0_attn_conv_w"])
    aux = {
        "Wdecay": cont(inputs["L0_attn_Wdecay"]),
        "Wgate": cont(inputs["L0_attn_Wgate"]),
        "Wk": cont(inputs["L0_attn_Wk"]),
        "MT": cont(inputs["L0_attn_M"].T),
        "Wmem": cont(inputs["L0_attn_Wmem"]),
        "Wc": cont(inputs["L0_attn_Wc"]),
        "Wg": cont(inputs["L0_attn_Wg"]),
        "Wup": cont(inputs["L0_attn_Wup"]),
        "Wdown": cont(inputs["L0_attn_Wdown"]),
        "W1": cont(inputs["L0_ffn_W1"]),
        "W2": cont(inputs["L0_ffn_W2"]),
        "Ws1": cont(inputs["L0_ffn_Ws1"]),
        "Ws2": cont(inputs["L0_ffn_Ws2"]),
        "gateW": cont(inputs["L0_ffn_gate_W"]),
        "n1w_c": cont(inputs["L0_attn_norm1_w"].reshape(DC, P).T),
        "n2w_c": cont(inputs["L0_attn_norm2_w"].reshape(DC, P).T),
        "bd_c": cont(inputs["L0_attn_bdecay"].reshape(DC, P).T),
        "bg_c": cont(inputs["L0_attn_bgate"].reshape(DC, P).T),
        "cw_cols": cont(conv_w.reshape(DC, P, 7).transpose(1, 0, 2).reshape(P, DC * 7)),
        "bc_row": cont(inputs["L0_attn_bc"].reshape(1, D)),
        "bs1_row": cont(inputs["L0_ffn_bs1"].reshape(1, H)),
        "bs2_row": cont(inputs["L0_ffn_bs2"].reshape(1, D)),
        "b1_rows": cont(inputs["L0_ffn_b1"].reshape(1, E, H)),
        "b2_pm": cont(inputs["L0_ffn_b2"]),
        "ss": cont(np.reshape(inputs["L0_attn_state_scale"], (1, 1))),
        "ms": cont(np.reshape(inputs["L0_attn_mem_scale"], (1, 1))),
        "rs": cont(np.reshape(inputs["L0_ffn_route_scale"], (1, 1))),
    }
    if _nc_cache is None:
        _nc_cache = build()
    nc = _nc_cache
    in_maps = []
    for c in range(8):
        m = dict(aux)
        m["x"] = x[c * NSEQ:(c + 1) * NSEQ]
        in_maps.append(m)
    res = run_bass_kernel_spmd(nc, in_maps, core_ids=list(range(8)))
    return np.concatenate([r["y"] for r in res.results], axis=0)


# version 13
# speedup vs baseline: 1.2508x; 1.0117x over previous
import numpy as np
from contextlib import ExitStack

import concourse.bass as bass
import concourse.mybir as mybir
from concourse.bass_utils import run_bass_kernel_spmd

F32 = mybir.dt.float32
BF16 = mybir.dt.bfloat16
AF = mybir.ActivationFunctionType
ALU = mybir.AluOpType
AX = mybir.AxisListType

P = 128
T = 2048
NSEQ = 2
D = 1024
DC = D // P
FT = 512
NF = T // FT
NTILE = T // P
TPF = FT // P
DM = 256
DF = 4096
E = 8
H = 512
LNC = 16.11809565095832

PHASES = ("A", "B", "C", "D")
DEBUG_SCRATCH = False
LIMIT_NF = None
LIMIT_NSEQ = None
A_PARTS = ("tiles", "stats", "c2r", "diag", "evac", "kmem", "conv", "decay")


class Rec:
    __slots__ = ("eng", "fn", "reads", "writes", "dma", "flag", "semval", "idx", "deps",
                 "slot", "use")

    def __init__(self, eng, fn, reads, writes, dma):
        self.eng, self.fn = eng, fn
        self.reads, self.writes = list(reads), list(writes)
        self.dma = dma
        self.flag = dma
        self.semval = None
        self.deps = []
        self.slot = self.use = 0


NSLOT = 8


class Prog:
    ENGINES = ("sync", "gpsimd", "tensor", "vector", "scalar")

    def __init__(self, nc, es):
        self.nc = nc
        self.ops = []
        self.last_w = {}
        self.readers = {}
        self.c_sem = {e: es.enter_context(nc.semaphore(f"c_{e}")) for e in self.ENGINES}
        self.d_sem = {e: [es.enter_context(nc.semaphore(f"d_{e}{i}")) for i in range(NSLOT)]
                      for e in ("sync", "gpsimd", "scalar")}
        self.dma_count = {e: 0 for e in ("sync", "gpsimd", "scalar")}

    PSUMNAMES = {"pbig", "pbig2", "ps0", "ps1", "ps2", "ps3"}

    def op(self, eng, fn, reads=(), writes=(), dma=False):
        o = Rec(eng, fn, reads, writes, dma)
        o.idx = len(self.ops)
        if dma:
            k = self.dma_count[eng]
            o.slot, o.use = k % NSLOT, k // NSLOT
            o.semval = 16 * (o.use + 1)
            self.dma_count[eng] = k + 1
        deps = {}
        for k in o.reads:
            d = self.last_w.get(k)
            if d is not None:
                deps[d.idx] = d
        for k in o.writes:
            for d in self.readers.get(k, ()):
                deps[d.idx] = d
            d = self.last_w.get(k)
            if d is not None:
                deps[d.idx] = d
        if not dma and eng in ("scalar", "vector"):
            if not hasattr(self, "_last_psum_by_key"):
                self._last_psum_by_key = {}
            for k in list(o.reads) + list(o.writes):
                base = k if isinstance(k, str) else k[0]
                if base in self.PSUMNAMES:
                    if base in ("pbig", "pbig2"):
                        banks = [(base, k[1])] if isinstance(k, tuple) else \
                            [(base, 0), (base, 1)]
                    else:
                        banks = [base]
                    for b in banks:
                        prev = self._last_psum_by_key.get(b)
                        if prev is not None and prev.eng != eng:
                            deps[prev.idx] = prev
                        self._last_psum_by_key[b] = o
        o.deps = list(deps.values())
        for k in o.writes:
            self.last_w[k] = o
            self.readers[k] = []
        for k in o.reads:
            self.readers.setdefault(k, []).append(o)
        self.ops.append(o)
        return o

    def finalize(self):
        for o in self.ops:
            for d in o.deps:
                if d.eng == o.eng == "tensor" and not d.dma:
                    continue
                d.flag = True
        cnt = {}
        for o in self.ops:
            if o.flag and not o.dma:
                cnt[o.eng] = cnt.get(o.eng, 0) + 1
                o.semval = cnt[o.eng]

    def emit(self, eng_name, eng):
        watermark = {}

        def wait(sem, v):
            if watermark.get(id(sem), -1) < v:
                eng.wait_ge(sem, v)
                watermark[id(sem)] = v

        for o in self.ops:
            if o.eng != eng_name:
                continue
            waits = {}
            for d in o.deps:
                if d.eng == o.eng and not d.dma and (o.eng == "tensor" or not d.flag):
                    continue
                sem = self.d_sem[d.eng][d.slot] if d.dma else self.c_sem[d.eng]
                cur = waits.get(id(sem))
                if cur is None or d.semval > cur[1]:
                    waits[id(sem)] = (sem, d.semval)
            for sem, v in waits.values():
                wait(sem, v)
            if o.dma and o.use > 0:
                wait(self.d_sem[o.eng][o.slot], 16 * o.use)
            inst = o.fn(eng)
            if o.flag:
                if o.dma:
                    inst.then_inc(self.d_sem[o.eng][o.slot], 16)
                else:
                    inst.then_inc(self.c_sem[o.eng], 1)

    def fence(self, eng):
        for e2 in ("sync", "gpsimd", "scalar"):
            n = self.dma_count[e2]
            for i in range(NSLOT):
                uses = (n - i + NSLOT - 1) // NSLOT if n > i else 0
                if uses > 0:
                    eng.wait_ge(self.d_sem[e2][i], 16 * uses)


def build():
    nc = bass.Bass("TRN2", target_bir_lowering=False, debug=False)

    def din(name, shape, dt=F32):
        return nc.dram_tensor(name, list(shape), dt, kind="ExternalInput").ap()

    x_in = din("x", [NSEQ, T, D])
    y_out = nc.dram_tensor("y", [NSEQ, T, D], F32, kind="ExternalOutput").ap()

    Wdecay = din("Wdecay", [D, D]); Wgate = din("Wgate", [D, D])
    Wk = din("Wk", [D, DM]); MT = din("MT", [DM, DM]); Wmem = din("Wmem", [DM, D])
    Wc = din("Wc", [3 * D, D])
    Wg = din("Wg", [D, DF]); Wup = din("Wup", [D, DF]); Wdown = din("Wdown", [DF, D])
    W1 = din("W1", [E, D, H]); W2 = din("W2", [E, H, D])
    Ws1 = din("Ws1", [D, H]); Ws2 = din("Ws2", [H, D])
    gateW = din("gateW", [D, E])
    n1w_i = din("n1w_c", [P, DC]); n2w_i = din("n2w_c", [P, DC])
    bd_i = din("bd_c", [P, DC]); bg_i = din("bg_c", [P, DC])
    cw_i = din("cw_cols", [P, DC * 7])
    bc_i = din("bc_row", [1, D]); bs1_i = din("bs1_row", [1, H]); bs2_i = din("bs2_row", [1, D])
    b1_i = din("b1_rows", [1, E, H]); b2_i = din("b2_pm", [E, D])
    ss_i = din("ss", [1, 1]); ms_i = din("ms", [1, 1]); rs_i = din("rs", [1, 1])

    skind = "ExternalOutput" if DEBUG_SCRATCH else "Internal"
    scr_loc = nc.dram_tensor("scr_loc", [NSEQ, P, DC, T], BF16, kind=skind).ap()
    scr_st = nc.dram_tensor("scr_st", [NSEQ, P, DC, T], BF16, kind=skind).ap()
    scr_mk = nc.dram_tensor("scr_mk", [NSEQ, P, 2, T], BF16, kind=skind).ap()
    scr_mix = nc.dram_tensor("scr_mix", [NSEQ, T, D], F32, kind=skind).ap()
    scr_x1 = nc.dram_tensor("scr_x1", [NSEQ, T, D], F32, kind=skind).ap()
    scr_ffa = nc.dram_tensor("scr_ffa", [NSEQ, T, D], F32, kind=skind).ap()
    scr_fna = nc.dram_tensor("scr_fna", [NSEQ, T, D], F32, kind=skind).ap()
    scr_dbg = nc.dram_tensor("scr_dbg", [P, DC, T], BF16, kind=skind).ap()
    scr_dbg2 = nc.dram_tensor("scr_dbg2", [P, DC, FT], F32, kind=skind).ap()
    scr_dbg3 = nc.dram_tensor("scr_dbg3", [P, 4, NTILE], F32, kind=skind).ap()

    eye_i = nc.inline_tensor(np.eye(P, dtype=np.float32), "eye_i").ap()

    es = ExitStack()
    with es:
        def sb(name, shape, dt=F32):
            return es.enter_context(nc.sbuf_tensor(name, list(shape), dt))

        def psum(name, shape):
            return es.enter_context(nc.psum_tensor(name, list(shape), F32))

        eye_f = sb("eye_f", [P, P]); eye_b = sb("eye_b", [P, P], BF16)
        ones_f = sb("ones_f", [1, P]); ones_b = sb("ones_b", [1, P], BF16)
        zerosF = sb("zerosF", [P, FT])
        consts = sb("consts", [P, 8])
        n1w = sb("n1w", [P, DC]); n2w = sb("n2w", [P, DC])
        bd_c = sb("bdcol", [P, DC]); bg_c = sb("bgcol", [P, DC])
        cw_c = sb("cwcol", [P, DC * 7])
        diags = sb("diags", [P, DC * 7 * P], BF16)
        b1_r = diags[0:1, 0:E * H].rearrange("o (e2 h2) -> o e2 h2", e2=E)
        b1K = ["b1"] + [("diag", i) for i in range(32)]
        MT_b = sb("MT_b", [P, 2, DM], BF16)
        gateW_b = sb("gateW_b", [P, DC, E], BF16)
        sscol = sb("sscol", [P, 1]); mscol = sb("mscol", [P, 1]); rscol = sb("rscol", [P, 1])
        nss = sb("nss", [P, DC])
        carA = sb("carA", [P, DC]); carC = sb("carC", [P, DC])

        MUc = [sb(f"MUc{s}", [P, NTILE]) for s in range(NSEQ)]
        Rc = [sb(f"Rc{s}", [P, NTILE]) for s in range(NSEQ)]
        Sc = [sb(f"Sc{s}", [P, NTILE]) for s in range(NSEQ)]
        NMSc = [sb(f"NMSc{s}", [P, NTILE]) for s in range(NSEQ)]
        R2c = [sb(f"R2c{s}", [P, NTILE]) for s in range(NSEQ)]
        MU4c = [sb(f"MU4c{s}", [P, NTILE]) for s in range(NSEQ)]
        R4c = [sb(f"R4c{s}", [P, NTILE]) for s in range(NSEQ)]
        GF = sb("GFt", [P, NTILE, E])
        GCM = sb("GCMt", [E, T], BF16)
        stat_t = sb("stat_t", [P, 8])
        rowb = sb("rowb", [1, 2 * P], BF16)

        actT = sb("actT", [P, DC, T], BF16)
        wpool = sb("wpool", [P, 32768], BF16)

        g32a = sb("g32a", [P, DC, FT])
        ldc = sb("ldc", [P, FT]); Acb = sb("Acb", [P, FT]); cdc = sb("cdc", [P, FT])
        ecb = sb("ecb", [P, FT]); ucb = sb("ucb", [P, FT]); Ccb = sb("Ccb", [P, FT])
        sluws = sb("sluws", [P, 1, FT])
        cma = [sb(f"cma{i}", [P, DC, FT], BF16) for i in range(2)]
        kbf = cma[1]
        cmb0 = sb("cmb0", [P, 2, FT], BF16)
        xws = sb("xws", [P, 2, D])
        sqh = sb("sqh", [P, FT])
        diagws = sb("diagws", [P, 1, P], BF16)
        mixws = sb("mixws", [P, 2, D])
        accws = sb("accws", [P, 2, D])
        fows = sb("fows", [P, D])
        x1ws = sb("x1ws", [P, D])
        gscb = sb("gscb", [P, 2, H], BF16)
        gcmw = sb("gcmw", [P, 1, 4, P], BF16)
        scb = sb("scb", [P, E])
        scb2 = sb("scb2", [P, E])

        pbig = psum("pbig", [P, 1024])
        pbig2 = psum("pbig2", [P, 1024])
        ps0 = psum("ps0", [P, FT]); ps1 = psum("ps1", [P, FT]); ps2 = psum("ps2", [P, FT])
        ps3 = psum("ps3", [P, FT])
        prow = ps3

        cvals = [1e-5, 1e-6, -0.5, -1.0]
        for i, cv in enumerate(cvals):
            nc.const_aps.register(F32, cv, consts[:, i:i + 1])

        prog = Prog(nc, es)
        OP = prog.op

        def wstrips(off, n):
            return [f"wp{j}" for j in range(off // 4096, (off + n - 1) // 4096 + 1)]

        def wview(off, shape):
            n = int(np.prod(shape))
            ap = wpool[:, off:off + n]
            if len(shape) > 1:
                pat = "p (" + " ".join(f"d{i}" for i in range(len(shape))) + ") -> p " + \
                      " ".join(f"d{i}" for i in range(len(shape)))
                ap = ap.rearrange(pat, **{f"d{i}": s for i, s in enumerate(shape)})
            return ap, wstrips(off, n)

        WdV, WdK = wview(0, [DC, D])
        WgV, WgK = wview(8192, [DC, D])
        WkV, WkK = wview(16384, [DC, DM])
        WcV, WcK = wview(0, [24, D])
        WmV, WmK = wview(24576, [2, D])
        bcV = wpool[0:1, 26624:27648]; bcK = wstrips(26624, 1024)
        Wgq2 = [wview(0, [DC, 1024]), wview(24576, [DC, 1024])]
        WuqV, WuqK = wview(8192, [DC, 1024])
        WdqV, WdqK = wview(16384, [DC, 1024])
        Ws1V, Ws1K = wview(0, [DC, H])
        Ws2V, Ws2K = wview(4096, [4, D])
        bs1V = wpool[0:1, 8192:8704]; bs1K = wstrips(8192, 512)
        bs2V = wpool[0:1, 8704:9728]; bs2K = wstrips(8704, 1024)
        b2V = wpool[0:8, 9728:10752]; b2K = wstrips(9728, 1024)
        W1hV, W1hK = wview(0, [4, DC, H])
        W2hV, W2hK = wview(16384, [4, 4, D])

        membf = actT[:, :, 0:FT]
        membfK = [("actT", q) for q in range(TPF)]

        def load_w(dst, src_ap, keys, eng="gpsimd"):
            OP(eng, lambda e, d=dst, s=src_ap: e.dma_start(out=d, in_=s),
               reads=[], writes=keys, dma=True)

        def setup():
            def ld(dst, src, keys, eng="sync"):
                OP(eng, lambda e, d=dst, s=src: e.dma_start(out=d, in_=s),
                   reads=[], writes=keys, dma=True)

            ld(eye_f[:], eye_i, ["eye_f"])
            ld(n1w[:], n1w_i, ["n1w"]); ld(n2w[:], n2w_i, ["n2w"])
            ld(bd_c[:], bd_i, ["bd"]); ld(bg_c[:], bg_i, ["bg"])
            ld(cw_c[:], cw_i, ["cw"])
            ld(MT_b[:], MT.rearrange("(c p) n -> p c n", p=P), ["MT"], eng="gpsimd")
            ld(gateW_b[:], gateW.rearrange("(c p) n -> p c n", p=P), ["gateW"], eng="gpsimd")
            ld(sscol[0:1, :], ss_i, ["ssv"]); ld(mscol[0:1, :], ms_i, ["msv"])
            ld(rscol[0:1, :], rs_i, ["rsv"])

            OP("vector", lambda e: e.memset(ones_f[:], 1.0), writes=["ones_f"])
            OP("vector", lambda e: e.memset(ones_b[:], 1.0), writes=["ones_b"])
            OP("vector", lambda e: e.memset(zerosF[:], 0.0), writes=["zerosF"])
            for i, cv in enumerate(cvals):
                OP("vector", lambda e, i=i, cv=cv: e.memset(consts[:, i:i + 1], cv),
                   writes=["consts"])
            OP("vector", lambda e: e.tensor_copy(eye_b[:], eye_f[:]),
               reads=["eye_f"], writes=["eye_b"])
            for name, col in (("ssv", sscol), ("msv", mscol), ("rsv", rscol)):
                OP("tensor", lambda e, c=col: e.matmul(ps2[:, 0:1], ones_f[:], c[0:1, :],
                                                       start=True, stop=True),
                   reads=[name, "ones_f"], writes=["ps2"])
                OP("vector", lambda e, c=col: e.tensor_copy(c[:], ps2[:, 0:1]),
                   reads=["ps2"], writes=[name])
            OP("vector", lambda e: e.tensor_scalar_mul(nss[:], n1w[:], sscol[:]),
               reads=["n1w", "ssv"], writes=["nss"])
            for c in range(DC):
                for j in range(7):
                    idx = c * 7 + j
                    OP("vector", lambda e, idx=idx: e.tensor_scalar_mul(
                        diags[:, idx * P:(idx + 1) * P], eye_f[:], cw_c[:, idx:idx + 1]),
                       reads=["eye_f", "cw"], writes=[("diag", idx)])

        def stats_pass(src_key, src_ap, mu_ap, mu_key, r_ap, r_key):
            OP("vector", lambda e: e.reduce_sum(stat_t[:, 0:1], src_ap, axis=AX.X),
               reads=[src_key], writes=["stat"])
            for hh in range(2):
                OP("scalar", lambda e, hh=hh: e.activation(
                    sqh[:], src_ap[:, hh * FT:(hh + 1) * FT], AF.Square),
                   reads=[src_key], writes=["sqh"])
                OP("vector", lambda e, hh=hh: e.reduce_sum(stat_t[:, 1 + hh:2 + hh], sqh[:],
                                                           axis=AX.X),
                   reads=["sqh"], writes=["stat"])
            OP("vector", lambda e: e.tensor_add(stat_t[:, 1:2], stat_t[:, 1:2], stat_t[:, 2:3]),
               reads=["stat"], writes=["stat"])
            OP("vector", lambda e: e.tensor_scalar_mul(mu_ap, stat_t[:, 0:1], 1.0 / D),
               reads=["stat"], writes=[mu_key])
            OP("vector", lambda e: e.tensor_tensor(stat_t[:, 3:4], mu_ap, mu_ap, ALU.mult),
               reads=[mu_key], writes=["stat"])
            OP("vector", lambda e: e.tensor_scalar_mul(stat_t[:, 2:3], stat_t[:, 1:2], 1.0 / D),
               reads=["stat"], writes=["stat"])
            OP("vector", lambda e: e.tensor_tensor(stat_t[:, 4:5], stat_t[:, 2:3],
                                                   stat_t[:, 3:4], ALU.subtract),
               reads=["stat"], writes=["stat"])
            OP("scalar", lambda e: e.activation(stat_t[:, 3:4], stat_t[:, 4:5], AF.Ln, bias=1e-5),
               reads=["stat"], writes=["stat"])
            OP("scalar", lambda e: e.activation(r_ap, stat_t[:, 3:4], AF.Exp, scale=-0.5),
               reads=["stat"], writes=[r_key])

        def col_to_row(col_ap, col_key, slot=0):
            OP("tensor", lambda e: e.matmul(ps3[0:1, 0:P], col_ap, eye_f[:], start=True, stop=True),
               reads=[col_key, "eye_f"], writes=["ps3"])
            OP("scalar", lambda e, slot=slot: e.copy(rowb[:, slot * P:(slot + 1) * P],
                                                     ps3[0:1, 0:P]),
               reads=["ps3"], writes=[("rowb", slot)])

        def diag_transpose(src_ap, src_key, scol_ap, scol_key, with_row,
                           pt=None, kb="pbig", slot=0):
            pt = pbig if pt is None else pt
            dw = diagws[:, 0]
            dwk = "diagws"
            OP("vector", lambda e, dw=dw: e.tensor_scalar_mul(dw, eye_b[:], scol_ap),
               reads=["eye_b", scol_key], writes=[dwk])
            for c in range(DC):
                OP("tensor", lambda e, c=c, pt=pt, dw=dw: e.matmul(
                    pt[:, c * P:(c + 1) * P], src_ap[:, c * P:(c + 1) * P], dw,
                    start=True, stop=not with_row),
                   reads=[src_key, dwk], writes=[(kb, c // 4)])
                if with_row:
                    OP("tensor", lambda e, c=c, pt=pt, slot=slot: e.matmul(
                        pt[:, c * P:(c + 1) * P], ones_b[:], rowb[:, slot * P:(slot + 1) * P],
                        start=False, stop=True),
                       reads=["ones_b", ("rowb", slot)], writes=[(kb, c // 4)])

        def phase_a(seq):
            OP("vector", lambda e: e.memset(carA[:], 0.0),
               writes=[("carA", c2) for c2 in range(DC)])
            OP("vector", lambda e: e.memset(carC[:], 0.0),
               writes=[("carC", c2) for c2 in range(DC)])
            for fi in range(LIMIT_NF or NF):
                t0 = fi * FT
                for tt in range(TPF):
                    g = fi * TPF + tt
                    xv = xws[:, g % 2]
                    xk = ("xws", g % 2)
                    xbv = cma[0][:, 2 * (g % 2):2 * (g % 2) + 2, :].rearrange(
                        "p a b -> p (a b)")
                    OP("sync", lambda e, g=g, xv=xv: e.dma_start(
                        out=xv, in_=x_in[seq, g * P:(g + 1) * P, :]),
                       reads=[], writes=[xk], dma=True)
                    OP("gpsimd", lambda e, g=g, xbv=xbv: e.dma_start(
                        out=xbv, in_=x_in[seq, g * P:(g + 1) * P, :]),
                       reads=[], writes=["cma0"], dma=True)
                    if "stats" not in A_PARTS:
                        continue
                    stats_pass(xk, xv, MUc[seq][:, g:g + 1], ("MUc", seq, g),
                               Rc[seq][:, g:g + 1], ("Rc", seq, g))
                    OP("vector", lambda e, g=g: e.tensor_scalar(
                        stat_t[:, 6:7], stat_t[:, 4:5], Rc[seq][:, g:g + 1],
                        Rc[seq][:, g:g + 1], ALU.mult, ALU.mult),
                       reads=["stat", ("Rc", seq, g)], writes=["stat"])
                    OP("scalar", lambda e: e.activation(stat_t[:, 6:7], stat_t[:, 6:7],
                                                        AF.Ln, bias=1e-6),
                       reads=["stat"], writes=["stat"])
                    OP("scalar", lambda e: e.activation(stat_t[:, 7:8], stat_t[:, 6:7],
                                                        AF.Exp, scale=-0.5),
                       reads=["stat"], writes=["stat"])
                    OP("vector", lambda e, g=g: e.tensor_tensor(
                        Sc[seq][:, g:g + 1], Rc[seq][:, g:g + 1], stat_t[:, 7:8], ALU.mult),
                       reads=["stat", ("Rc", seq, g)], writes=[("Sc", seq, g)])
                    OP("vector", lambda e, g=g: e.scalar_tensor_tensor(
                        NMSc[seq][:, g:g + 1], MUc[seq][:, g:g + 1], -1.0,
                        Sc[seq][:, g:g + 1], ALU.mult, ALU.mult),
                       reads=[("MUc", seq, g), ("Sc", seq, g)], writes=[("NMSc", seq, g)])
                    if "c2r" not in A_PARTS:
                        continue
                    col_to_row(NMSc[seq][:, g:g + 1], ("NMSc", seq, g), slot=g % 2)
                    if "diag" not in A_PARTS:
                        continue
                    diag_transpose(xbv, "cma0", Sc[seq][:, g:g + 1], ("Sc", seq, g), True,
                                   slot=g % 2)
                    if "evac" not in A_PARTS:
                        continue
                    for hh in range(2):
                        OP("scalar", lambda e, tt=tt, hh=hh: e.copy(
                            g32a[:, hh * 4:(hh + 1) * 4, tt * P:(tt + 1) * P],
                            pbig[:, hh * FT:(hh + 1) * FT].rearrange("p (c q) -> p c q", q=P)),
                           reads=[("pbig", hh)], writes=["g32a"])
                    for c in range(DC):
                        OP("vector", lambda e, c=c, g=g: e.tensor_scalar_mul(
                            actT[:, c, g * P:(g + 1) * P], pbig[:, c * P:(c + 1) * P],
                            n1w[:, c:c + 1]),
                           reads=[("pbig", c // 4), "n1w"], writes=[("actT", g)])
                hbT_keys = [("actT", fi * TPF + q) for q in range(TPF)]
                hbT_prev = [("actT", fi * TPF - 1)] if fi > 0 else []

                for koc in range(2 if "kmem" in A_PARTS else 0):
                    for kc in range(DC):
                        OP("tensor", lambda e, koc=koc, kc=kc, fi=fi: e.matmul(
                            ps2[:], WkV[:, kc, koc * P:(koc + 1) * P],
                            actT[:, kc, fi * FT:(fi + 1) * FT],
                            start=(kc == 0), stop=(kc == DC - 1)),
                           reads=WkK + hbT_keys, writes=["ps2"])
                    OP("vector", lambda e, koc=koc: e.tensor_copy(kbf[:, koc], ps2[:]),
                       reads=["ps2"], writes=["cma1"])
                for ioc in range(2 if "kmem" in A_PARTS else 0):
                    for jc in range(2):
                        OP("tensor", lambda e, ioc=ioc, jc=jc: e.matmul(
                            ps2[:], MT_b[:, jc, ioc * P:(ioc + 1) * P], kbf[:, jc],
                            start=(jc == 0), stop=(jc == 1)),
                           reads=["MT", "cma1"], writes=["ps2"])
                    OP("vector", lambda e, ioc=ioc: e.tensor_scalar_mul(
                        cmb0[:, ioc], ps2[:], mscol[:]),
                       reads=["ps2", "msv"], writes=["cmb0"])
                if "kmem" in A_PARTS:
                    OP("gpsimd", lambda e, t0=t0: e.dma_start(
                        out=scr_mk[seq, :, :, t0:t0 + FT], in_=cmb0[:]),
                       reads=["cmb0"], writes=[("scr_mk", seq, t0)], dma=True)

                for c in range(DC if "conv" in A_PARTS else 0):
                    pb, pk = (ps0, ["ps0"]) if c % 2 == 0 else \
                        (ps1, [("ps1", i) for i in range(4)])
                    for s in range(7):
                        idx = c * 7 + (6 - s)
                        if fi == 0 and s > 0:
                            OP("tensor", lambda e, c=c, s=s, idx=idx, pb=pb: e.matmul(
                                pb[:, s:FT], diags[:, idx * P:(idx + 1) * P],
                                actT[:, c, 0:FT - s], start=False, stop=(s == 6)),
                               reads=[("diag", idx)] + hbT_keys, writes=pk)
                        else:
                            OP("tensor", lambda e, c=c, s=s, idx=idx, pb=pb, fi=fi: e.matmul(
                                pb[:], diags[:, idx * P:(idx + 1) * P],
                                actT[:, c, fi * FT - s:(fi + 1) * FT - s],
                                start=(s == 0), stop=(s == 6)),
                               reads=[("diag", idx)] + hbT_keys + hbT_prev, writes=pk)
                    OP("scalar", lambda e, c=c, pb=pb: e.copy(cma[0][:, c, :], pb[:]),
                       reads=pk, writes=["cma0"])
                if "conv" in A_PARTS:
                    OP("gpsimd", lambda e, t0=t0: e.dma_start(
                        out=scr_loc[seq, :, :, t0:t0 + FT], in_=cma[0][:]),
                       reads=["cma0"], writes=[("scr_loc", seq, t0)], dma=True)

                def dk_bufs(c):
                    if c % 2 == 0:
                        return ((ldc[:], Acb[:], cdc[:], ecb[:], ucb[:], Ccb[:]),
                                ("ldc_a", "ldc_a", "cde_a", "cde_a", "uc_a", "uc_a"))
                    return ((mixws[:, 0, 0:FT], mixws[:, 0, FT:2 * FT],
                             mixws[:, 1, 0:FT], mixws[:, 1, FT:2 * FT],
                             accws[:, 0, 0:FT], accws[:, 0, FT:2 * FT]),
                            (("mixws", 0), ("mixws", 0), ("mixws", 1), ("mixws", 1),
                             ("accws", 0), ("accws", 0)))

                def decay_mm_sig(c, fi):
                    (bldc, bAcb, bcdc, becb, bucb, bCcb), \
                        (kldc, kAcb, kcdc, kecb, kucb, kCcb) = dk_bufs(c)
                    pzd, pzdk = (ps0, "ps0") if c % 2 == 0 else (ps2, "ps2")
                    pzg, pzgk = (ps1[:], [("ps1", i) for i in range(4)]) \
                        if c % 2 == 0 else (pbig2[:, 0:FT], [("pbig2", 0)])
                    for kc in range(DC):
                        OP("tensor", lambda e, c=c, kc=kc, fi=fi, pzd=pzd: e.matmul(
                            pzd[:], WdV[:, kc, c * P:(c + 1) * P],
                            actT[:, kc, fi * FT:(fi + 1) * FT],
                            start=(kc == 0), stop=(kc == DC - 1)),
                           reads=WdK + hbT_keys, writes=[pzdk])
                    for kc in range(DC):
                        OP("tensor", lambda e, c=c, kc=kc, fi=fi, pzg=pzg: e.matmul(
                            pzg, WgV[:, kc, c * P:(c + 1) * P],
                            actT[:, kc, fi * FT:(fi + 1) * FT],
                            start=(kc == 0), stop=(kc == DC - 1)),
                           reads=WgK + hbT_keys, writes=pzgk)
                    OP("scalar", lambda e, c=c, pzd=pzd, bldc=bldc: e.activation(
                        bldc, pzd[:], AF.Sigmoid, bias=bd_c[:, c:c + 1]),
                       reads=[pzdk, "bd"], writes=[kldc])
                    OP("scalar", lambda e, c=c, pzg=pzg, bucb=bucb: e.activation(
                        bucb, pzg, AF.Sigmoid, bias=bg_c[:, c:c + 1]),
                       reads=pzgk + ["bg"], writes=[kucb])

                def decay_rest(c):
                    (bldc, bAcb, bcdc, becb, bucb, bCcb), \
                        (kldc, kAcb, kcdc, kecb, kucb, kCcb) = dk_bufs(c)
                    OP("vector", lambda e, bldc=bldc: e.tensor_scalar(
                        bldc, bldc, 0.95, 0.05, ALU.min, ALU.max),
                       reads=[kldc], writes=[kldc])
                    OP("scalar", lambda e, bldc=bldc: e.activation(bldc, bldc, AF.Ln),
                       reads=[kldc], writes=[kldc])
                    OP("vector", lambda e, c=c, bldc=bldc, bAcb=bAcb: e.tensor_tensor_scan(
                        bAcb, bldc, zerosF[:], carA[:, c:c + 1], ALU.add, ALU.add),
                       reads=[kldc, "zerosF", ("carA", c)], writes=[kAcb])
                    OP("vector", lambda e, c=c, bldc=bldc: e.tensor_scalar(
                        bldc[:, 0:1], carA[:, c:c + 1], -1.0, LNC, ALU.mult, ALU.min),
                       reads=[("carA", c)], writes=[kldc])
                    OP("vector", lambda e, bldc=bldc, bAcb=bAcb: e.tensor_scalar(
                        bldc[:, 1:FT], bAcb[:, 0:FT - 1], -1.0, LNC, ALU.mult, ALU.min),
                       reads=[kAcb], writes=[kldc])
                    OP("vector", lambda e, c=c, bAcb=bAcb: e.tensor_copy(
                        carA[:, c:c + 1], bAcb[:, FT - 1:FT]),
                       reads=[kAcb], writes=[("carA", c)])
                    OP("scalar", lambda e, bcdc=bcdc, bAcb=bAcb: e.activation(
                        bcdc, bAcb, AF.Exp),
                       reads=[kAcb], writes=[kcdc])
                    OP("scalar", lambda e, becb=becb, bldc=bldc: e.activation(
                        becb, bldc, AF.Exp),
                       reads=[kldc], writes=[kecb])
                    OP("vector", lambda e, c=c, bucb=bucb: e.tensor_mul(
                        bucb, bucb, g32a[:, c, :]),
                       reads=[kucb, "g32a"], writes=[kucb])
                    OP("vector", lambda e, bldc=bldc, bucb=bucb, becb=becb: e.tensor_mul(
                        bldc, bucb, becb),
                       reads=[kucb, kecb], writes=[kldc])
                    OP("vector", lambda e, c=c, bCcb=bCcb, bldc=bldc: e.tensor_tensor_scan(
                        bCcb, bldc, zerosF[:], carC[:, c:c + 1], ALU.add, ALU.add),
                       reads=[kldc, "zerosF", ("carC", c)], writes=[kCcb])
                    OP("vector", lambda e, c=c, bCcb=bCcb: e.tensor_copy(
                        carC[:, c:c + 1], bCcb[:, FT - 1:FT]),
                       reads=[kCcb], writes=[("carC", c)])
                    OP("vector", lambda e, bucb=bucb, bCcb=bCcb, bcdc=bcdc: e.tensor_mul(
                        bucb, bCcb, bcdc),
                       reads=[kCcb, kcdc], writes=[kucb])
                    OP("vector", lambda e, c=c, bucb=bucb: e.tensor_scalar_mul(
                        cma[1][:, c, :], bucb, nss[:, c:c + 1]),
                       reads=[kucb, "nss"], writes=["cma1"])

                if "decay" in A_PARTS:
                    for cp in range(DC // 2):
                        decay_mm_sig(2 * cp, fi)
                        decay_mm_sig(2 * cp + 1, fi)
                        decay_rest(2 * cp)
                        decay_rest(2 * cp + 1)
                if "decay" in A_PARTS:
                    OP("gpsimd", lambda e, t0=t0: e.dma_start(
                        out=scr_st[seq, :, :, t0:t0 + FT], in_=cma[1][:]),
                       reads=["cma1"], writes=[("scr_st", seq, t0)], dma=True)
            if DEBUG_SCRATCH and seq == 0:
                OP("gpsimd", lambda e: e.dma_start(out=scr_dbg, in_=actT[:]),
                   reads=[("actT", q) for q in range(NTILE)], writes=[], dma=True)
                OP("gpsimd", lambda e: e.dma_start(out=scr_dbg2, in_=g32a[:]),
                   reads=["g32a"], writes=[], dma=True)
                for i3, (tn, tk) in enumerate([(MUc[0], "MUc"), (Rc[0], "Rc"),
                                               (Sc[0], "Sc"), (NMSc[0], "NMSc")]):
                    OP("gpsimd", lambda e, i3=i3, tn=tn: e.dma_start(
                        out=scr_dbg3[:, i3, :], in_=tn[:]),
                       reads=[(tk, 0, q) for q in range(NTILE)], writes=[], dma=True)

        def phase_b(seq):
            for fi in range(LIMIT_NF or NF):
                t0 = fi * FT
                OP("sync", lambda e, t0=t0: e.dma_start(
                    out=cma[0][:], in_=scr_loc[seq, :, :, t0:t0 + FT]),
                   reads=[("scr_loc", seq, t0)], writes=["cma0"], dma=True)
                OP("sync", lambda e, t0=t0: e.dma_start(
                    out=cma[1][:], in_=scr_st[seq, :, :, t0:t0 + FT]),
                   reads=[("scr_st", seq, t0)], writes=["cma1"], dma=True)
                OP("sync", lambda e, t0=t0: e.dma_start(
                    out=cmb0[:], in_=scr_mk[seq, :, :, t0:t0 + FT]),
                   reads=[("scr_mk", seq, t0)], writes=["cmb0"], dma=True)
                for moc in range(DC):
                    pb, pk = (ps0, ["ps0"]) if moc % 2 == 0 else \
                        (ps1, [("ps1", i) for i in range(4)])
                    for jc in range(2):
                        OP("tensor", lambda e, moc=moc, jc=jc, pb=pb: e.matmul(
                            pb[:], WmV[:, jc, moc * P:(moc + 1) * P], cmb0[:, jc],
                            start=(jc == 0), stop=(jc == 1)),
                           reads=WmK + ["cmb0"], writes=pk)
                    OP("vector", lambda e, moc=moc, pb=pb: e.tensor_copy(membf[:, moc], pb[:]),
                       reads=pk, writes=membfK)
                for tt in range(TPF):
                    g = fi * TPF + tt
                    xv, xk = xws[:, g % 2], ("xws", g % 2)
                    pt, kb = (pbig, "pbig") if g % 2 == 0 else (pbig2, "pbig2")
                    mv, mk2 = mixws[:, g % 2], ("mixws", g % 2)
                    OP("sync", lambda e, g=g, xv=xv: e.dma_start(
                        out=xv, in_=x_in[seq, g * P:(g + 1) * P, :]),
                       writes=[xk], dma=True)
                    for half in range(2):
                        OP("tensor", lambda e, half=half, pt=pt: e.matmul(
                            pt[:, half * FT:(half + 1) * FT], ones_b[:],
                            bcV[:, half * FT:(half + 1) * FT], start=True, stop=False),
                           reads=["ones_b"] + bcK, writes=[(kb, half)])
                    for kc in range(24):
                        if kc < 8:
                            opnd, rk = cma[0][:, kc, tt * P:(tt + 1) * P], ["cma0"]
                        elif kc < 16:
                            opnd, rk = cma[1][:, kc - 8, tt * P:(tt + 1) * P], ["cma1"]
                        else:
                            opnd, rk = membf[:, kc - 16, tt * P:(tt + 1) * P], membfK
                        for half in range(2):
                            OP("tensor", lambda e, kc=kc, half=half, opnd=opnd, pt=pt: e.matmul(
                                pt[:, half * FT:(half + 1) * FT], opnd,
                                WcV[:, kc, half * FT:(half + 1) * FT],
                                start=False, stop=(kc == 23)),
                               reads=rk + WcK, writes=[(kb, half)])
                    OP("vector", lambda e, g=g, xv=xv: e.tensor_scalar(
                        fows[:], xv, MUc[seq][:, g:g + 1], Rc[seq][:, g:g + 1],
                        ALU.subtract, ALU.mult),
                       reads=[xk, ("MUc", seq, g), ("Rc", seq, g)], writes=["fows"])
                    for hh in range(2):
                        OP("vector", lambda e, hh=hh, pt=pt, mv=mv: e.tensor_add(
                            mv[:, hh * FT:(hh + 1) * FT], pt[:, hh * FT:(hh + 1) * FT],
                            fows[:, hh * FT:(hh + 1) * FT]),
                           reads=[(kb, hh), "fows"], writes=[mk2])
                    for hh in range(2):
                        OP("scalar", lambda e, hh=hh, mv=mv: e.activation(
                            sqh[:], mv[:, hh * FT:(hh + 1) * FT], AF.Square),
                           reads=[mk2], writes=["sqh"])
                        OP("vector", lambda e, hh=hh: e.reduce_sum(
                            stat_t[:, 1 + hh:2 + hh], sqh[:], axis=AX.X),
                           reads=["sqh"], writes=["stat"])
                    OP("vector", lambda e: e.tensor_add(stat_t[:, 1:2], stat_t[:, 1:2],
                                                        stat_t[:, 2:3]),
                       reads=["stat"], writes=["stat"])
                    OP("vector", lambda e: e.tensor_scalar_mul(stat_t[:, 4:5], stat_t[:, 1:2],
                                                               1.0 / D),
                       reads=["stat"], writes=["stat"])
                    OP("scalar", lambda e: e.activation(stat_t[:, 3:4], stat_t[:, 4:5],
                                                        AF.Ln, bias=1e-6),
                       reads=["stat"], writes=["stat"])
                    OP("scalar", lambda e, g=g: e.activation(
                        R2c[seq][:, g:g + 1], stat_t[:, 3:4], AF.Exp, scale=-0.5),
                       reads=["stat"], writes=[("R2c", seq, g)])
                    OP("gpsimd", lambda e, g=g, mv=mv: e.dma_start(
                        out=scr_mix[seq, g * P:(g + 1) * P, :], in_=mv),
                       reads=[mk2], writes=[("scr_mix", seq, g)], dma=True)

        def phase_c(seq):
            for g in range(NTILE):
                pt, kb = (pbig, "pbig") if g % 2 == 0 else (pbig2, "pbig2")
                xbv = cma[0][:, 2 * (g % 2):2 * (g % 2) + 2, :].rearrange("p a b -> p (a b)")
                OP("gpsimd", lambda e, g=g, xbv=xbv: e.dma_start(
                    out=xbv, in_=scr_mix[seq, g * P:(g + 1) * P, :]),
                   reads=[("scr_mix", seq, g)], writes=["cma0"], dma=True)
                diag_transpose(xbv, "cma0", R2c[seq][:, g:g + 1], ("R2c", seq, g), False,
                               pt=pt, kb=kb, slot=g % 2)
                for c in range(DC):
                    OP("vector", lambda e, c=c, g=g, pt=pt: e.tensor_scalar_mul(
                        actT[:, c, g * P:(g + 1) * P], pt[:, c * P:(c + 1) * P],
                        n2w[:, c:c + 1]),
                       reads=[(kb, c // 4), "n2w"], writes=[("actT", g)])

            for q in range(4):
                hc0 = q * 1024
                qg = seq * 4 + q
                WgqV, WgqK = Wgq2[qg % 2]
                if qg == 0:
                    load_w(WgqV, Wg[:, 0:1024].rearrange("(c p) n -> p c n", p=P), WgqK)
                load_w(WuqV, Wup[:, hc0:hc0 + 1024].rearrange("(c p) n -> p c n", p=P), WuqK)
                load_w(WdqV, Wdown[hc0:hc0 + 1024, :].rearrange("(c p) n -> p c n", p=P), WdqK)
                if qg < 7:
                    nq = (q + 1) % 4
                    nv, nk = Wgq2[(qg + 1) % 2]
                    load_w(nv, Wg[:, nq * 1024:nq * 1024 + 1024]
                           .rearrange("(c p) n -> p c n", p=P), nk)
                for fi in range(NF):
                    t0 = fi * FT
                    hbF = [("actT", fi * TPF + qq) for qq in range(TPF)]
                    for fc in range(8):
                        if fc % 2 == 0:
                            pg, pgk = ps0[:], ["ps0"]
                            pu, puk = ps1[:], [("ps1", i) for i in range(4)]
                        else:
                            pg, pgk = pbig[:, 0:FT], [("pbig", 0)]
                            pu, puk = pbig[:, FT:2 * FT], [("pbig", 1)]
                        slu = sluws[:, 0]
                        sluk = "sluws"
                        for kc in range(DC):
                            OP("tensor", lambda e, fc=fc, kc=kc, t0=t0, pg=pg,
                               Wv=WgqV: e.matmul(
                                pg, Wv[:, kc, fc * P:(fc + 1) * P],
                                actT[:, kc, t0:t0 + FT],
                                start=(kc == 0), stop=(kc == DC - 1)),
                               reads=WgqK + hbF, writes=pgk)
                        for kc in range(DC):
                            OP("tensor", lambda e, fc=fc, kc=kc, t0=t0, pu=pu: e.matmul(
                                pu, WuqV[:, kc, fc * P:(fc + 1) * P],
                                actT[:, kc, t0:t0 + FT],
                                start=(kc == 0), stop=(kc == DC - 1)),
                               reads=WuqK + hbF, writes=puk)
                        OP("scalar", lambda e, pg=pg, slu=slu: e.activation(slu, pg, AF.Silu),
                           reads=pgk, writes=[sluk])
                        OP("vector", lambda e, fc=fc, pu=pu, slu=slu: e.tensor_mul(
                            cma[0][:, fc, :], slu, pu),
                           reads=[sluk] + puk, writes=["cma0"])
                    for tt in range(TPF):
                        g = fi * TPF + tt
                        av, avk = accws[:, g % 2], ("accws", g % 2)
                        mv, mk2 = mixws[:, g % 2], ("mixws", g % 2)
                        wd_ps = [(pbig2[:, 0:FT], ("pbig2", 0)),
                                 (pbig2[:, FT:2 * FT], ("pbig2", 1))] if g % 2 == 0 else \
                                [(ps2[:], "ps2"), (ps3[:], "ps3")]
                        for fc in range(8):
                            for hh in range(2):
                                wp_, wpk_ = wd_ps[hh]
                                OP("tensor", lambda e, fc=fc, hh=hh, tt=tt, wp_=wp_: e.matmul(
                                    wp_, cma[0][:, fc, tt * P:(tt + 1) * P],
                                    WdqV[:, fc, hh * FT:(hh + 1) * FT],
                                    start=(fc == 0), stop=(fc == 7)),
                                   reads=["cma0"] + WdqK, writes=[wpk_])
                        if q < 3:
                            if q == 0:
                                for hh in range(2):
                                    wp_, wpk_ = wd_ps[hh]
                                    OP("scalar", lambda e, hh=hh, av=av, wp_=wp_: e.copy(
                                        av[:, hh * FT:(hh + 1) * FT], wp_),
                                       reads=[wpk_], writes=[avk])
                            else:
                                OP("sync", lambda e, g=g, av=av: e.dma_start(
                                    out=av, in_=scr_ffa[seq, g * P:(g + 1) * P, :]),
                                   reads=[("scr_ffa", seq, g)], writes=[avk], dma=True)
                                for hh in range(2):
                                    wp_, wpk_ = wd_ps[hh]
                                    OP("vector", lambda e, hh=hh, av=av, wp_=wp_: e.tensor_add(
                                        av[:, hh * FT:(hh + 1) * FT],
                                        av[:, hh * FT:(hh + 1) * FT], wp_),
                                       reads=[wpk_, avk], writes=[avk])
                            OP("gpsimd", lambda e, g=g, av=av: e.dma_start(
                                out=scr_ffa[seq, g * P:(g + 1) * P, :], in_=av),
                               reads=[avk], writes=[("scr_ffa", seq, g)], dma=True)
                        else:
                            OP("sync", lambda e, g=g, av=av: e.dma_start(
                                out=av, in_=scr_ffa[seq, g * P:(g + 1) * P, :]),
                               reads=[("scr_ffa", seq, g)], writes=[avk], dma=True)
                            OP("sync", lambda e, g=g, mv=mv: e.dma_start(
                                out=mv, in_=scr_mix[seq, g * P:(g + 1) * P, :]),
                               reads=[("scr_mix", seq, g)], writes=[mk2], dma=True)
                            OP("sync", lambda e, g=g: e.dma_start(
                                out=x1ws[:], in_=x_in[seq, g * P:(g + 1) * P, :]),
                               writes=["x1ws"], dma=True)
                            for hh in range(2):
                                wp_, wpk_ = wd_ps[hh]
                                OP("vector", lambda e, hh=hh, av=av, wp_=wp_: e.tensor_add(
                                    av[:, hh * FT:(hh + 1) * FT],
                                    av[:, hh * FT:(hh + 1) * FT], wp_),
                                   reads=[wpk_, avk], writes=[avk])
                            OP("vector", lambda e, av=av, mv=mv: e.tensor_add(fows[:], av, mv),
                               reads=[avk, mk2], writes=["fows"])
                            stats_pass("fows", fows[:], stat_t[:, 5:6], "stat",
                                       stat_t[:, 6:7], "stat")
                            OP("vector", lambda e: e.tensor_scalar(
                                fows[:], fows[:], stat_t[:, 5:6], stat_t[:, 6:7],
                                ALU.subtract, ALU.mult),
                               reads=["fows", "stat"], writes=["fows"])
                            OP("vector", lambda e: e.tensor_add(x1ws[:], x1ws[:], fows[:]),
                               reads=["fows", "x1ws"], writes=["x1ws"])
                            stats_pass("x1ws", x1ws[:], MU4c[seq][:, g:g + 1],
                                       ("MU4c", seq, g), R4c[seq][:, g:g + 1], ("R4c", seq, g))
                            OP("gpsimd", lambda e, g=g: e.dma_start(
                                out=scr_x1[seq, g * P:(g + 1) * P, :], in_=x1ws[:]),
                               reads=["x1ws"], writes=[("scr_x1", seq, g)], dma=True)

        def phase_d(seq):
            for g in range(NTILE):
                pt, kb = (pbig, "pbig") if g % 2 == 0 else (pbig2, "pbig2")
                xbv = cma[0][:, 2 * (g % 2):2 * (g % 2) + 2, :].rearrange("p a b -> p (a b)")
                OP("gpsimd", lambda e, g=g, xbv=xbv: e.dma_start(
                    out=xbv, in_=scr_x1[seq, g * P:(g + 1) * P, :]),
                   reads=[("scr_x1", seq, g)], writes=["cma0"], dma=True)
                OP("vector", lambda e, g=g: e.scalar_tensor_tensor(
                    stat_t[:, 7:8], MU4c[seq][:, g:g + 1], -1.0, R4c[seq][:, g:g + 1],
                    ALU.mult, ALU.mult),
                   reads=[("MU4c", seq, g), ("R4c", seq, g)], writes=["stat"])
                col_to_row(stat_t[:, 7:8], "stat", slot=g % 2)
                diag_transpose(xbv, "cma0", R4c[seq][:, g:g + 1], ("R4c", seq, g), True,
                               pt=pt, kb=kb, slot=g % 2)
                for hh in range(2):
                    OP("scalar", lambda e, g=g, hh=hh, pt=pt: e.copy(
                        actT[:, hh * 4:(hh + 1) * 4, g * P:(g + 1) * P],
                        pt[:, hh * FT:(hh + 1) * FT].rearrange("p (c q) -> p c q", q=P)),
                       reads=[(kb, hh)], writes=[("actT", g)])
                for kc in range(DC):
                    OP("tensor", lambda e, kc=kc, g=g: e.matmul(
                        ps2[:, 0:E], actT[:, kc, g * P:(g + 1) * P], gateW_b[:, kc, :],
                        start=(kc == 0), stop=(kc == DC - 1)),
                       reads=["gateW", ("actT", g)], writes=["ps2"])
                OP("scalar", lambda e: e.activation(scb[:], ps2[:, 0:E], AF.Sigmoid),
                   reads=["ps2"], writes=["scb"])
                OP("vector", lambda e: e.tensor_scalar_mul(scb[:], scb[:], rscol[:]),
                   reads=["scb", "rsv"], writes=["scb"])
                bc_v1 = bass.broadcast_tensor_aps(scb[:], stat_t[:, 0:1])
                bc_v2 = bass.broadcast_tensor_aps(scb[:], stat_t[:, 1:2])
                OP("vector", lambda e: e.reduce_max(stat_t[:, 0:1], scb[:], axis=AX.X),
                   reads=["scb"], writes=["stat"])
                OP("vector", lambda e, bc=bc_v1: e.tensor_tensor(scb2[:], bc[0], bc[1], ALU.is_ge),
                   reads=["scb", "stat"], writes=["scb2"])
                OP("vector", lambda e: e.scalar_tensor_tensor(
                    scb2[:], scb2[:], -1e30, scb[:], ALU.mult, ALU.add),
                   reads=["scb", "scb2"], writes=["scb2"])
                OP("vector", lambda e: e.reduce_max(stat_t[:, 1:2], scb2[:], axis=AX.X),
                   reads=["scb2"], writes=["stat"])
                OP("vector", lambda e, bc=bc_v2: e.tensor_tensor(scb2[:], bc[0], bc[1], ALU.is_ge),
                   reads=["scb", "stat"], writes=["scb2"])
                OP("vector", lambda e: e.tensor_mul(scb2[:], scb2[:], scb[:]),
                   reads=["scb", "scb2"], writes=["scb2"])
                OP("vector", lambda e: e.reduce_sum(stat_t[:, 2:3], scb2[:], axis=AX.X),
                   reads=["scb2"], writes=["stat"])
                OP("vector", lambda e: e.reciprocal(stat_t[:, 3:4], stat_t[:, 2:3]),
                   reads=["stat"], writes=["stat"])
                OP("vector", lambda e, g=g: e.tensor_scalar_mul(
                    GF[:, g, :], scb2[:], stat_t[:, 3:4]),
                   reads=["scb2", "stat"], writes=[("GF", g)])
                OP("tensor", lambda e, g=g: e.matmul(
                    ps3[0:E, 0:P], GF[:, g, :], eye_f[:], start=True, stop=True),
                   reads=[("GF", g), "eye_f"], writes=["ps3"])
                OP("vector", lambda e, g=g: e.tensor_copy(GCM[:, g * P:(g + 1) * P],
                                                          ps3[0:E, 0:P]),
                   reads=["ps3"], writes=[("GCM", g)])

            load_w(Ws1V, Ws1.rearrange("(c p) n -> p c n", p=P), Ws1K)
            load_w(Ws2V, Ws2.rearrange("(c p) n -> p c n", p=P), Ws2K)
            load_w(bs1V, bs1_i, bs1K)
            load_w(bs2V, bs2_i, bs2K)
            load_w(b2V, b2_i, b2K)
            for g in range(NTILE):
                gc0 = g * P
                pt, kb = (pbig2, "pbig2") if g % 2 == 0 else (pbig, "pbig")
                psh, pshk = (ps0, "ps0") if g % 2 == 0 else (ps2, "ps2")
                gsv, gsk = gscb[:, g % 2], ("gscb", g % 2)
                av, avk = accws[:, g % 2], ("accws", g % 2)
                for hh in range(2):
                    OP("tensor", lambda e, hh=hh, pt=pt: e.matmul(
                        pt[:, hh * FT:(hh + 1) * FT], ones_b[:],
                        bs2V[:, hh * FT:(hh + 1) * FT], start=True, stop=False),
                       reads=["ones_b"] + bs2K, writes=[(kb, hh)])
                    OP("tensor", lambda e, hh=hh, g=g, gc0=gc0, pt=pt: e.matmul(
                        pt[:, hh * FT:(hh + 1) * FT], GCM[:, gc0:gc0 + P],
                        b2V[:, hh * FT:(hh + 1) * FT], start=False, stop=False),
                       reads=[("GCM", g)] + b2K, writes=[(kb, hh)])
                OP("tensor", lambda e, psh=psh: e.matmul(
                    psh[:], ones_b[:], bs1V[:], start=True, stop=False),
                   reads=["ones_b"] + bs1K, writes=[pshk])
                for kc in range(DC):
                    OP("tensor", lambda e, kc=kc, g=g, gc0=gc0, psh=psh: e.matmul(
                        psh[:], actT[:, kc, gc0:gc0 + P], Ws1V[:, kc, :],
                        start=False, stop=(kc == DC - 1)),
                       reads=Ws1K + [("actT", g)], writes=[pshk])
                OP("scalar", lambda e, psh=psh, gsv=gsv: e.activation(gsv, psh[:], AF.Gelu),
                   reads=[pshk], writes=[gsk])
                for hcc in range(4):
                    psq, psqk = (ps1, "ps1") if hcc % 2 == 0 else (ps3, "ps3")
                    OP("tensor", lambda e, hcc=hcc, gsv=gsv, g=g, psq=psq: e.matmul(
                        psq[:, 0:P], gsv[:, hcc * P:(hcc + 1) * P],
                        eye_b[:], start=True, stop=True),
                       reads=[gsk, "eye_b"], writes=[psqk])
                    OP("vector", lambda e, hcc=hcc, g=g, psq=psq: e.tensor_copy(
                        gcmw[:, 0, hcc], psq[:, 0:P]),
                       reads=[psqk], writes=[("gcmw", hcc)])
                for hcc in range(4):
                    for hh in range(2):
                        OP("tensor", lambda e, hcc=hcc, hh=hh, g=g, pt=pt: e.matmul(
                            pt[:, hh * FT:(hh + 1) * FT], gcmw[:, 0, hcc],
                            Ws2V[:, hcc, hh * FT:(hh + 1) * FT],
                            start=False, stop=(hcc == 3)),
                           reads=[("gcmw", hcc)] + Ws2K, writes=[(kb, hh)])
                for hh in range(2):
                    OP("scalar", lambda e, hh=hh, pt=pt, av=av: e.copy(
                        av[:, hh * FT:(hh + 1) * FT], pt[:, hh * FT:(hh + 1) * FT]),
                       reads=[(kb, hh)], writes=[avk])
                OP("gpsimd", lambda e, g=g, gc0=gc0, av=av: e.dma_start(
                    out=scr_fna[seq, gc0:gc0 + P, :], in_=av),
                   reads=[avk], writes=[("scr_fna", seq, g)], dma=True)

            for dh in range(2):
                for ei in range(4):
                    ex = dh * 4 + ei
                    load_w(W1hV[:, ei], W1[ex].rearrange("(c p) h -> p c h", p=P),
                           wstrips(ei * 4096, 4096))
                    load_w(W2hV[:, ei], W2[ex].rearrange("(c p) n -> p c n", p=P),
                           wstrips(16384 + ei * 4096, 4096))
                for g in range(NTILE):
                    gc0 = g * P
                    pt, kb = (pbig2, "pbig2") if g % 2 == 0 else (pbig, "pbig")
                    av, avk = accws[:, g % 2], ("accws", g % 2)

                    def e_front(ei, g=g, gc0=gc0):
                        ex = dh * 4 + ei
                        pe1, pe1k = (ps0, "ps0") if ei % 2 == 0 else (ps2, "ps2")
                        slu, sluk = sluws[:, 0], "sluws"
                        gsv, gsk = gscb[:, ei % 2], ("gscb", ei % 2)
                        OP("tensor", lambda e, ex=ex, pe1=pe1: e.matmul(
                            pe1[:], ones_b[:], b1_r[:, ex, :], start=True, stop=False),
                           reads=["ones_b", "b1"], writes=[pe1k])
                        for kc in range(DC):
                            OP("tensor", lambda e, kc=kc, ei=ei, g=g, gc0=gc0,
                               pe1=pe1: e.matmul(
                                pe1[:], actT[:, kc, gc0:gc0 + P], W1hV[:, ei, kc, :],
                                start=False, stop=(kc == DC - 1)),
                               reads=wstrips(ei * 4096, 4096) + [("actT", g)], writes=[pe1k])
                        OP("scalar", lambda e, pe1=pe1, slu=slu: e.activation(
                            slu, pe1[:], AF.Gelu),
                           reads=[pe1k], writes=[sluk])
                        OP("vector", lambda e, ex=ex, g=g, slu=slu, gsv=gsv: e.tensor_scalar_mul(
                            gsv, slu, GF[:, g, ex:ex + 1]),
                           reads=[sluk, ("GF", g)], writes=[gsk])

                    def e_back(ei, g=g):
                        gsv, gsk = gscb[:, ei % 2], ("gscb", ei % 2)
                        for hcc in range(4):
                            psq, psqk = (ps1, "ps1") if hcc % 2 == 0 else (ps3, "ps3")
                            OP("tensor", lambda e, hcc=hcc, gsv=gsv, psq=psq: e.matmul(
                                psq[:, 0:P],
                                gsv[:, hcc * P:(hcc + 1) * P], eye_b[:],
                                start=True, stop=True),
                               reads=[gsk, "eye_b"], writes=[psqk])
                            OP("vector", lambda e, hcc=hcc, ei=ei, psq=psq: e.tensor_copy(
                                gcmw[:, 0, hcc], psq[:, 0:P]),
                               reads=[psqk], writes=[("gcmw", hcc)])
                        for hcc in range(4):
                            for hh in range(2):
                                st_f = (ei == 0 and hcc == 0)
                                OP("tensor", lambda e, hcc=hcc, hh=hh, ei=ei, pt=pt,
                                   st=st_f: e.matmul(
                                    pt[:, hh * FT:(hh + 1) * FT], gcmw[:, 0, hcc],
                                    W2hV[:, ei, hcc, hh * FT:(hh + 1) * FT],
                                    start=st, stop=(ei == 3 and hcc == 3)),
                                   reads=[("gcmw", hcc)]
                                       + wstrips(16384 + ei * 4096, 4096),
                                   writes=[(kb, hh)])

                    e_front(0)
                    for ei in range(1, 4):
                        e_front(ei)
                        e_back(ei - 1)
                    e_back(3)
                    OP("sync", lambda e, g=g, gc0=gc0, av=av: e.dma_start(
                        out=av, in_=scr_fna[seq, gc0:gc0 + P, :]),
                       reads=[("scr_fna", seq, g)], writes=[avk], dma=True)
                    for hh in range(2):
                        OP("vector", lambda e, hh=hh, pt=pt, av=av: e.tensor_add(
                            av[:, hh * FT:(hh + 1) * FT], av[:, hh * FT:(hh + 1) * FT],
                            pt[:, hh * FT:(hh + 1) * FT]),
                           reads=[(kb, hh), avk], writes=[avk])
                    if dh == 0:
                        OP("gpsimd", lambda e, g=g, gc0=gc0, av=av: e.dma_start(
                            out=scr_fna[seq, gc0:gc0 + P, :], in_=av),
                           reads=[avk], writes=[("scr_fna", seq, g)], dma=True)
                    else:
                        OP("sync", lambda e, g=g, gc0=gc0: e.dma_start(
                            out=x1ws[:], in_=scr_x1[seq, gc0:gc0 + P, :]),
                           reads=[("scr_x1", seq, g)], writes=["x1ws"], dma=True)
                        stats_pass(avk, av, stat_t[:, 5:6], "stat",
                                   stat_t[:, 6:7], "stat")
                        OP("vector", lambda e, av=av: e.tensor_scalar(
                            av, av, stat_t[:, 5:6], stat_t[:, 6:7],
                            ALU.subtract, ALU.mult),
                           reads=[avk, "stat"], writes=[avk])
                        OP("vector", lambda e, av=av: e.tensor_add(av, av, x1ws[:]),
                           reads=["x1ws", avk], writes=[avk])
                        OP("gpsimd", lambda e, g=g, gc0=gc0, av=av: e.dma_start(
                            out=y_out[seq, gc0:gc0 + P, :], in_=av),
                           reads=[avk], writes=[], dma=True)

        setup()
        if "A" in PHASES:
            load_w(WdV, Wdecay.rearrange("(c p) n -> p c n", p=P), WdK)
            load_w(WgV, Wgate.rearrange("(c p) n -> p c n", p=P), WgK)
            load_w(WkV, Wk.rearrange("(c p) n -> p c n", p=P), WkK)
            for seq in range(LIMIT_NSEQ or NSEQ):
                phase_a(seq)
        if "B" in PHASES:
            load_w(WcV, Wc.rearrange("(c p) n -> p c n", p=P), WcK)
            load_w(WmV, Wmem.rearrange("(c p) n -> p c n", p=P), WmK)
            load_w(bcV, bc_i, bcK)
            for seq in range(LIMIT_NSEQ or NSEQ):
                phase_b(seq)
        if "C" in PHASES:
            for seq in range(LIMIT_NSEQ or NSEQ):
                phase_c(seq)
        if "D" in PHASES:
            OP("gpsimd", lambda e: e.dma_start(out=b1_r, in_=b1_i),
               reads=[], writes=b1K, dma=True)
            for seq in range(LIMIT_NSEQ or NSEQ):
                phase_d(seq)

        prog.finalize()

        with nc.Block() as block:
            @block.sync
            def _(eng):
                prog.emit("sync", eng)

            @block.gpsimd
            def _(eng):
                prog.emit("gpsimd", eng)
                prog.fence(eng)

            @block.tensor
            def _(eng):
                prog.emit("tensor", eng)

            @block.vector
            def _(eng):
                prog.emit("vector", eng)

            @block.scalar
            def _(eng):
                prog.emit("scalar", eng)

    return nc


_nc_cache = None


def kernel(**inputs):
    global _nc_cache
    f32 = np.float32

    def cont(a):
        return np.ascontiguousarray(a, dtype=f32)

    x = cont(inputs["L0_attn_x"])
    conv_w = cont(inputs["L0_attn_conv_w"])
    aux = {
        "Wdecay": cont(inputs["L0_attn_Wdecay"]),
        "Wgate": cont(inputs["L0_attn_Wgate"]),
        "Wk": cont(inputs["L0_attn_Wk"]),
        "MT": cont(inputs["L0_attn_M"].T),
        "Wmem": cont(inputs["L0_attn_Wmem"]),
        "Wc": cont(inputs["L0_attn_Wc"]),
        "Wg": cont(inputs["L0_attn_Wg"]),
        "Wup": cont(inputs["L0_attn_Wup"]),
        "Wdown": cont(inputs["L0_attn_Wdown"]),
        "W1": cont(inputs["L0_ffn_W1"]),
        "W2": cont(inputs["L0_ffn_W2"]),
        "Ws1": cont(inputs["L0_ffn_Ws1"]),
        "Ws2": cont(inputs["L0_ffn_Ws2"]),
        "gateW": cont(inputs["L0_ffn_gate_W"]),
        "n1w_c": cont(inputs["L0_attn_norm1_w"].reshape(DC, P).T),
        "n2w_c": cont(inputs["L0_attn_norm2_w"].reshape(DC, P).T),
        "bd_c": cont(inputs["L0_attn_bdecay"].reshape(DC, P).T),
        "bg_c": cont(inputs["L0_attn_bgate"].reshape(DC, P).T),
        "cw_cols": cont(conv_w.reshape(DC, P, 7).transpose(1, 0, 2).reshape(P, DC * 7)),
        "bc_row": cont(inputs["L0_attn_bc"].reshape(1, D)),
        "bs1_row": cont(inputs["L0_ffn_bs1"].reshape(1, H)),
        "bs2_row": cont(inputs["L0_ffn_bs2"].reshape(1, D)),
        "b1_rows": cont(inputs["L0_ffn_b1"].reshape(1, E, H)),
        "b2_pm": cont(inputs["L0_ffn_b2"]),
        "ss": cont(np.reshape(inputs["L0_attn_state_scale"], (1, 1))),
        "ms": cont(np.reshape(inputs["L0_attn_mem_scale"], (1, 1))),
        "rs": cont(np.reshape(inputs["L0_ffn_route_scale"], (1, 1))),
    }
    if _nc_cache is None:
        _nc_cache = build()
    nc = _nc_cache
    in_maps = []
    for c in range(8):
        m = dict(aux)
        m["x"] = x[c * NSEQ:(c + 1) * NSEQ]
        in_maps.append(m)
    res = run_bass_kernel_spmd(nc, in_maps, core_ids=list(range(8)))
    return np.concatenate([r["y"] for r in res.results], axis=0)
